# version 07
# baseline (speedup 1.0000x reference)
import math
from contextlib import ExitStack

import numpy as np
import ml_dtypes

import concourse.bass as bass
import concourse.mybir as mybir
from concourse.bass_utils import run_bass_kernel_spmd

F32 = mybir.dt.float32
BF16 = mybir.dt.bfloat16
I32 = mybir.dt.int32
AF = mybir.ActivationFunctionType
ALU = mybir.AluOpType
AX = mybir.AxisListType

BATCH = 2
S = 4096
D = 2048
DH = 128
HPC = 4
OSL = HPC * DH
OWN = 1024
II = 8192
NL, R = 4, 16
LR = NL * R
NBLK = 8
BLK = 512
EPS = 1e-5
SM_SCALE = 1.0 / math.sqrt(DH)
SM_BIAS = -4.0
BAND = 256
NBAND = II // BAND
TB = BAND // 128

DEBUG = False
_CACHE = {}


class Ctr:
    __slots__ = ("sem", "v")

    def __init__(self, stack, nc, name):
        self.sem = stack.enter_context(nc.semaphore(name))
        self.v = 0

    def inc(self, ins, n=1):
        ins.then_inc(self.sem, n)
        self.v += n
        return self.v


def _w(eng, ctr, v):
    if v > 0:
        eng.wait_ge(ctr.sem, v)


def _host_consts():
    inv_freq = 10000.0 ** (-np.arange(0, DH, 2, dtype=np.float64) / DH)
    ang = np.arange(S, dtype=np.float64)[:, None] * inv_freq[None, :]
    cos = np.cos(ang).T
    sin = np.sin(ang).T
    CC = np.empty((DH, S), np.float32)
    SS = np.empty((DH, S), np.float32)
    CC[0::2] = cos
    CC[1::2] = cos
    SS[0::2] = -sin
    SS[1::2] = sin
    swp = np.zeros((DH, DH), ml_dtypes.bfloat16)
    for i in range(0, DH, 2):
        swp[i + 1, i] = 1
        swp[i, i + 1] = 1
    idn = np.eye(DH, dtype=ml_dtypes.bfloat16)
    masks = np.zeros((4, 128, BLK), ml_dtypes.bfloat16)
    for v in range(4):
        kk = np.arange(128)[:, None] + 128 * v
        qq = np.arange(BLK)[None, :]
        masks[v] = (kk <= qq).astype(ml_dtypes.bfloat16)
    ones_col = np.ones((128, 1), ml_dtypes.bfloat16)
    ones_row = np.ones((1, 128), ml_dtypes.bfloat16)
    aidx = (np.arange(LR) // R).astype(np.float32)[:, None]
    return CC, SS, swp, idn, masks, ones_col, ones_row, aidx


def _build():
    nc = bass.Bass("TRN2", target_bir_lowering=False)
    ap = lambda name, shape, dt, kind="ExternalInput": nc.dram_tensor(
        name, shape, dt, kind=kind).ap()

    xg = ap("xg", [S, D], F32)
    xow = ap("xow", [OWN, D], F32)
    wq_i = ap("wq", [OSL, D], F32)
    wk_i = ap("wk", [OSL, D], F32)
    wv_i = ap("wv", [OSL, D], F32)
    wo_i = ap("wo", [D, OSL], F32)
    aq_i = ap("aq", [LR, D], F32)
    ak_i = ap("ak", [LR, D], F32)
    av_i = ap("av", [LR, D], F32)
    ao_i = ap("ao", [LR, OSL], F32)
    bq_i = ap("bq", [NL, OSL, R], F32)
    bk_i = ap("bk", [NL, OSL, R], F32)
    bv_i = ap("bv", [NL, OSL, R], F32)
    bo_i = ap("bo", [NL, D, R], F32)
    gw_i = ap("gw", [II, D], I32)
    uw_i = ap("uw", [II, D], I32)
    dw_i = ap("dw", [D, II], I32)
    gsc_i = ap("gsc", [128, II // 128], F32)
    usc_i = ap("usc", [128, II // 128], F32)
    dsc_i = ap("dsc", [128, D // 128], F32)
    starts_i = ap("starts", [1, NL], I32)
    bmeta_i = ap("bmeta", [1, 1], F32)
    own0_i = ap("own0m", [1, 1], I32)
    cc_i = ap("cc", [DH, S], F32)
    ss_i = ap("ss", [DH, S], F32)
    mask_i = ap("masks", [4, 128, BLK], BF16)
    swp_i = ap("swp", [DH, DH], BF16)
    idn_i = ap("idn", [DH, DH], BF16)
    onc_i = ap("onc", [128, 1], BF16)
    onr_i = ap("onr", [1, 128], BF16)
    mner_i = ap("mner", [1, 128], BF16)
    aidx_i = ap("aidx", [LR, 1], F32)

    out_own = ap("out_own", [OWN, D], F32, "ExternalOutput")
    dbg = {}
    if DEBUG:
        dbg["k"] = ap("dbg_k", [128, HPC * S], BF16, "ExternalOutput")
        dbg["q"] = ap("dbg_q", [128, HPC * S], BF16, "ExternalOutput")
        dbg["v"] = ap("dbg_v", [S, OSL], BF16, "ExternalOutput")
        dbg["attn"] = ap("dbg_attn", [128, HPC * S], BF16, "ExternalOutput")
        dbg["ccin"] = ap("dbg_ccin", [S, D], F32, "ExternalOutput")

    xn_all = ap("xn_all", [16, 128, S], BF16, "Internal")
    v_all = ap("v_all", [S, OSL], BF16, "Internal")
    wo_bf = ap("wo_bf", [D, OSL], BF16, "Internal")
    gw_bf = ap("gw_bf", [II, D], BF16, "Internal")
    uw_bf = ap("uw_bf", [II, D], BF16, "Internal")
    dw_bf = ap("dw_bf", [D, II], BF16, "Internal")
    ccin = ap("ccin", [S, D], F32, "Internal")
    ccout = ap("ccout", [OWN, D], F32, "Internal")

    with ExitStack() as st:
        sb = lambda name, shape, dt, **kw: st.enter_context(
            nc.sbuf_tensor(name, shape, dt, **kw))
        C = lambda name: Ctr(st, nc, name)

        mask_sb = sb("mask_sb", [128, 4 * BLK], BF16)
        swp_sb = sb("swp_sb", [DH, DH], BF16)
        idn_sb = sb("idn_sb", [DH, DH], BF16)
        onc_sb = sb("onc_sb", [128, 1], BF16)
        onr_sb = sb("onr_sb", [1, 128], BF16)
        aidx_sb = sb("aidx_sb", [LR, 1], F32)
        gsc_sb = sb("gsc_sb", [128, II // 128], F32)
        usc_sb = sb("usc_sb", [128, II // 128], F32)
        dsc_sb = sb("dsc_sb", [128, D // 128], F32)
        ind_sb = sb("ind_sb", [LR, 1], F32)
        starts_sb = sb("starts_sb", [1, NL], I32)
        startsf_sb = sb("startsf_sb", [1, NL], F32)
        bmeta_sb = sb("bmeta_sb", [1, 1], F32)
        own0_sb = sb("own0_sb", [1, 1], I32)
        idsc_sb = sb("idsc_sb", [1, 8], F32)
        idbf_sb = sb("idbf_sb", [1, 1], BF16)
        smbias_sb = sb("smbias_sb", [128, 1], F32)
        aot = sb("aot", [128, HPC * LR], BF16)
        bot = sb("bot", [LR, D], BF16)
        mdsc_sb = sb("mdsc_sb", [128, D // 128], F32)
        mner_sb = sb("mner_sb", [1, 128], BF16)
        FJW = 1024
        fraw = sb("fraw", [128, 2 * FJW], I32)
        fbf = sb("fbf", [128, 2 * FJW], BF16)

        c_lds = C("c_lds")
        c_ldg = C("c_ldg")
        s_ct = C("s_ct")
        s_x = [C("s_x0"), C("s_x1")]
        s_rp = C("s_rp")
        s_vs = [C("s_vs0"), C("s_vs1")]
        s_vg = C("s_vg")
        s_wo = C("s_wo")
        s_cc = [C("s_cc0"), C("s_cc1")]
        s_gw = C("s_gw")
        s_ga = C("s_ga")
        s_dw = C("s_dw")
        s_xo = [C("s_xo0"), C("s_xo1")]
        s_fa = [C("s_fa0"), C("s_fa1")]
        s_fl = [C("s_fl0"), C("s_fl1")]
        s_fs = [C("s_fs0"), C("s_fs1")]
        c_po = C("c_po")
        c_sq2 = C("c_sq2")
        c_ffc = C("c_ffc")
        c_xcs = C("c_xcs")
        c_tr = C("c_tr")
        c_ev = C("c_ev")
        c_mmq = C("c_mmq")
        c_mlow = C("c_mlow")
        c_vlow = C("c_vlow")
        c_qraw = C("c_qraw")
        c_msw = C("c_msw")
        c_rope = C("c_rope")
        c_vev = C("c_vev")
        c_sq = C("c_sq")
        c_vst = C("c_vst")
        c_rstd = C("c_rstd")
        c_xn = C("c_xn")
        c_msc = C("c_msc")
        c_exp = C("c_exp")
        c_msk = C("c_msk")
        c_mav = C("c_mav")
        c_rcp = C("c_rcp")
        c_mbc = C("c_mbc")
        c_vbc = C("c_vbc")
        c_vat = C("c_vat")
        c_vlo = C("c_vlo")
        c_mo = C("c_mo")
        c_evo = C("c_evo")
        c_scc = C("c_scc")
        c_cc = C("c_cc")
        c_mg = C("c_mg")
        c_sil = C("c_sil")
        c_vh = C("c_vh")
        c_md = C("c_md")
        c_vacc = C("c_vacc")
        c_fin = C("c_fin")
        c_out = C("c_out")
        c_p0 = C("c_p0")
        vse = C("vse")
        ase = C("ase")

        blk0 = st.enter_context(nc.Block())

        @blk0.sync
        def _(_sy):
            SY = nc.sync
            TE = nc.tensor
            VE = nc.vector
            AC = nc.scalar
            GP = nc.gpsimd

            def vi(ins):
                return vse.inc(ins)

            def vw():
                VE.wait_ge(vse.sem, vse.v)

            def ai(ins):
                return ase.inc(ins)

            def aw():
                AC.wait_ge(ase.sem, ase.v)

            with ExitStack() as sA:
                sbA = lambda n, sh, dt, **kw: sA.enter_context(
                    nc.sbuf_tensor(n, sh, dt, **kw))
                ksb = sbA("ksb", [128, HPC * S], BF16)
                qsb = sbA("qsb", [128, HPC * S], BF16)

                with ExitStack() as sB:
                    sbB = lambda n, sh, dt, **kw: sB.enter_context(
                        nc.sbuf_tensor(n, sh, dt, **kw))
                    psB = lambda n, sh, dt: sB.enter_context(
                        nc.psum_tensor(n, sh, dt))
                    wqt = sbB("wqt", [128, 16 * BLK], BF16)
                    wkt = sbB("wkt", [128, 16 * BLK], BF16)
                    wvt = sbB("wvt", [128, 16 * BLK], BF16)
                    aqt = sbB("aqt", [128, 16 * LR], BF16)
                    akt = sbB("akt", [128, 16 * LR], BF16)
                    avt = sbB("avt", [128, 16 * LR], BF16)
                    bqt = sbB("bqt", [LR, OSL], BF16)
                    bkt = sbB("bkt", [LR, OSL], BF16)
                    bvt = sbB("bvt", [LR, OSL], BF16)
                    xt_sb = sbB("xt_sb", [128, 2 * D], F32)
                    stats = sbB("stats", [128, 64], F32)
                    xntm = sbB("xntm", [128, 2 * D], BF16)
                    xnfm = sbB("xnfm", [128, 16 * BLK], BF16)
                    ccb = sbB("ccb", [128, BLK], F32)
                    ssb = sbB("ssb", [128, BLK], F32)
                    lowq = sbB("lowq", [LR, BLK], BF16)
                    lowk = sbB("lowk", [LR, BLK], BF16)
                    lowv = sbB("lowv", [LR, BLK], BF16)
                    qraw = sbB("qraw", [128, 2 * BLK], BF16)
                    rtmp1 = sbB("rtmp1", [128, BLK], F32)
                    rtmp2 = sbB("rtmp2", [128, BLK], F32)
                    vstg = sbB("vstg", [128, 2 * OSL], BF16)
                    ps_tr = psB("ps_tr", [128, 2 * 1024], BF16)
                    ps_qk0 = psB("ps_qk0", [128, 512], F32)
                    ps_qk1 = psB("ps_qk1", [128, 512], F32)
                    ps_v0 = psB("ps_v0", [128, 512], F32)
                    ps_v1 = psB("ps_v1", [128, 512], F32)
                    ps_low = psB("ps_low", [128, 512], F32)
                    ps_swap = psB("ps_swap", [128, 512], F32)

                    for dst, src in [
                        (swp_sb[:, :], swp_i[:, :]), (idn_sb[:, :], idn_i[:, :]),
                        (onc_sb[:, :], onc_i[:, :]), (onr_sb[:, :], onr_i[:, :]),
                        (aidx_sb[:, :], aidx_i[:, :]), (gsc_sb[:, :], gsc_i[:, :]),
                        (usc_sb[:, :], usc_i[:, :]), (dsc_sb[:, :], dsc_i[:, :]),
                        (starts_sb[:, :], starts_i[:, :]),
                        (bmeta_sb[:, :], bmeta_i[:, :]),
                        (own0_sb[:, :], own0_i[:, :]),
                        (mner_sb[:, :], mner_i[:, :]),
                    ]:
                        s_ct.inc(SY.dma_start(dst, src), 16)
                    for v in range(4):
                        s_ct.inc(SY.dma_start(
                            mask_sb[:, v * BLK:(v + 1) * BLK], mask_i[v, :, :]), 16)
                    ld_consts = s_ct.v

                    c_ldg.inc(GP.dma_start(wo_bf[:, :], wo_i[:, :]), 16)
                    ld_wcast = c_ldg.v

                    TE.wait_ge(s_ct.sem, ld_consts)
                    with ExitStack() as sP0:
                        astg = sP0.enter_context(
                            nc.sbuf_tensor("astg", [LR, D], BF16))
                        bstk = sP0.enter_context(
                            nc.sbuf_tensor("bstk", [128, 16 * LR], BF16))
                        for adr, dstT, wid in [(aq_i, aqt, D), (ak_i, akt, D),
                                               (av_i, avt, D), (ao_i, aot, OSL)]:
                            _w(GP, c_tr, c_tr.v)
                            c_ldg.inc(GP.dma_start(astg[:, 0:wid], adr[:, :]), 16)
                            TE.wait_ge(c_ldg.sem, c_ldg.v)
                            _w(TE, c_ev, c_ev.v)
                            nch = wid // 128
                            for dk in range(nch):
                                ins = TE.matmul(
                                    ps_tr[:, dk * LR:(dk + 1) * LR],
                                    astg[:, dk * 128:(dk + 1) * 128],
                                    idn_sb[0:LR, 0:LR],
                                    is_transpose=True, start=True, stop=True)
                            c_tr.inc(ins)
                            VE.wait_ge(c_tr.sem, c_tr.v)
                            c_ev.inc(VE.tensor_copy(dstT[:, 0:nch * LR],
                                                    ps_tr[:, 0:nch * LR]))
                        for bdr, dstT, nch in [(bq_i, bqt, OSL // 128),
                                               (bk_i, bkt, OSL // 128),
                                               (bv_i, bvt, OSL // 128),
                                               (bo_i, bot, D // 128)]:
                            _w(GP, c_tr, c_tr.v)
                            for ch in range(nch):
                                src3 = bdr[:, ch * 128:(ch + 1) * 128, :].rearrange(
                                    "a p r -> p a r")
                                c_ldg.inc(GP.dma_start(
                                    bstk[:, ch * LR:(ch + 1) * LR].rearrange(
                                        "p (a r) -> p a r", a=NL), src3), 16)
                            TE.wait_ge(c_ldg.sem, c_ldg.v)
                            for half in range((nch + 7) // 8):
                                _w(TE, c_ev, c_ev.v)
                                lo = half * 8
                                hi = min(lo + 8, nch)
                                for ch in range(lo, hi):
                                    ins = TE.matmul(
                                        ps_tr[0:LR,
                                              (ch - lo) * 128:(ch - lo + 1) * 128],
                                        bstk[:, ch * LR:(ch + 1) * LR],
                                        idn_sb[:, :],
                                        is_transpose=True, start=True, stop=True)
                                c_tr.inc(ins)
                                VE.wait_ge(c_tr.sem, c_tr.v)
                                c_ev.inc(VE.tensor_copy(
                                    dstT[0:LR, lo * 128:hi * 128],
                                    ps_tr[0:LR, 0:(hi - lo) * 128]))

                    VE.wait_ge(s_ct.sem, ld_consts)
                    VE.memset(smbias_sb[:, :], SM_BIAS)
                    vi(VE.tensor_copy(startsf_sb[:, :], starts_sb[:, :]))
                    vw()
                    vi(VE.tensor_single_scalar(idsc_sb[0:1, 0:NL],
                                               startsf_sb[0:1, :],
                                               bmeta_sb[0:1, 0:1], ALU.is_le))
                    vw()
                    vi(VE.reduce_sum(idsc_sb[0:1, 4:5], idsc_sb[0:1, 0:NL], AX.X))
                    vw()
                    vi(VE.tensor_scalar(idsc_sb[0:1, 5:6], idsc_sb[0:1, 4:5],
                                        -1.0, 0.0, ALU.add, ALU.max))
                    vw()
                    vi(VE.tensor_single_scalar(idsc_sb[0:1, 6:7], idsc_sb[0:1, 5:6],
                                               3.0, ALU.min))
                    vw()
                    c_p0.inc(VE.tensor_copy(idbf_sb[0:1, 0:1], idsc_sb[0:1, 6:7]))
                    TE.wait_ge(c_p0.sem, c_p0.v)
                    c_p0.inc(TE.matmul(ps_qk0[0:LR, 0:1], onr_sb[0:1, 0:LR],
                                       idbf_sb[0:1, 0:1], start=True, stop=True))
                    VE.wait_ge(c_p0.sem, c_p0.v)
                    vi(VE.tensor_tensor(ind_sb[:, :], ps_qk0[0:LR, 0:1],
                                        aidx_sb[:, :], ALU.is_equal))
                    vw()
                    for dstT, wid in [(bqt, OSL), (bkt, OSL), (bvt, OSL), (bot, D)]:
                        ins = VE.tensor_scalar(dstT[:, 0:wid], dstT[:, 0:wid],
                                               ind_sb[:, 0:1], None, ALU.mult)
                    c_p0.inc(ins)
                    p0_done = c_p0.v

                    wstg = xnfm
                    wtr_v = {}
                    n_wc = 0
                    for wsrc, dstT in [(wq_i, wqt), (wk_i, wkt), (wv_i, wvt)]:
                        dstv = dstT[:, :].rearrange("p (dk q) -> p dk q", dk=16)
                        for fc in range(4):
                            sl = n_wc % 2
                            if n_wc >= 2:
                                GP.wait_ge(c_tr.sem, wtr_v[n_wc - 2])
                            c_ldg.inc(GP.dma_start(
                                wstg[:, sl * D:(sl + 1) * D],
                                wsrc[fc * 128:(fc + 1) * 128, :]), 16)
                            ldv = c_ldg.v
                            TE.wait_ge(c_ldg.sem, ldv)
                            for grp in range(2):
                                _w(TE, c_ev, c_ev.v if n_wc == 0 and grp == 0
                                   else c_ev.v - 1)
                                psl = (grp % 2) * 1024
                                for j in range(8):
                                    dk = grp * 8 + j
                                    ins = TE.matmul(
                                        ps_tr[:, psl + j * 128:psl + (j + 1) * 128],
                                        wstg[:, sl * D + dk * 128:
                                             sl * D + (dk + 1) * 128],
                                        idn_sb[:, :], is_transpose=True,
                                        start=True, stop=True)
                                c_tr.inc(ins)
                                VE.wait_ge(c_tr.sem, c_tr.v)
                                c_ev.inc(VE.tensor_copy(
                                    dstv[:, grp * 8:(grp + 1) * 8,
                                         fc * 128:(fc + 1) * 128],
                                    ps_tr[:, psl:psl + 1024].rearrange(
                                        "p (c q) -> p c q", c=8)))
                            wtr_v[n_wc] = c_tr.v
                            n_wc += 1

                    GP.wait_ge(c_ev.sem, c_ev.v)
                    GP.wait_ge(s_ct.sem, ld_consts)
                    c_ffc.inc(GP.dma_start(gw_bf[:, :], gw_i[:, :]), 16)
                    c_ffc.inc(GP.dma_start(uw_bf[:, :], uw_i[:, :]), 16)
                    vi(VE.tensor_scalar(mdsc_sb[:, :], dsc_sb[:, :], -128.0,
                                        None, ALU.mult))
                    p0_msc = vse.v
                    gp_jobs = []
                    for rc in range(16):
                        for cp in range(4):
                            for hf in range(2):
                                gp_jobs.append((dw_i, dw_bf,
                                                (dsc_sb, mdsc_sb, rc),
                                                (128 * rc, D * cp + FJW * hf)))
                    prep_dq = []
                    prep_state = {"j": 0}

                    def emit_prep(count):
                        j0 = prep_state["j"]
                        for j in range(j0, min(j0 + count, len(gp_jobs))):
                            srcw, dstw, (scb, mscb, ci), (r0, c0) = gp_jobs[j]
                            sc = scb[:, ci:ci + 1]
                            msc = mscb[:, ci:ci + 1]
                            par = j % 2
                            eng = VE if par == 0 else AC
                            dq = c_po if par == 0 else c_sq2
                            srcap = srcw[r0:r0 + 128, c0:c0 + FJW]
                            dstap = dstw[r0:r0 + 128, c0:c0 + FJW]
                            if j >= 2:
                                GP.wait_ge(*prep_dq[j - 2])
                            s_fl[par].inc(GP.dma_start(
                                fraw[:, par * FJW:(par + 1) * FJW], srcap), 16)
                            eng.wait_ge(s_fl[par].sem, s_fl[par].v)
                            eng.wait_ge(s_fs[par].sem, s_fs[par].v)
                            if par == 0:
                                ins = eng.tensor_scalar(
                                    fbf[:, 0:FJW], fraw[:, 0:FJW],
                                    128.0, sc, ALU.subtract, ALU.mult)
                            else:
                                if not prep_state.get("mw"):
                                    AC.wait_ge(vse.sem, p0_msc)
                                    prep_state["mw"] = True
                                ins = eng.activation(
                                    fbf[:, FJW:2 * FJW], fraw[:, FJW:2 * FJW],
                                    AF.Identity, bias=msc, scale=sc)
                            dq.inc(ins)
                            prep_dq.append((dq.sem, dq.v))
                            GP.wait_ge(dq.sem, dq.v)
                            s_fs[par].inc(GP.dma_start(
                                dstap, fbf[:, par * FJW:(par + 1) * FJW]), 16)
                        prep_state["j"] = min(j0 + count, len(gp_jobs))

                    p0_tr = c_tr.v
                    n_qk = 0
                    n_v = 0
                    n_trf = 0
                    v_st = []
                    mmq_blk = {}
                    store_blk = {}
                    ld_rope = 0
                    for b in range(NBLK):
                        xld = []
                        for tt in range(4):
                            gi = 4 * b + tt
                            _w(SY, c_xn, gi - 1)
                            s_x[gi % 2].inc(SY.dma_start(
                                xt_sb[:, (gi % 2) * D:(gi % 2 + 1) * D],
                                xg[b * BLK + tt * 128:b * BLK + (tt + 1) * 128, :]),
                                16)
                            xld.append((s_x[gi % 2], s_x[gi % 2].v))
                        _w(SY, c_rope, 8 * b)
                        s_rp.inc(SY.dma_start(ccb[:, :],
                                              cc_i[:, b * BLK:(b + 1) * BLK]), 16)
                        s_rp.inc(SY.dma_start(ssb[:, :],
                                              ss_i[:, b * BLK:(b + 1) * BLK]), 16)
                        ld_rope = s_rp.v
                        for tt in range(4):
                            gi = 4 * b + tt
                            tp = gi % 2
                            sp = (gi % 2) * 32 + 8
                            xt = xt_sb[:, tp * D:(tp + 1) * D]
                            AC.wait_ge(xld[tt][0].sem, xld[tt][1])
                            _w(AC, c_tr, p0_tr + 2 * gi - 2 if gi >= 2 else 0)
                            c_sq.inc(AC.activation(
                                xntm[:, tp * D:(tp + 1) * D], xt, AF.Square,
                                accum_out=stats[:, sp:sp + 1]))
                            VE.wait_ge(xld[tt][0].sem, xld[tt][1])
                            vi(VE.reduce_sum(stats[:, sp + 1:sp + 2], xt, AX.X))
                            VE.wait_ge(c_sq.sem, c_sq.v)
                            vw()
                            vi(VE.tensor_scalar(stats[:, sp + 2:sp + 3],
                                                stats[:, sp + 1:sp + 2],
                                                1.0 / D, None, ALU.mult))
                            vw()
                            vi(VE.tensor_scalar(stats[:, sp + 3:sp + 4],
                                                stats[:, sp:sp + 1],
                                                1.0 / D, EPS, ALU.mult, ALU.add))
                            vw()
                            vi(VE.tensor_tensor(stats[:, sp + 4:sp + 5],
                                                stats[:, sp + 2:sp + 3],
                                                stats[:, sp + 2:sp + 3], ALU.mult))
                            vw()
                            c_vst.inc(VE.tensor_sub(stats[:, sp + 3:sp + 4],
                                                    stats[:, sp + 3:sp + 4],
                                                    stats[:, sp + 4:sp + 5]))
                            AC.wait_ge(c_vst.sem, c_vst.v)
                            ai(AC.activation(stats[:, sp + 5:sp + 6],
                                             stats[:, sp + 3:sp + 4], AF.Ln))
                            aw()
                            c_rstd.inc(AC.activation(stats[:, sp + 6:sp + 7],
                                                     stats[:, sp + 5:sp + 6],
                                                     AF.Exp, scale=-0.5))
                            VE.wait_ge(c_rstd.sem, c_rstd.v)
                            c_xn.inc(VE.tensor_scalar(
                                xntm[:, tp * D:(tp + 1) * D], xt,
                                stats[:, sp + 2:sp + 3], stats[:, sp + 6:sp + 7],
                                ALU.subtract, ALU.mult))
                            TE.wait_ge(c_xn.sem, c_xn.v)
                            xnv = xnfm[:, :].rearrange("p (dk q) -> p dk q", dk=16)
                            for grp in range(2):
                                _w(TE, c_ev, c_ev.v if n_trf == 0 else c_ev.v - 1)
                                sl = (n_trf % 2) * 1024
                                for j in range(8):
                                    dk = grp * 8 + j
                                    ins = TE.matmul(
                                        ps_tr[:, sl + j * 128:sl + (j + 1) * 128],
                                        xntm[:, tp * D + dk * 128:
                                             tp * D + (dk + 1) * 128],
                                        idn_sb[:, :], is_transpose=True,
                                        start=True, stop=True)
                                c_tr.inc(ins)
                                n_trf += 1
                                VE.wait_ge(c_tr.sem, c_tr.v)
                                if b >= 1 and tt == 0 and grp == 0:
                                    _w(VE, c_mmq, mmq_blk[b - 1])
                                c_ev.inc(VE.tensor_copy(
                                    xnv[:, grp * 8:(grp + 1) * 8,
                                        tt * 128:(tt + 1) * 128],
                                    ps_tr[:, sl:sl + 1024].rearrange(
                                        "p (c q) -> p c q", c=8)))
                            emit_prep(4)
                        ev_blk = c_ev.v
                        xb = lambda dk: xnfm[:, dk * BLK:(dk + 1) * BLK]

                        TE.wait_ge(c_ev.sem, ev_blk)
                        if b == 0:
                            TE.wait_ge(c_p0.sem, p0_done)
                        for at, lowsb in [(aqt, lowq), (akt, lowk), (avt, lowv)]:
                            _w(TE, c_vlow, c_vlow.v)
                            for dk in range(16):
                                ins = TE.matmul(ps_low[0:LR, 0:BLK],
                                                at[:, dk * LR:(dk + 1) * LR],
                                                xb(dk), start=(dk == 0),
                                                stop=(dk == 15))
                            c_mlow.inc(ins)
                            VE.wait_ge(c_mlow.sem, c_mlow.v)
                            c_vlow.inc(VE.tensor_copy(lowsb[:, :],
                                                      ps_low[0:LR, 0:BLK]))

                        for wt, low, bt, dstsb in [(wqt, lowq, bqt, qsb),
                                                   (wkt, lowk, bkt, ksb)]:
                            for hh in range(HPC):
                                u = n_qk
                                pst = ps_qk0 if (u % 2) == 0 else ps_qk1
                                _w(TE, c_qraw, u - 1)
                                TE.wait_ge(c_vlow.sem, c_vlow.v)
                                for dk in range(16):
                                    TE.matmul(pst[:, 0:BLK],
                                              wt[:, dk * BLK + hh * 128:
                                                 dk * BLK + (hh + 1) * 128],
                                              xb(dk), start=(dk == 0), stop=False)
                                c_mmq.inc(TE.matmul(
                                    pst[:, 0:BLK], bt[:, hh * 128:(hh + 1) * 128],
                                    low[:, :], start=False, stop=True))
                                VE.wait_ge(c_mmq.sem, c_mmq.v)
                                _w(VE, c_msw, u - 1)
                                c_qraw.inc(VE.tensor_copy(
                                    qraw[:, (u % 2) * BLK:(u % 2 + 1) * BLK],
                                    pst[:, 0:BLK]))
                                TE.wait_ge(c_qraw.sem, c_qraw.v)
                                _w(TE, c_rope, u)
                                c_msw.inc(TE.matmul(
                                    ps_swap[:, 0:BLK], swp_sb[:, :],
                                    qraw[:, (u % 2) * BLK:(u % 2 + 1) * BLK],
                                    start=True, stop=True))
                                VE.wait_ge(s_rp.sem, ld_rope)
                                VE.wait_ge(c_qraw.sem, c_qraw.v)
                                vi(VE.tensor_tensor(
                                    rtmp1[:, :],
                                    qraw[:, (u % 2) * BLK:(u % 2 + 1) * BLK],
                                    ccb[:, :], ALU.mult))
                                VE.wait_ge(c_msw.sem, c_msw.v)
                                vi(VE.tensor_tensor(rtmp2[:, :], ps_swap[:, 0:BLK],
                                                    ssb[:, :], ALU.mult))
                                vw()
                                c_rope.inc(VE.tensor_add(
                                    dstsb[:, hh * S + b * BLK:
                                          hh * S + (b + 1) * BLK],
                                    rtmp1[:, :], rtmp2[:, :]))
                                n_qk += 1

                        for tt in range(4):
                            pst = ps_v0 if (n_v % 2) == 0 else ps_v1
                            _w(TE, c_vev, n_v - 1)
                            for dk in range(16):
                                TE.matmul(pst[:, 0:OSL],
                                          xnfm[:, dk * BLK + tt * 128:
                                               dk * BLK + (tt + 1) * 128],
                                          wvt[:, dk * BLK:(dk + 1) * BLK],
                                          start=(dk == 0), stop=False)
                            c_mmq.inc(TE.matmul(pst[:, 0:OSL],
                                                lowv[:, tt * 128:(tt + 1) * 128],
                                                bvt[:, :], start=False, stop=True))
                            VE.wait_ge(c_mmq.sem, c_mmq.v)
                            if n_v >= 2:
                                _w(VE, s_vs[n_v % 2], s_vs[n_v % 2].v)
                            c_vev.inc(VE.tensor_copy(
                                vstg[:, (n_v % 2) * OSL:(n_v % 2 + 1) * OSL],
                                pst[:, 0:OSL]))
                            SY.wait_ge(c_vev.sem, c_vev.v)
                            kc = b * 4 + tt
                            s_vs[n_v % 2].inc(SY.dma_start(
                                v_all[kc * 128:(kc + 1) * 128, :],
                                vstg[:, (n_v % 2) * OSL:(n_v % 2 + 1) * OSL]), 16)
                            n_v += 1
                        mmq_blk[b] = c_mmq.v
                    emit_prep(len(gp_jobs))
                    p1_rope = c_rope.v

                with ExitStack() as sC:
                    sbC = lambda n, sh, dt, **kw: sC.enter_context(
                        nc.sbuf_tensor(n, sh, dt, **kw))
                    psC = lambda n, sh, dt: sC.enter_context(
                        nc.psum_tensor(n, sh, dt))
                    vsb = sbC("vsb", [128, HPC * S], BF16)
                    asb = sbC("asb", [128, HPC * S], BF16)
                    psb = sbC("psb", [128, 2 * BLK], BF16)
                    sums_f = sbC("sums_f", [1, BLK], F32)
                    rinvf = sbC("rinvf", [1, BLK], F32)
                    rinvb = sbC("rinvb", [1, BLK], BF16)
                    binv = sbC("binv", [128, BLK], BF16)
                    sP2 = ExitStack()
                    ps_sc = sP2.enter_context(
                        nc.psum_tensor("ps_sc", [128, 2 * 512], F32))
                    ps_attn = sP2.enter_context(
                        nc.psum_tensor("ps_attn", [128, 512], F32))
                    ps_sum = sP2.enter_context(
                        nc.psum_tensor("ps_sum", [128, 512], F32))
                    ps_bc = sP2.enter_context(
                        nc.psum_tensor("ps_bc", [128, 512], F32))

                    SY.wait_ge(s_vs[0].sem, s_vs[0].v)
                    SY.wait_ge(s_vs[1].sem, s_vs[1].v)
                    vsrc = v_all.rearrange("(kc p) d -> p kc d", p=128)
                    for h in range(HPC):
                        s_vg.inc(SY.dma_start(
                            vsb[:, h * S:(h + 1) * S].rearrange(
                                "p (kc d) -> p kc d", kc=32),
                            vsrc[:, :, h * 128:(h + 1) * 128]), 16)
                    ld_v = s_vg.v

                    TE.wait_ge(c_rope.sem, p1_rope)
                    TE.wait_ge(s_vg.sem, ld_v)
                    VE.wait_ge(s_ct.sem, ld_consts)
                    n_kc = 0
                    n_qb = 0
                    for u in range(HPC):
                        for qb in range(NBLK):
                            nk = 4 * qb + 4
                            pend = []

                            def consume(TE=TE, u=u, qb=qb, nk=nk):
                                par, ctr, val, kcc = pend.pop(0)
                                _w(TE, ctr, val)
                                if kcc == 0:
                                    _w(TE, c_vat, n_qb)
                                    _w(TE, c_rcp, n_qb)
                                TE.matmul(
                                    ps_attn[:, 0:BLK],
                                    vsb[:, u * S + kcc * 128:u * S + (kcc + 1) * 128],
                                    psb[:, par * BLK:(par + 1) * BLK],
                                    start=(kcc == 0), stop=(kcc == nk - 1))
                                c_mav.inc(TE.matmul(
                                    ps_sum[0:1, 0:BLK], onc_sb[:, 0:1],
                                    psb[:, par * BLK:(par + 1) * BLK],
                                    start=(kcc == 0), stop=(kcc == nk - 1)))

                            for kc in range(nk):
                                par = n_kc % 2
                                _w(TE, c_exp, n_kc - 1)
                                c_msc.inc(TE.matmul(
                                    ps_sc[:, par * 512:par * 512 + BLK],
                                    ksb[:, u * S + kc * 128:u * S + (kc + 1) * 128],
                                    qsb[:, u * S + qb * BLK:u * S + (qb + 1) * BLK],
                                    start=True, stop=True))
                                AC.wait_ge(c_msc.sem, c_msc.v)
                                _w(AC, c_mav, n_kc - 1)
                                c_exp.inc(AC.activation(
                                    psb[:, par * BLK:(par + 1) * BLK],
                                    ps_sc[:, par * 512:par * 512 + BLK],
                                    AF.Exp, bias=smbias_sb[:, 0:1],
                                    scale=SM_SCALE))
                                diag = kc - 4 * qb
                                if diag >= 0:
                                    VE.wait_ge(c_exp.sem, c_exp.v)
                                    c_msk.inc(VE.tensor_tensor(
                                        psb[:, par * BLK:(par + 1) * BLK],
                                        psb[:, par * BLK:(par + 1) * BLK],
                                        mask_sb[:, diag * BLK:(diag + 1) * BLK],
                                        ALU.mult))
                                    pend.append((par, c_msk, c_msk.v, kc))
                                else:
                                    pend.append((par, c_exp, c_exp.v, kc))
                                if len(pend) > 1:
                                    consume()
                                n_kc += 1
                            consume()
                            VE.wait_ge(c_mav.sem, c_mav.v)
                            vi(VE.tensor_copy(sums_f[0:1, :], ps_sum[0:1, 0:BLK]))
                            vw()
                            vi(VE.reciprocal(rinvf[0:1, :], sums_f[0:1, :]))
                            vw()
                            c_rcp.inc(VE.tensor_copy(rinvb[0:1, :], rinvf[0:1, :]))
                            TE.wait_ge(c_rcp.sem, c_rcp.v)
                            _w(TE, c_vbc, n_qb)
                            c_mbc.inc(TE.matmul(ps_bc[:, 0:BLK], onr_sb[0:1, :],
                                                rinvb[0:1, :],
                                                start=True, stop=True))
                            VE.wait_ge(c_mbc.sem, c_mbc.v)
                            c_vbc.inc(VE.tensor_copy(binv[:, :], ps_bc[:, 0:BLK]))
                            VE.wait_ge(c_vbc.sem, c_vbc.v)
                            c_vat.inc(VE.tensor_tensor(
                                asb[:, u * S + qb * BLK:u * S + (qb + 1) * BLK],
                                ps_attn[:, 0:BLK], binv[:, :], ALU.mult))
                            n_qb += 1
                    p2_done = c_vat.v
                    sP2.close()

                    if DEBUG:
                        SY.wait_ge(c_vat.sem, p2_done)
                        c_lds.inc(SY.dma_start(dbg["k"][:, :], ksb[:, :]), 16)
                        c_lds.inc(SY.dma_start(dbg["q"][:, :], qsb[:, :]), 16)
                        c_lds.inc(SY.dma_start(dbg["v"][:, :], v_all[:, :]), 16)
                        c_lds.inc(SY.dma_start(dbg["attn"][:, :], asb[:, :]), 16)

                    wot = sbC("wot", [128, HPC * D], BF16, is_transpose_dest=True)
                    lowo = sbC("lowo", [LR, S], BF16)
                    osb = sbC("osb", [128, 2 * D], F32)
                    for c in range(HPC):
                        s_wo.inc(SY.dma_start(
                            wot[:, c * D:(c + 1) * D],
                            wo_bf[:, c * 128:(c + 1) * 128], transpose=True), 16)
                    ld_wot = s_wo.v

                    with ExitStack() as sL:
                        ps_lo3 = sL.enter_context(
                            nc.psum_tensor("ps_lo3", [128, 512], F32))
                        TE.wait_ge(c_vat.sem, p2_done)
                        for blk in range(NBLK):
                            _w(TE, c_vlo, blk)
                            for c in range(HPC):
                                ins = TE.matmul(
                                    ps_lo3[0:LR, 0:BLK],
                                    aot[:, c * LR:(c + 1) * LR],
                                    asb[:, c * S + blk * BLK:
                                        c * S + (blk + 1) * BLK],
                                    start=(c == 0), stop=(c == HPC - 1))
                            c_mlow.inc(ins)
                            VE.wait_ge(c_mlow.sem, c_mlow.v)
                            c_vlo.inc(VE.tensor_copy(
                                lowo[:, blk * BLK:(blk + 1) * BLK],
                                ps_lo3[0:LR, 0:BLK]))
                    with ExitStack() as sO:
                        ps_o = sO.enter_context(
                            nc.psum_tensor("ps_o", [128, 2 * D], F32))
                        TE.wait_ge(s_wo.sem, ld_wot)
                        TE.wait_ge(c_vlo.sem, c_vlo.v)
                        for tt in range(S // 128):
                            par = tt % 2
                            op = par * D
                            _w(TE, c_evo, 2 * (tt - 1))
                            for j in range(4):
                                for c in range(HPC):
                                    TE.matmul(
                                        ps_o[:, op + j * 512:op + (j + 1) * 512],
                                        asb[:, c * S + tt * 128:
                                            c * S + (tt + 1) * 128],
                                        wot[:, c * D + j * 512:c * D + (j + 1) * 512],
                                        start=(c == 0), stop=False)
                                ins = TE.matmul(
                                    ps_o[:, op + j * 512:op + (j + 1) * 512],
                                    lowo[:, tt * 128:(tt + 1) * 128],
                                    bot[:, j * 512:(j + 1) * 512],
                                    start=False, stop=True)
                            c_mo.inc(ins)
                            VE.wait_ge(c_mo.sem, c_mo.v)
                            _w(VE, s_cc[par], s_cc[par].v)
                            c_evo.inc(VE.tensor_copy(osb[:, op:op + 1024],
                                                     ps_o[:, op:op + 1024]))
                            AC.wait_ge(c_mo.sem, c_mo.v)
                            _w(AC, s_cc[par], s_cc[par].v)
                            c_evo.inc(AC.copy(osb[:, op + 1024:op + D],
                                              ps_o[:, op + 1024:op + D]))
                            SY.wait_ge(c_evo.sem, c_evo.v)
                            s_cc[par].inc(SY.dma_start(
                                ccin[tt * 128:(tt + 1) * 128, :],
                                osb[:, op:op + D]), 16)

                    if DEBUG:
                        SY.wait_ge(s_cc[0].sem, s_cc[0].v)
                        SY.wait_ge(s_cc[1].sem, s_cc[1].v)
                        c_lds.inc(SY.dma_start(dbg["ccin"][:, :], ccin[:, :]), 16)

                    GP.wait_ge(s_cc[0].sem, s_cc[0].v)
                    GP.wait_ge(s_cc[1].sem, s_cc[1].v)
                    c_cc.inc(GP.collective_compute(
                        "ReduceScatter", ALU.add,
                        replica_groups=[[0, 1, 2, 3], [4, 5, 6, 7]],
                        ins=[ccin[:, :].opt()], outs=[ccout[:, :].opt()]), 1)

            with ExitStack() as sD:
                sbD = lambda n, sh, dt, **kw: sD.enter_context(
                    nc.sbuf_tensor(n, sh, dt, **kw))
                psD = lambda n, sh, dt: sD.enter_context(nc.psum_tensor(n, sh, dt))
                xnown = sbD("xnown", [128, 16 * OWN], BF16)
                wgt_b = sbD("wgt_b", [128, 2 * 16 * BAND], BF16,
                            is_transpose_dest=True)
                wut_b = sbD("wut_b", [128, 2 * 16 * BAND], BF16,
                            is_transpose_dest=True)
                wdt_b = sbD("wdt_b", [128, TB * D], BF16, is_transpose_dest=True)
                hact = sbD("hact", [128, 2 * TB * OWN], BF16)
                sg_sb = sbD("sg_sb", [128, 2 * BLK], BF16)
                oacc = sbD("oacc", [128, 8 * D], F32)
                xcs_hi = sbD("xcs_hi", [1, 2 * BLK], BF16)
                xcs_lo = sbD("xcs_lo", [1, 2 * BLK], BF16)
                xcs_tmp = sbD("xcs_tmp", [1, BLK], F32)

                with ExitStack() as sP4a:
                    xts = sP4a.enter_context(
                        nc.sbuf_tensor("xts", [128, 2 * D], F32))
                    xnt4 = sP4a.enter_context(
                        nc.sbuf_tensor("xnt4", [128, 2 * D], BF16))
                    st4 = sP4a.enter_context(
                        nc.sbuf_tensor("st4", [128, 64], F32))
                    ps_t4 = sP4a.enter_context(
                        nc.psum_tensor("ps_t4", [128, 2 * 1024], BF16))
                    TE.wait_ge(c_evo.sem, c_evo.v)
                    p4_tr0 = c_tr.v
                    n_t4 = 0
                    xo_ld = []
                    for ti in range(8):
                        _w(SY, c_xn, c_xn.v + ti - 1)
                        s_xo[ti % 2].inc(SY.dma_start(
                            xts[:, (ti % 2) * D:(ti % 2 + 1) * D],
                            xow[ti * 128:(ti + 1) * 128, :]), 16)
                        xo_ld.append((s_xo[ti % 2], s_xo[ti % 2].v))
                    xnv4 = xnown[:, :].rearrange("p (dk q) -> p dk q", dk=16)
                    for ti in range(8):
                        tp = ti % 2
                        sp = tp * 32 + 8
                        xt = xts[:, tp * D:(tp + 1) * D]
                        AC.wait_ge(xo_ld[ti][0].sem, xo_ld[ti][1])
                        _w(AC, c_tr, p4_tr0 + 2 * ti - 2 if ti >= 2 else 0)
                        c_sq.inc(AC.activation(
                            xnt4[:, tp * D:(tp + 1) * D], xt, AF.Square,
                            accum_out=st4[:, sp:sp + 1]))
                        VE.wait_ge(xo_ld[ti][0].sem, xo_ld[ti][1])
                        vi(VE.reduce_sum(st4[:, sp + 1:sp + 2], xt, AX.X))
                        VE.wait_ge(c_sq.sem, c_sq.v)
                        vw()
                        vi(VE.tensor_scalar(st4[:, sp + 2:sp + 3],
                                            st4[:, sp + 1:sp + 2],
                                            1.0 / D, None, ALU.mult))
                        vw()
                        vi(VE.tensor_scalar(st4[:, sp + 3:sp + 4],
                                            st4[:, sp:sp + 1],
                                            1.0 / D, EPS, ALU.mult, ALU.add))
                        vw()
                        vi(VE.tensor_tensor(st4[:, sp + 4:sp + 5],
                                            st4[:, sp + 2:sp + 3],
                                            st4[:, sp + 2:sp + 3], ALU.mult))
                        vw()
                        c_vst.inc(VE.tensor_sub(st4[:, sp + 3:sp + 4],
                                                st4[:, sp + 3:sp + 4],
                                                st4[:, sp + 4:sp + 5]))
                        AC.wait_ge(c_vst.sem, c_vst.v)
                        ai(AC.activation(st4[:, sp + 5:sp + 6],
                                         st4[:, sp + 3:sp + 4], AF.Ln))
                        aw()
                        c_rstd.inc(AC.activation(st4[:, sp + 6:sp + 7],
                                                 st4[:, sp + 5:sp + 6],
                                                 AF.Exp, scale=-0.5))
                        VE.wait_ge(c_rstd.sem, c_rstd.v)
                        c_xn.inc(VE.tensor_scalar(
                            xnt4[:, tp * D:(tp + 1) * D], xt,
                            st4[:, sp + 2:sp + 3], st4[:, sp + 6:sp + 7],
                            ALU.subtract, ALU.mult))
                        TE.wait_ge(c_xn.sem, c_xn.v)
                        for grp in range(2):
                            _w(TE, c_ev, c_ev.v if n_t4 == 0 else c_ev.v - 1)
                            sl = (n_t4 % 2) * 1024
                            for j in range(8):
                                dk = grp * 8 + j
                                ins = TE.matmul(
                                    ps_t4[:, sl + j * 128:sl + (j + 1) * 128],
                                    xnt4[:, tp * D + dk * 128:
                                         tp * D + (dk + 1) * 128],
                                    idn_sb[:, :], is_transpose=True,
                                    start=True, stop=True)
                            c_tr.inc(ins)
                            n_t4 += 1
                            VE.wait_ge(c_tr.sem, c_tr.v)
                            c_ev.inc(VE.tensor_copy(
                                xnv4[:, grp * 8:(grp + 1) * 8,
                                     ti * 128:(ti + 1) * 128],
                                ps_t4[:, sl:sl + 1024].rearrange(
                                    "p (c q) -> p c q", c=8)))
                    TE.wait_ge(c_ev.sem, c_ev.v)
                    ps_cs = sP4a.enter_context(
                        nc.psum_tensor("ps_cs", [128, 2 * 512], F32))
                    for tb in range(2):
                        cs = ps_cs[0:1, tb * 512:tb * 512 + BLK]
                        for dk in range(16):
                            ins = TE.matmul(cs, onc_sb[:, 0:1],
                                            xnown[:, dk * OWN + tb * BLK:
                                                  dk * OWN + (tb + 1) * BLK],
                                            start=(dk == 0), stop=(dk == 15))
                        c_tr.inc(ins)
                        VE.wait_ge(c_tr.sem, c_tr.v)
                        vi(VE.tensor_copy(xcs_hi[0:1, tb * BLK:(tb + 1) * BLK],
                                          cs))
                        vw()
                        vi(VE.tensor_sub(xcs_tmp[0:1, :], cs,
                                         xcs_hi[0:1, tb * BLK:(tb + 1) * BLK]))
                        vw()
                        c_xcs.inc(VE.tensor_copy(
                            xcs_lo[0:1, tb * BLK:(tb + 1) * BLK],
                            xcs_tmp[0:1, :]))
                ps_g = psD("ps_g", [128, 2 * 512], F32)
                ps_u = psD("ps_u", [128, 2 * 512], F32)
                ps_d = psD("ps_d", [128, 2 * 1024], F32)

                n_gu = 0
                n_d = 0
                mg_band = {}
                md_band = {}
                acc_v = {}
                SY.wait_ge(s_fs[0].sem, s_fs[0].v)
                SY.wait_ge(s_fs[1].sem, s_fs[1].v)
                SY.wait_ge(c_ffc.sem, c_ffc.v)
                TE.wait_ge(c_xcs.sem, c_xcs.v)
                for band in range(NBAND):
                    fp = band % 2
                    if band >= 2:
                        _w(SY, c_mg, mg_band[band - 2])
                        _w(AC, c_mg, mg_band[band - 2])
                    wsrc = band * BAND
                    for dk in range(8):
                        s_gw.inc(SY.dma_start(
                            wgt_b[:, fp * 16 * BAND + dk * BAND:
                                  fp * 16 * BAND + (dk + 1) * BAND],
                            gw_bf[wsrc:wsrc + BAND, dk * 128:(dk + 1) * 128],
                            transpose=True), 16)
                        s_gw.inc(SY.dma_start(
                            wut_b[:, fp * 16 * BAND + dk * BAND:
                                  fp * 16 * BAND + (dk + 1) * BAND],
                            uw_bf[wsrc:wsrc + BAND, dk * 128:(dk + 1) * 128],
                            transpose=True), 16)
                    for dk in range(8, 16):
                        s_ga.inc(SY.dma_start(
                            wgt_b[:, fp * 16 * BAND + dk * BAND:
                                  fp * 16 * BAND + (dk + 1) * BAND],
                            gw_bf[wsrc:wsrc + BAND, dk * 128:(dk + 1) * 128],
                            transpose=True), 16)
                        s_ga.inc(SY.dma_start(
                            wut_b[:, fp * 16 * BAND + dk * BAND:
                                  fp * 16 * BAND + (dk + 1) * BAND],
                            uw_bf[wsrc:wsrc + BAND, dk * 128:(dk + 1) * 128],
                            transpose=True), 16)
                    gld = s_gw.v
                    gld_a = s_ga.v
                    if band >= 1:
                        _w(SY, c_md, md_band[band - 1])
                    for ic in range(TB):
                        s_dw.inc(SY.dma_start(
                            wdt_b[:, ic * D:(ic + 1) * D],
                            dw_bf[:, wsrc + ic * 128:wsrc + (ic + 1) * 128],
                            transpose=True), 16)
                    dld = s_dw.v

                    for ic in range(TB):
                        for tb in range(2):
                            par = n_gu % 2
                            _w(TE, c_sil, n_gu - 1)
                            _w(TE, c_vh, n_gu - 1)
                            TE.wait_ge(s_gw.sem, gld)
                            TE.wait_ge(s_ga.sem, gld_a)
                            ich = band * TB + ic
                            for dk in range(16):
                                TE.matmul(
                                    ps_g[:, par * 512:par * 512 + BLK],
                                    wgt_b[:, fp * 16 * BAND + dk * BAND + ic * 128:
                                          fp * 16 * BAND + dk * BAND + (ic + 1) * 128],
                                    xnown[:, dk * OWN + tb * BLK:
                                          dk * OWN + (tb + 1) * BLK],
                                    start=(dk == 0), stop=False)
                            TE.matmul(ps_g[:, par * 512:par * 512 + BLK],
                                      mner_sb[0:1, :],
                                      xcs_hi[0:1, tb * BLK:(tb + 1) * BLK],
                                      start=False, stop=False)
                            TE.matmul(ps_g[:, par * 512:par * 512 + BLK],
                                      mner_sb[0:1, :],
                                      xcs_lo[0:1, tb * BLK:(tb + 1) * BLK],
                                      start=False, stop=True)
                            for dk in range(16):
                                TE.matmul(
                                    ps_u[:, par * 512:par * 512 + BLK],
                                    wut_b[:, fp * 16 * BAND + dk * BAND + ic * 128:
                                          fp * 16 * BAND + dk * BAND + (ic + 1) * 128],
                                    xnown[:, dk * OWN + tb * BLK:
                                          dk * OWN + (tb + 1) * BLK],
                                    start=(dk == 0), stop=False)
                            TE.matmul(ps_u[:, par * 512:par * 512 + BLK],
                                      mner_sb[0:1, :],
                                      xcs_hi[0:1, tb * BLK:(tb + 1) * BLK],
                                      start=False, stop=False)
                            ins = TE.matmul(ps_u[:, par * 512:par * 512 + BLK],
                                            mner_sb[0:1, :],
                                            xcs_lo[0:1, tb * BLK:(tb + 1) * BLK],
                                            start=False, stop=True)
                            c_mg.inc(ins)
                            AC.wait_ge(c_mg.sem, c_mg.v)
                            _w(AC, c_vh, n_gu - 1)
                            c_sil.inc(AC.activation(
                                sg_sb[:, par * BLK:(par + 1) * BLK],
                                ps_g[:, par * 512:par * 512 + BLK], AF.Silu,
                                scale=gsc_sb[:, ich:ich + 1]))
                            VE.wait_ge(c_sil.sem, c_sil.v)
                            if band >= 2 and ic == 0 and tb == 0:
                                _w(VE, c_md, md_band[band - 2])
                            c_vh.inc(VE.scalar_tensor_tensor(
                                hact[:, fp * TB * OWN + ic * OWN + tb * BLK:
                                     fp * TB * OWN + ic * OWN + (tb + 1) * BLK],
                                ps_u[:, par * 512:par * 512 + BLK],
                                usc_sb[:, ich:ich + 1],
                                sg_sb[:, par * BLK:(par + 1) * BLK],
                                ALU.mult, ALU.mult))
                            n_gu += 1
                    mg_band[band] = c_mg.v
                    TE.wait_ge(s_dw.sem, dld)
                    TE.wait_ge(c_vh.sem, c_vh.v)
                    for tt in range(8):
                        for half in range(2):
                            dp = n_d % 2
                            _w(TE, c_vacc, n_d - 1)
                            for j in range(2):
                                h0 = half * 1024 + j * 512
                                for ic in range(TB):
                                    ins = TE.matmul(
                                        ps_d[:, dp * 1024 + j * 512:
                                             dp * 1024 + (j + 1) * 512],
                                        hact[:, fp * TB * OWN + ic * OWN + tt * 128:
                                             fp * TB * OWN + ic * OWN + (tt + 1) * 128],
                                        wdt_b[:, ic * D + h0:ic * D + h0 + 512],
                                        start=(ic == 0), stop=(ic == TB - 1))
                            c_md.inc(ins)
                            VE.wait_ge(c_md.sem, c_md.v)
                            dst = oacc[:, tt * D + half * 1024:
                                       tt * D + (half + 1) * 1024]
                            if band == 0:
                                c_vacc.inc(VE.tensor_copy(
                                    dst, ps_d[:, dp * 1024:(dp + 1) * 1024]))
                            else:
                                _w(VE, c_vacc, acc_v[(tt, half)])
                                c_vacc.inc(VE.tensor_add(
                                    dst, ps_d[:, dp * 1024:(dp + 1) * 1024], dst))
                            acc_v[(tt, half)] = c_vacc.v
                            n_d += 1
                    md_band[band] = c_md.v

                GP.wait_ge(c_vacc.sem, c_vacc.v)
                GP.wait_ge(c_cc.sem, 1)
                for tt in range(8):
                    s_fa[0].inc(GP.dma_start(oacc[:, tt * D:(tt + 1) * D],
                                             xow[tt * 128:(tt + 1) * 128, :],
                                             accum_op=ALU.add), 16)
                GP.wait_ge(s_fa[0].sem, s_fa[0].v)
                for tt in range(8):
                    s_fa[1].inc(GP.dma_start(oacc[:, tt * D:(tt + 1) * D],
                                             ccout[tt * 128:(tt + 1) * 128, :],
                                             accum_op=ALU.add), 16)
                SY.wait_ge(s_fa[0].sem, s_fa[0].v)
                SY.wait_ge(s_fa[1].sem, s_fa[1].v)
                for tt in range(8):
                    c_out.inc(SY.dma_start(out_own[tt * 128:(tt + 1) * 128, :],
                                           oacc[:, tt * D:(tt + 1) * D]), 16)
                SY.wait_ge(c_out.sem, c_out.v)
    return nc


def _get_nc():
    key = ("v1", DEBUG)
    if key not in _CACHE:
        _CACHE[key] = _build()
    return _CACHE[key]


def make_in_maps(inputs):
    f32 = lambda a: np.ascontiguousarray(np.asarray(a, dtype=np.float32))
    i32 = lambda a: np.ascontiguousarray(np.asarray(a, dtype=np.int32))
    x = f32(inputs["L0_attn_x"])
    starts = i32(inputs["L0_attn_lora_start_indices"]).reshape(1, NL)
    Wq = f32(inputs["L0_attn_Wq"])
    Wk = f32(inputs["L0_attn_Wk"])
    Wv = f32(inputs["L0_attn_Wv"])
    Wo = f32(inputs["L0_attn_Wo"])
    Aq = f32(inputs["L0_attn_Aq"]).reshape(LR, D)
    Ak = f32(inputs["L0_attn_Ak"]).reshape(LR, D)
    Av = f32(inputs["L0_attn_Av"]).reshape(LR, D)
    Ao = f32(inputs["L0_attn_Ao"]).reshape(LR, D)
    Bq = f32(inputs["L0_attn_Bq"])
    Bk = f32(inputs["L0_attn_Bk"])
    Bv = f32(inputs["L0_attn_Bv"])
    Bo = f32(inputs["L0_attn_Bo"])
    gwv = i32(inputs["L0_ffn_gate_w"])
    uwv = i32(inputs["L0_ffn_up_w"])
    dwv = i32(inputs["L0_ffn_down_w"])
    gs = f32(inputs["L0_ffn_gate_scale"]).reshape(II // 128, 128).T.copy()
    us = f32(inputs["L0_ffn_up_scale"]).reshape(II // 128, 128).T.copy()
    ds_ = f32(inputs["L0_ffn_down_scale"]).reshape(D // 128, 128).T.copy()

    CC, SS, swp, idn, masks, onc, onr, aidx = _host_consts()
    in_maps = []
    for c in range(8):
        g, r = c // 4, c % 4
        osl = slice(OSL * r, OSL * (r + 1))
        own = slice(OWN * r, OWN * (r + 1))
        in_maps.append({
            "xg": x[g],
            "xow": np.ascontiguousarray(x[g][own]),
            "wq": np.ascontiguousarray(Wq[osl]),
            "wk": np.ascontiguousarray(Wk[osl]),
            "wv": np.ascontiguousarray(Wv[osl]),
            "wo": np.ascontiguousarray(Wo[:, osl]),
            "aq": Aq, "ak": Ak, "av": Av,
            "ao": np.ascontiguousarray(Ao[:, osl]),
            "bq": np.ascontiguousarray(Bq[:, osl, :]),
            "bk": np.ascontiguousarray(Bk[:, osl, :]),
            "bv": np.ascontiguousarray(Bv[:, osl, :]),
            "bo": Bo,
            "gw": gwv, "uw": uwv, "dw": dwv,
            "gsc": gs, "usc": us, "dsc": ds_,
            "starts": starts,
            "bmeta": np.array([[float(g)]], np.float32),
            "own0m": np.array([[OWN * r]], np.int32),
            "cc": CC, "ss": SS, "masks": masks,
            "swp": swp, "idn": idn, "onc": onc, "onr": onr, "aidx": aidx,
            "mner": np.full((1, 128), -128.0, ml_dtypes.bfloat16),
        })
    return in_maps


def kernel(**inputs):
    in_maps = make_in_maps(inputs)
    nc = _get_nc()
    res = run_bass_kernel_spmd(nc, in_maps, core_ids=list(range(8)))
    out = np.empty((BATCH, S, D), np.float32)
    for c in range(8):
        g, r = c // 4, c % 4
        out[g, OWN * r:OWN * (r + 1), :] = res.results[c]["out_own"]
    if DEBUG:
        kernel.last_results = res
    return out


# version 09
# speedup vs baseline: 1.0195x; 1.0195x over previous
import math
from contextlib import ExitStack

import numpy as np
import ml_dtypes

import concourse.bass as bass
import concourse.mybir as mybir
from concourse.bass_utils import run_bass_kernel_spmd

F32 = mybir.dt.float32
BF16 = mybir.dt.bfloat16
I32 = mybir.dt.int32
AF = mybir.ActivationFunctionType
ALU = mybir.AluOpType
AX = mybir.AxisListType

BATCH = 2
S = 4096
D = 2048
DH = 128
HPC = 4
OSL = HPC * DH
OWN = 1024
II = 8192
NL, R = 4, 16
LR = NL * R
NBLK = 8
BLK = 512
EPS = 1e-5
SM_SCALE = 1.0 / math.sqrt(DH)
SM_BIAS = -4.0
BAND = 256
NBAND = II // BAND
TB = BAND // 128

DEBUG = False
_CACHE = {}


class Ctr:
    __slots__ = ("sem", "v")

    def __init__(self, stack, nc, name):
        self.sem = stack.enter_context(nc.semaphore(name))
        self.v = 0

    def inc(self, ins, n=1):
        ins.then_inc(self.sem, n)
        self.v += n
        return self.v


def _w(eng, ctr, v):
    if v > 0:
        eng.wait_ge(ctr.sem, v)


def _host_consts():
    inv_freq = 10000.0 ** (-np.arange(0, DH, 2, dtype=np.float64) / DH)
    ang = np.arange(S, dtype=np.float64)[:, None] * inv_freq[None, :]
    cos = np.cos(ang).T
    sin = np.sin(ang).T
    CC = np.empty((DH, S), np.float32)
    SS = np.empty((DH, S), np.float32)
    CC[0::2] = cos
    CC[1::2] = cos
    SS[0::2] = -sin
    SS[1::2] = sin
    swp = np.zeros((DH, DH), ml_dtypes.bfloat16)
    for i in range(0, DH, 2):
        swp[i + 1, i] = 1
        swp[i, i + 1] = 1
    idn = np.eye(DH, dtype=ml_dtypes.bfloat16)
    masks = np.zeros((4, 128, BLK), ml_dtypes.bfloat16)
    for v in range(4):
        kk = np.arange(128)[:, None] + 128 * v
        qq = np.arange(BLK)[None, :]
        masks[v] = (kk <= qq).astype(ml_dtypes.bfloat16)
    ones_col = np.ones((128, 1), ml_dtypes.bfloat16)
    ones_row = np.ones((1, 128), ml_dtypes.bfloat16)
    aidx = (np.arange(LR) // R).astype(np.float32)[:, None]
    return CC, SS, swp, idn, masks, ones_col, ones_row, aidx


def _build():
    nc = bass.Bass("TRN2", target_bir_lowering=False)
    ap = lambda name, shape, dt, kind="ExternalInput": nc.dram_tensor(
        name, shape, dt, kind=kind).ap()

    xg = ap("xg", [S, D], F32)
    xow = ap("xow", [OWN, D], F32)
    wq_i = ap("wq", [OSL, D], F32)
    wk_i = ap("wk", [OSL, D], F32)
    wv_i = ap("wv", [OSL, D], F32)
    wo_i = ap("wo", [D, OSL], F32)
    aq_i = ap("aq", [LR, D], F32)
    ak_i = ap("ak", [LR, D], F32)
    av_i = ap("av", [LR, D], F32)
    ao_i = ap("ao", [LR, OSL], F32)
    bq_i = ap("bq", [NL, OSL, R], F32)
    bk_i = ap("bk", [NL, OSL, R], F32)
    bv_i = ap("bv", [NL, OSL, R], F32)
    bo_i = ap("bo", [NL, D, R], F32)
    gw_i = ap("gw", [II, D], I32)
    uw_i = ap("uw", [II, D], I32)
    dw_i = ap("dw", [D, II], I32)
    gsc_i = ap("gsc", [128, II // 128], F32)
    usc_i = ap("usc", [128, II // 128], F32)
    dsrow_i = ap("dsrow", [1, D], F32)
    starts_i = ap("starts", [1, NL], I32)
    bmeta_i = ap("bmeta", [1, 1], F32)
    own0_i = ap("own0m", [1, 1], I32)
    cc_i = ap("cc", [DH, S], F32)
    ss_i = ap("ss", [DH, S], F32)
    mask_i = ap("masks", [4, 128, BLK], BF16)
    swp_i = ap("swp", [DH, DH], BF16)
    idn_i = ap("idn", [DH, DH], BF16)
    onc_i = ap("onc", [128, 1], BF16)
    onr_i = ap("onr", [1, 128], BF16)
    mner_i = ap("mner", [1, 128], BF16)
    aidx_i = ap("aidx", [LR, 1], F32)

    out_own = ap("out_own", [OWN, D], F32, "ExternalOutput")
    dbg = {}
    if DEBUG:
        dbg["k"] = ap("dbg_k", [128, HPC * S], BF16, "ExternalOutput")
        dbg["q"] = ap("dbg_q", [128, HPC * S], BF16, "ExternalOutput")
        dbg["v"] = ap("dbg_v", [S, OSL], BF16, "ExternalOutput")
        dbg["attn"] = ap("dbg_attn", [128, HPC * S], BF16, "ExternalOutput")
        dbg["ccin"] = ap("dbg_ccin", [S, D], F32, "ExternalOutput")

    xn_all = ap("xn_all", [16, 128, S], BF16, "Internal")
    v_all = ap("v_all", [S, OSL], BF16, "Internal")
    wo_bf = ap("wo_bf", [D, OSL], BF16, "Internal")
    gw_bf = ap("gw_bf", [II, D], BF16, "Internal")
    uw_bf = ap("uw_bf", [II, D], BF16, "Internal")
    dw_bf = ap("dw_bf", [D, II], BF16, "Internal")
    ccin = ap("ccin", [S, D], F32, "Internal")
    ccout = ap("ccout", [OWN, D], F32, "Internal")

    with ExitStack() as st:
        sb = lambda name, shape, dt, **kw: st.enter_context(
            nc.sbuf_tensor(name, shape, dt, **kw))
        C = lambda name: Ctr(st, nc, name)

        mask_sb = sb("mask_sb", [128, 4 * BLK], BF16)
        swp_sb = sb("swp_sb", [DH, DH], BF16)
        idn_sb = sb("idn_sb", [DH, DH], BF16)
        onc_sb = sb("onc_sb", [128, 1], BF16)
        onr_sb = sb("onr_sb", [1, 128], BF16)
        aidx_sb = sb("aidx_sb", [LR, 1], F32)
        gsc_sb = sb("gsc_sb", [128, II // 128], F32)
        usc_sb = sb("usc_sb", [128, II // 128], F32)
        dsrow_bf = sb("dsrow_bf", [1, D], BF16)
        dsfull = sb("dsfull", [128, D], BF16)
        ind_sb = sb("ind_sb", [LR, 1], F32)
        starts_sb = sb("starts_sb", [1, NL], I32)
        startsf_sb = sb("startsf_sb", [1, NL], F32)
        bmeta_sb = sb("bmeta_sb", [1, 1], F32)
        own0_sb = sb("own0_sb", [1, 1], I32)
        idsc_sb = sb("idsc_sb", [1, 8], F32)
        idbf_sb = sb("idbf_sb", [1, 1], BF16)
        smbias_sb = sb("smbias_sb", [128, 1], F32)
        aot = sb("aot", [128, HPC * LR], BF16)
        bot = sb("bot", [LR, D], BF16)
        mner_sb = sb("mner_sb", [1, 128], BF16)

        c_lds = C("c_lds")
        c_ldg = C("c_ldg")
        s_ct = C("s_ct")
        s_x = [C("s_x0"), C("s_x1")]
        s_rp = C("s_rp")
        s_vs = [C("s_vs0"), C("s_vs1")]
        s_vg = C("s_vg")
        s_wo = C("s_wo")
        s_cc = [C("s_cc0"), C("s_cc1")]
        s_gw = C("s_gw")
        s_ga = C("s_ga")
        s_dw = C("s_dw")
        s_xo = [C("s_xo0"), C("s_xo1")]
        s_fa = [C("s_fa0"), C("s_fa1")]
        c_dqd = C("c_dqd")
        c_ffc = C("c_ffc")
        c_xcs = C("c_xcs")
        c_tr = C("c_tr")
        c_ev = C("c_ev")
        c_mmq = C("c_mmq")
        c_mlow = C("c_mlow")
        c_vlow = C("c_vlow")
        c_qraw = C("c_qraw")
        c_msw = C("c_msw")
        c_rope = C("c_rope")
        c_vev = C("c_vev")
        c_sq = C("c_sq")
        c_vst = C("c_vst")
        c_rstd = C("c_rstd")
        c_xn = C("c_xn")
        c_msc = C("c_msc")
        c_exp = C("c_exp")
        c_msk = C("c_msk")
        c_mav = C("c_mav")
        c_rcp = C("c_rcp")
        c_mbc = C("c_mbc")
        c_vbc = C("c_vbc")
        c_vat = C("c_vat")
        c_vlo = C("c_vlo")
        c_mo = C("c_mo")
        c_evo = C("c_evo")
        c_scc = C("c_scc")
        c_cc = C("c_cc")
        c_mg = C("c_mg")
        c_sil = C("c_sil")
        c_vh = C("c_vh")
        c_md = C("c_md")
        c_vacc = C("c_vacc")
        c_fin = C("c_fin")
        c_out = C("c_out")
        c_p0 = C("c_p0")
        vse = C("vse")
        ase = C("ase")

        blk0 = st.enter_context(nc.Block())

        @blk0.sync
        def _(_sy):
            SY = nc.sync
            TE = nc.tensor
            VE = nc.vector
            AC = nc.scalar
            GP = nc.gpsimd

            def vi(ins):
                return vse.inc(ins)

            def vw():
                VE.wait_ge(vse.sem, vse.v)

            def ai(ins):
                return ase.inc(ins)

            def aw():
                AC.wait_ge(ase.sem, ase.v)

            with ExitStack() as sA:
                sbA = lambda n, sh, dt, **kw: sA.enter_context(
                    nc.sbuf_tensor(n, sh, dt, **kw))
                ksb = sbA("ksb", [128, HPC * S], BF16)
                qsb = sbA("qsb", [128, HPC * S], BF16)

                with ExitStack() as sB:
                    sbB = lambda n, sh, dt, **kw: sB.enter_context(
                        nc.sbuf_tensor(n, sh, dt, **kw))
                    psB = lambda n, sh, dt: sB.enter_context(
                        nc.psum_tensor(n, sh, dt))
                    wqt = sbB("wqt", [128, 16 * BLK], BF16)
                    wkt = sbB("wkt", [128, 16 * BLK], BF16)
                    wvt = sbB("wvt", [128, 16 * BLK], BF16)
                    aqt = sbB("aqt", [128, 16 * LR], BF16)
                    akt = sbB("akt", [128, 16 * LR], BF16)
                    avt = sbB("avt", [128, 16 * LR], BF16)
                    bqt = sbB("bqt", [LR, OSL], BF16)
                    bkt = sbB("bkt", [LR, OSL], BF16)
                    bvt = sbB("bvt", [LR, OSL], BF16)
                    xt_sb = sbB("xt_sb", [128, 2 * D], F32)
                    stats = sbB("stats", [128, 64], F32)
                    xntm = sbB("xntm", [128, 2 * D], BF16)
                    xnfm = sbB("xnfm", [128, 16 * BLK], BF16)
                    ccb = sbB("ccb", [128, BLK], F32)
                    ssb = sbB("ssb", [128, BLK], F32)
                    lowq = sbB("lowq", [LR, BLK], BF16)
                    lowk = sbB("lowk", [LR, BLK], BF16)
                    lowv = sbB("lowv", [LR, BLK], BF16)
                    qraw = sbB("qraw", [128, 2 * BLK], BF16)
                    rtmp1 = sbB("rtmp1", [128, BLK], F32)
                    rtmp2 = sbB("rtmp2", [128, BLK], F32)
                    vstg = sbB("vstg", [128, 2 * OSL], BF16)
                    ps_tr = psB("ps_tr", [128, 2 * 1024], BF16)
                    ps_qk0 = psB("ps_qk0", [128, 512], F32)
                    ps_qk1 = psB("ps_qk1", [128, 512], F32)
                    ps_v0 = psB("ps_v0", [128, 512], F32)
                    ps_v1 = psB("ps_v1", [128, 512], F32)
                    ps_low = psB("ps_low", [128, 512], F32)
                    ps_swap = psB("ps_swap", [128, 512], F32)

                    for dst, src in [
                        (swp_sb[:, :], swp_i[:, :]), (idn_sb[:, :], idn_i[:, :]),
                        (onc_sb[:, :], onc_i[:, :]), (onr_sb[:, :], onr_i[:, :]),
                        (aidx_sb[:, :], aidx_i[:, :]), (gsc_sb[:, :], gsc_i[:, :]),
                        (usc_sb[:, :], usc_i[:, :]),
                        (starts_sb[:, :], starts_i[:, :]),
                        (bmeta_sb[:, :], bmeta_i[:, :]),
                        (own0_sb[:, :], own0_i[:, :]),
                        (mner_sb[:, :], mner_i[:, :]),
                    ]:
                        s_ct.inc(SY.dma_start(dst, src), 16)
                    for v in range(4):
                        s_ct.inc(SY.dma_start(
                            mask_sb[:, v * BLK:(v + 1) * BLK], mask_i[v, :, :]), 16)
                    ld_consts = s_ct.v

                    c_ldg.inc(GP.dma_start(wo_bf[:, :], wo_i[:, :]), 16)
                    ld_wcast = c_ldg.v

                    TE.wait_ge(s_ct.sem, ld_consts)
                    with ExitStack() as sP0:
                        astg = sP0.enter_context(
                            nc.sbuf_tensor("astg", [LR, D], BF16))
                        bstk = sP0.enter_context(
                            nc.sbuf_tensor("bstk", [128, 16 * LR], BF16))
                        for adr, dstT, wid in [(aq_i, aqt, D), (ak_i, akt, D),
                                               (av_i, avt, D), (ao_i, aot, OSL)]:
                            _w(GP, c_tr, c_tr.v)
                            c_ldg.inc(GP.dma_start(astg[:, 0:wid], adr[:, :]), 16)
                            TE.wait_ge(c_ldg.sem, c_ldg.v)
                            _w(TE, c_ev, c_ev.v)
                            nch = wid // 128
                            for dk in range(nch):
                                ins = TE.matmul(
                                    ps_tr[:, dk * LR:(dk + 1) * LR],
                                    astg[:, dk * 128:(dk + 1) * 128],
                                    idn_sb[0:LR, 0:LR],
                                    is_transpose=True, start=True, stop=True)
                            c_tr.inc(ins)
                            VE.wait_ge(c_tr.sem, c_tr.v)
                            c_ev.inc(VE.tensor_copy(dstT[:, 0:nch * LR],
                                                    ps_tr[:, 0:nch * LR]))
                        for bdr, dstT, nch in [(bq_i, bqt, OSL // 128),
                                               (bk_i, bkt, OSL // 128),
                                               (bv_i, bvt, OSL // 128),
                                               (bo_i, bot, D // 128)]:
                            _w(GP, c_tr, c_tr.v)
                            for ch in range(nch):
                                src3 = bdr[:, ch * 128:(ch + 1) * 128, :].rearrange(
                                    "a p r -> p a r")
                                c_ldg.inc(GP.dma_start(
                                    bstk[:, ch * LR:(ch + 1) * LR].rearrange(
                                        "p (a r) -> p a r", a=NL), src3), 16)
                            TE.wait_ge(c_ldg.sem, c_ldg.v)
                            for half in range((nch + 7) // 8):
                                _w(TE, c_ev, c_ev.v)
                                lo = half * 8
                                hi = min(lo + 8, nch)
                                for ch in range(lo, hi):
                                    ins = TE.matmul(
                                        ps_tr[0:LR,
                                              (ch - lo) * 128:(ch - lo + 1) * 128],
                                        bstk[:, ch * LR:(ch + 1) * LR],
                                        idn_sb[:, :],
                                        is_transpose=True, start=True, stop=True)
                                c_tr.inc(ins)
                                VE.wait_ge(c_tr.sem, c_tr.v)
                                c_ev.inc(VE.tensor_copy(
                                    dstT[0:LR, lo * 128:hi * 128],
                                    ps_tr[0:LR, 0:(hi - lo) * 128]))

                    VE.wait_ge(s_ct.sem, ld_consts)
                    VE.memset(smbias_sb[:, :], SM_BIAS)
                    vi(VE.tensor_copy(startsf_sb[:, :], starts_sb[:, :]))
                    vw()
                    vi(VE.tensor_single_scalar(idsc_sb[0:1, 0:NL],
                                               startsf_sb[0:1, :],
                                               bmeta_sb[0:1, 0:1], ALU.is_le))
                    vw()
                    vi(VE.reduce_sum(idsc_sb[0:1, 4:5], idsc_sb[0:1, 0:NL], AX.X))
                    vw()
                    vi(VE.tensor_scalar(idsc_sb[0:1, 5:6], idsc_sb[0:1, 4:5],
                                        -1.0, 0.0, ALU.add, ALU.max))
                    vw()
                    vi(VE.tensor_single_scalar(idsc_sb[0:1, 6:7], idsc_sb[0:1, 5:6],
                                               3.0, ALU.min))
                    vw()
                    c_p0.inc(VE.tensor_copy(idbf_sb[0:1, 0:1], idsc_sb[0:1, 6:7]))
                    TE.wait_ge(c_p0.sem, c_p0.v)
                    c_p0.inc(TE.matmul(ps_qk0[0:LR, 0:1], onr_sb[0:1, 0:LR],
                                       idbf_sb[0:1, 0:1], start=True, stop=True))
                    VE.wait_ge(c_p0.sem, c_p0.v)
                    vi(VE.tensor_tensor(ind_sb[:, :], ps_qk0[0:LR, 0:1],
                                        aidx_sb[:, :], ALU.is_equal))
                    vw()
                    for dstT, wid in [(bqt, OSL), (bkt, OSL), (bvt, OSL), (bot, D)]:
                        ins = VE.tensor_scalar(dstT[:, 0:wid], dstT[:, 0:wid],
                                               ind_sb[:, 0:1], None, ALU.mult)
                    c_p0.inc(ins)
                    p0_done = c_p0.v

                    wstg = xnfm
                    wtr_v = {}
                    n_wc = 0
                    for wsrc, dstT in [(wq_i, wqt), (wk_i, wkt), (wv_i, wvt)]:
                        dstv = dstT[:, :].rearrange("p (dk q) -> p dk q", dk=16)
                        for fc in range(4):
                            sl = n_wc % 2
                            if n_wc >= 2:
                                GP.wait_ge(c_tr.sem, wtr_v[n_wc - 2])
                            c_ldg.inc(GP.dma_start(
                                wstg[:, sl * D:(sl + 1) * D],
                                wsrc[fc * 128:(fc + 1) * 128, :]), 16)
                            ldv = c_ldg.v
                            TE.wait_ge(c_ldg.sem, ldv)
                            for grp in range(2):
                                _w(TE, c_ev, c_ev.v if n_wc == 0 and grp == 0
                                   else c_ev.v - 1)
                                psl = (grp % 2) * 1024
                                for j in range(8):
                                    dk = grp * 8 + j
                                    ins = TE.matmul(
                                        ps_tr[:, psl + j * 128:psl + (j + 1) * 128],
                                        wstg[:, sl * D + dk * 128:
                                             sl * D + (dk + 1) * 128],
                                        idn_sb[:, :], is_transpose=True,
                                        start=True, stop=True)
                                c_tr.inc(ins)
                                VE.wait_ge(c_tr.sem, c_tr.v)
                                c_ev.inc(VE.tensor_copy(
                                    dstv[:, grp * 8:(grp + 1) * 8,
                                         fc * 128:(fc + 1) * 128],
                                    ps_tr[:, psl:psl + 1024].rearrange(
                                        "p (c q) -> p c q", c=8)))
                            wtr_v[n_wc] = c_tr.v
                            n_wc += 1

                    GP.wait_ge(s_ct.sem, ld_consts)
                    c_ffc.inc(GP.dma_start(gw_bf[:, :], gw_i[:, :]), 16)
                    c_ffc.inc(GP.dma_start(uw_bf[:, :], uw_i[:, :]), 16)
                    c_ffc.inc(GP.dma_start(dw_bf[:, :], dw_i[:, :]), 16)

                    c_ldg.inc(GP.dma_start(dsrow_bf[0:1, :], dsrow_i[0:1, :]), 16)
                    c_dsr = c_ldg.v
                    for j in range(4):
                        if j == 0:
                            TE.wait_ge(c_ldg.sem, c_dsr)
                        TE.wait_ge(vse.sem, vse.v)
                        c_p0.inc(TE.matmul(
                            ps_qk0[:, 0:BLK], onr_sb[0:1, :],
                            dsrow_bf[0:1, j * BLK:(j + 1) * BLK],
                            start=True, stop=True))
                        VE.wait_ge(c_p0.sem, c_p0.v)
                        vi(VE.tensor_copy(dsfull[:, j * BLK:(j + 1) * BLK],
                                          ps_qk0[:, 0:BLK]))
                    vw()
                    c_p0.inc(VE.tensor_copy(dsrow_bf[0:1, 0:1],
                                            dsrow_bf[0:1, 0:1]))
                    p0_done = c_p0.v

                    p0_tr = c_tr.v
                    n_qk = 0
                    n_v = 0
                    n_trf = 0
                    v_st = []
                    mmq_blk = {}
                    store_blk = {}
                    ld_rope = 0
                    for b in range(NBLK):
                        xld = []
                        for tt in range(4):
                            gi = 4 * b + tt
                            _w(SY, c_xn, gi - 1)
                            s_x[gi % 2].inc(SY.dma_start(
                                xt_sb[:, (gi % 2) * D:(gi % 2 + 1) * D],
                                xg[b * BLK + tt * 128:b * BLK + (tt + 1) * 128, :]),
                                16)
                            xld.append((s_x[gi % 2], s_x[gi % 2].v))
                        _w(SY, c_rope, 8 * b)
                        s_rp.inc(SY.dma_start(ccb[:, :],
                                              cc_i[:, b * BLK:(b + 1) * BLK]), 16)
                        s_rp.inc(SY.dma_start(ssb[:, :],
                                              ss_i[:, b * BLK:(b + 1) * BLK]), 16)
                        ld_rope = s_rp.v
                        for tt in range(4):
                            gi = 4 * b + tt
                            tp = gi % 2
                            sp = (gi % 2) * 32 + 8
                            xt = xt_sb[:, tp * D:(tp + 1) * D]
                            AC.wait_ge(xld[tt][0].sem, xld[tt][1])
                            _w(AC, c_tr, p0_tr + 2 * gi - 2 if gi >= 2 else 0)
                            c_sq.inc(AC.activation(
                                xntm[:, tp * D:(tp + 1) * D], xt, AF.Square,
                                accum_out=stats[:, sp:sp + 1]))
                            VE.wait_ge(xld[tt][0].sem, xld[tt][1])
                            vi(VE.reduce_sum(stats[:, sp + 1:sp + 2], xt, AX.X))
                            VE.wait_ge(c_sq.sem, c_sq.v)
                            vw()
                            vi(VE.tensor_scalar(stats[:, sp + 2:sp + 3],
                                                stats[:, sp + 1:sp + 2],
                                                1.0 / D, None, ALU.mult))
                            vw()
                            vi(VE.tensor_scalar(stats[:, sp + 3:sp + 4],
                                                stats[:, sp:sp + 1],
                                                1.0 / D, EPS, ALU.mult, ALU.add))
                            vw()
                            vi(VE.tensor_tensor(stats[:, sp + 4:sp + 5],
                                                stats[:, sp + 2:sp + 3],
                                                stats[:, sp + 2:sp + 3], ALU.mult))
                            vw()
                            c_vst.inc(VE.tensor_sub(stats[:, sp + 3:sp + 4],
                                                    stats[:, sp + 3:sp + 4],
                                                    stats[:, sp + 4:sp + 5]))
                            AC.wait_ge(c_vst.sem, c_vst.v)
                            ai(AC.activation(stats[:, sp + 5:sp + 6],
                                             stats[:, sp + 3:sp + 4], AF.Ln))
                            aw()
                            c_rstd.inc(AC.activation(stats[:, sp + 6:sp + 7],
                                                     stats[:, sp + 5:sp + 6],
                                                     AF.Exp, scale=-0.5))
                            VE.wait_ge(c_rstd.sem, c_rstd.v)
                            c_xn.inc(VE.tensor_scalar(
                                xntm[:, tp * D:(tp + 1) * D], xt,
                                stats[:, sp + 2:sp + 3], stats[:, sp + 6:sp + 7],
                                ALU.subtract, ALU.mult))
                            TE.wait_ge(c_xn.sem, c_xn.v)
                            xnv = xnfm[:, :].rearrange("p (dk q) -> p dk q", dk=16)
                            for grp in range(2):
                                _w(TE, c_ev, c_ev.v if n_trf == 0 else c_ev.v - 1)
                                sl = (n_trf % 2) * 1024
                                for j in range(8):
                                    dk = grp * 8 + j
                                    ins = TE.matmul(
                                        ps_tr[:, sl + j * 128:sl + (j + 1) * 128],
                                        xntm[:, tp * D + dk * 128:
                                             tp * D + (dk + 1) * 128],
                                        idn_sb[:, :], is_transpose=True,
                                        start=True, stop=True)
                                c_tr.inc(ins)
                                n_trf += 1
                                VE.wait_ge(c_tr.sem, c_tr.v)
                                if b >= 1 and tt == 0 and grp == 0:
                                    _w(VE, c_mmq, mmq_blk[b - 1])
                                c_ev.inc(VE.tensor_copy(
                                    xnv[:, grp * 8:(grp + 1) * 8,
                                        tt * 128:(tt + 1) * 128],
                                    ps_tr[:, sl:sl + 1024].rearrange(
                                        "p (c q) -> p c q", c=8)))
                        ev_blk = c_ev.v
                        xb = lambda dk: xnfm[:, dk * BLK:(dk + 1) * BLK]

                        TE.wait_ge(c_ev.sem, ev_blk)
                        if b == 0:
                            TE.wait_ge(c_p0.sem, p0_done)
                        for at, lowsb in [(aqt, lowq), (akt, lowk), (avt, lowv)]:
                            _w(TE, c_vlow, c_vlow.v)
                            for dk in range(16):
                                ins = TE.matmul(ps_low[0:LR, 0:BLK],
                                                at[:, dk * LR:(dk + 1) * LR],
                                                xb(dk), start=(dk == 0),
                                                stop=(dk == 15))
                            c_mlow.inc(ins)
                            VE.wait_ge(c_mlow.sem, c_mlow.v)
                            c_vlow.inc(VE.tensor_copy(lowsb[:, :],
                                                      ps_low[0:LR, 0:BLK]))

                        for wt, low, bt, dstsb in [(wqt, lowq, bqt, qsb),
                                                   (wkt, lowk, bkt, ksb)]:
                            for hh in range(HPC):
                                u = n_qk
                                pst = ps_qk0 if (u % 2) == 0 else ps_qk1
                                _w(TE, c_qraw, u - 1)
                                TE.wait_ge(c_vlow.sem, c_vlow.v)
                                for dk in range(16):
                                    TE.matmul(pst[:, 0:BLK],
                                              wt[:, dk * BLK + hh * 128:
                                                 dk * BLK + (hh + 1) * 128],
                                              xb(dk), start=(dk == 0), stop=False)
                                c_mmq.inc(TE.matmul(
                                    pst[:, 0:BLK], bt[:, hh * 128:(hh + 1) * 128],
                                    low[:, :], start=False, stop=True))
                                VE.wait_ge(c_mmq.sem, c_mmq.v)
                                _w(VE, c_msw, u - 1)
                                c_qraw.inc(VE.tensor_copy(
                                    qraw[:, (u % 2) * BLK:(u % 2 + 1) * BLK],
                                    pst[:, 0:BLK]))
                                TE.wait_ge(c_qraw.sem, c_qraw.v)
                                _w(TE, c_rope, u)
                                c_msw.inc(TE.matmul(
                                    ps_swap[:, 0:BLK], swp_sb[:, :],
                                    qraw[:, (u % 2) * BLK:(u % 2 + 1) * BLK],
                                    start=True, stop=True))
                                VE.wait_ge(s_rp.sem, ld_rope)
                                VE.wait_ge(c_qraw.sem, c_qraw.v)
                                vi(VE.tensor_tensor(
                                    rtmp1[:, :],
                                    qraw[:, (u % 2) * BLK:(u % 2 + 1) * BLK],
                                    ccb[:, :], ALU.mult))
                                VE.wait_ge(c_msw.sem, c_msw.v)
                                vi(VE.tensor_tensor(rtmp2[:, :], ps_swap[:, 0:BLK],
                                                    ssb[:, :], ALU.mult))
                                vw()
                                c_rope.inc(VE.tensor_add(
                                    dstsb[:, hh * S + b * BLK:
                                          hh * S + (b + 1) * BLK],
                                    rtmp1[:, :], rtmp2[:, :]))
                                n_qk += 1

                        for tt in range(4):
                            pst = ps_v0 if (n_v % 2) == 0 else ps_v1
                            _w(TE, c_vev, n_v - 1)
                            for dk in range(16):
                                TE.matmul(pst[:, 0:OSL],
                                          xnfm[:, dk * BLK + tt * 128:
                                               dk * BLK + (tt + 1) * 128],
                                          wvt[:, dk * BLK:(dk + 1) * BLK],
                                          start=(dk == 0), stop=False)
                            c_mmq.inc(TE.matmul(pst[:, 0:OSL],
                                                lowv[:, tt * 128:(tt + 1) * 128],
                                                bvt[:, :], start=False, stop=True))
                            VE.wait_ge(c_mmq.sem, c_mmq.v)
                            if n_v >= 2:
                                _w(VE, s_vs[n_v % 2], s_vs[n_v % 2].v)
                            c_vev.inc(VE.tensor_copy(
                                vstg[:, (n_v % 2) * OSL:(n_v % 2 + 1) * OSL],
                                pst[:, 0:OSL]))
                            SY.wait_ge(c_vev.sem, c_vev.v)
                            kc = b * 4 + tt
                            s_vs[n_v % 2].inc(SY.dma_start(
                                v_all[kc * 128:(kc + 1) * 128, :],
                                vstg[:, (n_v % 2) * OSL:(n_v % 2 + 1) * OSL]), 16)
                            n_v += 1
                        mmq_blk[b] = c_mmq.v
                    p1_rope = c_rope.v

                with ExitStack() as sC:
                    sbC = lambda n, sh, dt, **kw: sC.enter_context(
                        nc.sbuf_tensor(n, sh, dt, **kw))
                    psC = lambda n, sh, dt: sC.enter_context(
                        nc.psum_tensor(n, sh, dt))
                    vsb = sbC("vsb", [128, HPC * S], BF16)
                    asb = sbC("asb", [128, HPC * S], BF16)
                    psb = sbC("psb", [128, 2 * BLK], BF16)
                    sums_f = sbC("sums_f", [1, BLK], F32)
                    rinvf = sbC("rinvf", [1, BLK], F32)
                    rinvb = sbC("rinvb", [1, BLK], BF16)
                    binv = sbC("binv", [128, BLK], BF16)
                    sP2 = ExitStack()
                    ps_sc = sP2.enter_context(
                        nc.psum_tensor("ps_sc", [128, 2 * 512], F32))
                    ps_attn = sP2.enter_context(
                        nc.psum_tensor("ps_attn", [128, 512], F32))
                    ps_sum = sP2.enter_context(
                        nc.psum_tensor("ps_sum", [128, 512], F32))
                    ps_bc = sP2.enter_context(
                        nc.psum_tensor("ps_bc", [128, 512], F32))

                    SY.wait_ge(s_vs[0].sem, s_vs[0].v)
                    SY.wait_ge(s_vs[1].sem, s_vs[1].v)
                    vsrc = v_all.rearrange("(kc p) d -> p kc d", p=128)
                    for h in range(HPC):
                        s_vg.inc(SY.dma_start(
                            vsb[:, h * S:(h + 1) * S].rearrange(
                                "p (kc d) -> p kc d", kc=32),
                            vsrc[:, :, h * 128:(h + 1) * 128]), 16)
                    ld_v = s_vg.v

                    TE.wait_ge(c_rope.sem, p1_rope)
                    TE.wait_ge(s_vg.sem, ld_v)
                    VE.wait_ge(s_ct.sem, ld_consts)
                    n_kc = 0
                    n_qb = 0
                    for u in range(HPC):
                        for qb in range(NBLK):
                            nk = 4 * qb + 4
                            pend = []

                            def consume(TE=TE, u=u, qb=qb, nk=nk):
                                par, ctr, val, kcc = pend.pop(0)
                                _w(TE, ctr, val)
                                if kcc == 0:
                                    _w(TE, c_vat, n_qb)
                                    _w(TE, c_rcp, n_qb)
                                TE.matmul(
                                    ps_attn[:, 0:BLK],
                                    vsb[:, u * S + kcc * 128:u * S + (kcc + 1) * 128],
                                    psb[:, par * BLK:(par + 1) * BLK],
                                    start=(kcc == 0), stop=(kcc == nk - 1))
                                c_mav.inc(TE.matmul(
                                    ps_sum[0:1, 0:BLK], onc_sb[:, 0:1],
                                    psb[:, par * BLK:(par + 1) * BLK],
                                    start=(kcc == 0), stop=(kcc == nk - 1)))

                            for kc in range(nk):
                                par = n_kc % 2
                                _w(TE, c_exp, n_kc - 1)
                                c_msc.inc(TE.matmul(
                                    ps_sc[:, par * 512:par * 512 + BLK],
                                    ksb[:, u * S + kc * 128:u * S + (kc + 1) * 128],
                                    qsb[:, u * S + qb * BLK:u * S + (qb + 1) * BLK],
                                    start=True, stop=True))
                                AC.wait_ge(c_msc.sem, c_msc.v)
                                _w(AC, c_mav, n_kc - 1)
                                c_exp.inc(AC.activation(
                                    psb[:, par * BLK:(par + 1) * BLK],
                                    ps_sc[:, par * 512:par * 512 + BLK],
                                    AF.Exp, bias=smbias_sb[:, 0:1],
                                    scale=SM_SCALE))
                                diag = kc - 4 * qb
                                if diag >= 0:
                                    VE.wait_ge(c_exp.sem, c_exp.v)
                                    c_msk.inc(VE.tensor_tensor(
                                        psb[:, par * BLK:(par + 1) * BLK],
                                        psb[:, par * BLK:(par + 1) * BLK],
                                        mask_sb[:, diag * BLK:(diag + 1) * BLK],
                                        ALU.mult))
                                    pend.append((par, c_msk, c_msk.v, kc))
                                else:
                                    pend.append((par, c_exp, c_exp.v, kc))
                                if len(pend) > 1:
                                    consume()
                                n_kc += 1
                            consume()
                            VE.wait_ge(c_mav.sem, c_mav.v)
                            vi(VE.tensor_copy(sums_f[0:1, :], ps_sum[0:1, 0:BLK]))
                            vw()
                            vi(VE.reciprocal(rinvf[0:1, :], sums_f[0:1, :]))
                            vw()
                            c_rcp.inc(VE.tensor_copy(rinvb[0:1, :], rinvf[0:1, :]))
                            TE.wait_ge(c_rcp.sem, c_rcp.v)
                            _w(TE, c_vbc, n_qb)
                            c_mbc.inc(TE.matmul(ps_bc[:, 0:BLK], onr_sb[0:1, :],
                                                rinvb[0:1, :],
                                                start=True, stop=True))
                            VE.wait_ge(c_mbc.sem, c_mbc.v)
                            c_vbc.inc(VE.tensor_copy(binv[:, :], ps_bc[:, 0:BLK]))
                            VE.wait_ge(c_vbc.sem, c_vbc.v)
                            c_vat.inc(VE.tensor_tensor(
                                asb[:, u * S + qb * BLK:u * S + (qb + 1) * BLK],
                                ps_attn[:, 0:BLK], binv[:, :], ALU.mult))
                            n_qb += 1
                    p2_done = c_vat.v
                    sP2.close()

                    if DEBUG:
                        SY.wait_ge(c_vat.sem, p2_done)
                        c_lds.inc(SY.dma_start(dbg["k"][:, :], ksb[:, :]), 16)
                        c_lds.inc(SY.dma_start(dbg["q"][:, :], qsb[:, :]), 16)
                        c_lds.inc(SY.dma_start(dbg["v"][:, :], v_all[:, :]), 16)
                        c_lds.inc(SY.dma_start(dbg["attn"][:, :], asb[:, :]), 16)

                    wot = sbC("wot", [128, HPC * D], BF16, is_transpose_dest=True)
                    lowo = sbC("lowo", [LR, S], BF16)
                    osb = sbC("osb", [128, 2 * D], F32)
                    for c in range(HPC):
                        s_wo.inc(SY.dma_start(
                            wot[:, c * D:(c + 1) * D],
                            wo_bf[:, c * 128:(c + 1) * 128], transpose=True), 16)
                    ld_wot = s_wo.v

                    with ExitStack() as sL:
                        ps_lo3 = sL.enter_context(
                            nc.psum_tensor("ps_lo3", [128, 512], F32))
                        TE.wait_ge(c_vat.sem, p2_done)
                        for blk in range(NBLK):
                            _w(TE, c_vlo, blk)
                            for c in range(HPC):
                                ins = TE.matmul(
                                    ps_lo3[0:LR, 0:BLK],
                                    aot[:, c * LR:(c + 1) * LR],
                                    asb[:, c * S + blk * BLK:
                                        c * S + (blk + 1) * BLK],
                                    start=(c == 0), stop=(c == HPC - 1))
                            c_mlow.inc(ins)
                            VE.wait_ge(c_mlow.sem, c_mlow.v)
                            c_vlo.inc(VE.tensor_copy(
                                lowo[:, blk * BLK:(blk + 1) * BLK],
                                ps_lo3[0:LR, 0:BLK]))
                    with ExitStack() as sO:
                        ps_o = sO.enter_context(
                            nc.psum_tensor("ps_o", [128, 2 * D], F32))
                        TE.wait_ge(s_wo.sem, ld_wot)
                        TE.wait_ge(c_vlo.sem, c_vlo.v)
                        for tt in range(S // 128):
                            par = tt % 2
                            op = par * D
                            _w(TE, c_evo, 2 * (tt - 1))
                            for j in range(4):
                                for c in range(HPC):
                                    TE.matmul(
                                        ps_o[:, op + j * 512:op + (j + 1) * 512],
                                        asb[:, c * S + tt * 128:
                                            c * S + (tt + 1) * 128],
                                        wot[:, c * D + j * 512:c * D + (j + 1) * 512],
                                        start=(c == 0), stop=False)
                                ins = TE.matmul(
                                    ps_o[:, op + j * 512:op + (j + 1) * 512],
                                    lowo[:, tt * 128:(tt + 1) * 128],
                                    bot[:, j * 512:(j + 1) * 512],
                                    start=False, stop=True)
                            c_mo.inc(ins)
                            VE.wait_ge(c_mo.sem, c_mo.v)
                            _w(VE, s_cc[par], s_cc[par].v)
                            c_evo.inc(VE.tensor_copy(osb[:, op:op + 1024],
                                                     ps_o[:, op:op + 1024]))
                            AC.wait_ge(c_mo.sem, c_mo.v)
                            _w(AC, s_cc[par], s_cc[par].v)
                            c_evo.inc(AC.copy(osb[:, op + 1024:op + D],
                                              ps_o[:, op + 1024:op + D]))
                            SY.wait_ge(c_evo.sem, c_evo.v)
                            s_cc[par].inc(SY.dma_start(
                                ccin[tt * 128:(tt + 1) * 128, :],
                                osb[:, op:op + D]), 16)

                    if DEBUG:
                        SY.wait_ge(s_cc[0].sem, s_cc[0].v)
                        SY.wait_ge(s_cc[1].sem, s_cc[1].v)
                        c_lds.inc(SY.dma_start(dbg["ccin"][:, :], ccin[:, :]), 16)

                    GP.wait_ge(s_cc[0].sem, s_cc[0].v)
                    GP.wait_ge(s_cc[1].sem, s_cc[1].v)
                    c_cc.inc(GP.collective_compute(
                        "ReduceScatter", ALU.add,
                        replica_groups=[[0, 1, 2, 3], [4, 5, 6, 7]],
                        ins=[ccin[:, :].opt()], outs=[ccout[:, :].opt()]), 1)

            with ExitStack() as sD:
                sbD = lambda n, sh, dt, **kw: sD.enter_context(
                    nc.sbuf_tensor(n, sh, dt, **kw))
                psD = lambda n, sh, dt: sD.enter_context(nc.psum_tensor(n, sh, dt))
                xnown = sbD("xnown", [128, 16 * OWN], BF16)
                wgt_b = sbD("wgt_b", [128, 2 * 16 * BAND], BF16,
                            is_transpose_dest=True)
                wut_b = sbD("wut_b", [128, 2 * 16 * BAND], BF16,
                            is_transpose_dest=True)
                wdt_b = sbD("wdt_b", [128, TB * D], BF16, is_transpose_dest=True)
                hact = sbD("hact", [128, 2 * TB * OWN], BF16)
                sg_sb = sbD("sg_sb", [128, 2 * BLK], BF16)
                oacc = sbD("oacc", [128, 8 * D], F32)
                xcs_hi = sbD("xcs_hi", [1, 2 * BLK], BF16)
                xcs_lo = sbD("xcs_lo", [1, 2 * BLK], BF16)
                xcs_tmp = sbD("xcs_tmp", [1, BLK], F32)

                with ExitStack() as sP4a:
                    xts = sP4a.enter_context(
                        nc.sbuf_tensor("xts", [128, 2 * D], F32))
                    xnt4 = sP4a.enter_context(
                        nc.sbuf_tensor("xnt4", [128, 2 * D], BF16))
                    st4 = sP4a.enter_context(
                        nc.sbuf_tensor("st4", [128, 64], F32))
                    ps_t4 = sP4a.enter_context(
                        nc.psum_tensor("ps_t4", [128, 2 * 1024], BF16))
                    TE.wait_ge(c_evo.sem, c_evo.v)
                    p4_tr0 = c_tr.v
                    n_t4 = 0
                    xo_ld = []
                    for ti in range(8):
                        _w(SY, c_xn, c_xn.v + ti - 1)
                        s_xo[ti % 2].inc(SY.dma_start(
                            xts[:, (ti % 2) * D:(ti % 2 + 1) * D],
                            xow[ti * 128:(ti + 1) * 128, :]), 16)
                        xo_ld.append((s_xo[ti % 2], s_xo[ti % 2].v))
                    xnv4 = xnown[:, :].rearrange("p (dk q) -> p dk q", dk=16)
                    for ti in range(8):
                        tp = ti % 2
                        sp = tp * 32 + 8
                        xt = xts[:, tp * D:(tp + 1) * D]
                        AC.wait_ge(xo_ld[ti][0].sem, xo_ld[ti][1])
                        _w(AC, c_tr, p4_tr0 + 2 * ti - 2 if ti >= 2 else 0)
                        c_sq.inc(AC.activation(
                            xnt4[:, tp * D:(tp + 1) * D], xt, AF.Square,
                            accum_out=st4[:, sp:sp + 1]))
                        VE.wait_ge(xo_ld[ti][0].sem, xo_ld[ti][1])
                        vi(VE.reduce_sum(st4[:, sp + 1:sp + 2], xt, AX.X))
                        VE.wait_ge(c_sq.sem, c_sq.v)
                        vw()
                        vi(VE.tensor_scalar(st4[:, sp + 2:sp + 3],
                                            st4[:, sp + 1:sp + 2],
                                            1.0 / D, None, ALU.mult))
                        vw()
                        vi(VE.tensor_scalar(st4[:, sp + 3:sp + 4],
                                            st4[:, sp:sp + 1],
                                            1.0 / D, EPS, ALU.mult, ALU.add))
                        vw()
                        vi(VE.tensor_tensor(st4[:, sp + 4:sp + 5],
                                            st4[:, sp + 2:sp + 3],
                                            st4[:, sp + 2:sp + 3], ALU.mult))
                        vw()
                        c_vst.inc(VE.tensor_sub(st4[:, sp + 3:sp + 4],
                                                st4[:, sp + 3:sp + 4],
                                                st4[:, sp + 4:sp + 5]))
                        AC.wait_ge(c_vst.sem, c_vst.v)
                        ai(AC.activation(st4[:, sp + 5:sp + 6],
                                         st4[:, sp + 3:sp + 4], AF.Ln))
                        aw()
                        c_rstd.inc(AC.activation(st4[:, sp + 6:sp + 7],
                                                 st4[:, sp + 5:sp + 6],
                                                 AF.Exp, scale=-0.5))
                        VE.wait_ge(c_rstd.sem, c_rstd.v)
                        c_xn.inc(VE.tensor_scalar(
                            xnt4[:, tp * D:(tp + 1) * D], xt,
                            st4[:, sp + 2:sp + 3], st4[:, sp + 6:sp + 7],
                            ALU.subtract, ALU.mult))
                        TE.wait_ge(c_xn.sem, c_xn.v)
                        for grp in range(2):
                            _w(TE, c_ev, c_ev.v if n_t4 == 0 else c_ev.v - 1)
                            sl = (n_t4 % 2) * 1024
                            for j in range(8):
                                dk = grp * 8 + j
                                ins = TE.matmul(
                                    ps_t4[:, sl + j * 128:sl + (j + 1) * 128],
                                    xnt4[:, tp * D + dk * 128:
                                         tp * D + (dk + 1) * 128],
                                    idn_sb[:, :], is_transpose=True,
                                    start=True, stop=True)
                            c_tr.inc(ins)
                            n_t4 += 1
                            VE.wait_ge(c_tr.sem, c_tr.v)
                            c_ev.inc(VE.tensor_copy(
                                xnv4[:, grp * 8:(grp + 1) * 8,
                                     ti * 128:(ti + 1) * 128],
                                ps_t4[:, sl:sl + 1024].rearrange(
                                    "p (c q) -> p c q", c=8)))
                    TE.wait_ge(c_ev.sem, c_ev.v)
                    ps_cs = sP4a.enter_context(
                        nc.psum_tensor("ps_cs", [128, 2 * 512], F32))
                    for tb in range(2):
                        cs = ps_cs[0:1, tb * 512:tb * 512 + BLK]
                        for dk in range(16):
                            ins = TE.matmul(cs, onc_sb[:, 0:1],
                                            xnown[:, dk * OWN + tb * BLK:
                                                  dk * OWN + (tb + 1) * BLK],
                                            start=(dk == 0), stop=(dk == 15))
                        c_tr.inc(ins)
                        VE.wait_ge(c_tr.sem, c_tr.v)
                        vi(VE.tensor_copy(xcs_hi[0:1, tb * BLK:(tb + 1) * BLK],
                                          cs))
                        vw()
                        vi(VE.tensor_sub(xcs_tmp[0:1, :], cs,
                                         xcs_hi[0:1, tb * BLK:(tb + 1) * BLK]))
                        vw()
                        c_xcs.inc(VE.tensor_copy(
                            xcs_lo[0:1, tb * BLK:(tb + 1) * BLK],
                            xcs_tmp[0:1, :]))
                ps_g = psD("ps_g", [128, 2 * 512], F32)
                ps_u = psD("ps_u", [128, 2 * 512], F32)
                ps_d = psD("ps_d", [128, 2 * 1024], F32)

                n_gu = 0
                n_d = 0
                mg_band = {}
                md_band = {}
                acc_v = {}
                SY.wait_ge(c_ffc.sem, c_ffc.v)
                TE.wait_ge(c_xcs.sem, c_xcs.v)
                for band in range(NBAND):
                    fp = band % 2
                    if band >= 2:
                        _w(SY, c_mg, mg_band[band - 2])
                        _w(AC, c_mg, mg_band[band - 2])
                    wsrc = band * BAND
                    for dk in range(8):
                        s_gw.inc(SY.dma_start(
                            wgt_b[:, fp * 16 * BAND + dk * BAND:
                                  fp * 16 * BAND + (dk + 1) * BAND],
                            gw_bf[wsrc:wsrc + BAND, dk * 128:(dk + 1) * 128],
                            transpose=True), 16)
                        s_gw.inc(SY.dma_start(
                            wut_b[:, fp * 16 * BAND + dk * BAND:
                                  fp * 16 * BAND + (dk + 1) * BAND],
                            uw_bf[wsrc:wsrc + BAND, dk * 128:(dk + 1) * 128],
                            transpose=True), 16)
                    for dk in range(8, 16):
                        s_ga.inc(SY.dma_start(
                            wgt_b[:, fp * 16 * BAND + dk * BAND:
                                  fp * 16 * BAND + (dk + 1) * BAND],
                            gw_bf[wsrc:wsrc + BAND, dk * 128:(dk + 1) * 128],
                            transpose=True), 16)
                        s_ga.inc(SY.dma_start(
                            wut_b[:, fp * 16 * BAND + dk * BAND:
                                  fp * 16 * BAND + (dk + 1) * BAND],
                            uw_bf[wsrc:wsrc + BAND, dk * 128:(dk + 1) * 128],
                            transpose=True), 16)
                    gld = s_gw.v
                    gld_a = s_ga.v
                    if band >= 1:
                        _w(SY, c_md, md_band[band - 1])
                    for ic in range(TB):
                        s_dw.inc(SY.dma_start(
                            wdt_b[:, ic * D:(ic + 1) * D],
                            dw_bf[:, wsrc + ic * 128:wsrc + (ic + 1) * 128],
                            transpose=True), 16)
                    dld = s_dw.v

                    for ic in range(TB):
                        for tb in range(2):
                            par = n_gu % 2
                            _w(TE, c_sil, n_gu - 1)
                            _w(TE, c_vh, n_gu - 1)
                            TE.wait_ge(s_gw.sem, gld)
                            TE.wait_ge(s_ga.sem, gld_a)
                            ich = band * TB + ic
                            for dk in range(16):
                                TE.matmul(
                                    ps_g[:, par * 512:par * 512 + BLK],
                                    wgt_b[:, fp * 16 * BAND + dk * BAND + ic * 128:
                                          fp * 16 * BAND + dk * BAND + (ic + 1) * 128],
                                    xnown[:, dk * OWN + tb * BLK:
                                          dk * OWN + (tb + 1) * BLK],
                                    start=(dk == 0), stop=False)
                            TE.matmul(ps_g[:, par * 512:par * 512 + BLK],
                                      mner_sb[0:1, :],
                                      xcs_hi[0:1, tb * BLK:(tb + 1) * BLK],
                                      start=False, stop=False)
                            TE.matmul(ps_g[:, par * 512:par * 512 + BLK],
                                      mner_sb[0:1, :],
                                      xcs_lo[0:1, tb * BLK:(tb + 1) * BLK],
                                      start=False, stop=True)
                            for dk in range(16):
                                TE.matmul(
                                    ps_u[:, par * 512:par * 512 + BLK],
                                    wut_b[:, fp * 16 * BAND + dk * BAND + ic * 128:
                                          fp * 16 * BAND + dk * BAND + (ic + 1) * 128],
                                    xnown[:, dk * OWN + tb * BLK:
                                          dk * OWN + (tb + 1) * BLK],
                                    start=(dk == 0), stop=False)
                            TE.matmul(ps_u[:, par * 512:par * 512 + BLK],
                                      mner_sb[0:1, :],
                                      xcs_hi[0:1, tb * BLK:(tb + 1) * BLK],
                                      start=False, stop=False)
                            ins = TE.matmul(ps_u[:, par * 512:par * 512 + BLK],
                                            mner_sb[0:1, :],
                                            xcs_lo[0:1, tb * BLK:(tb + 1) * BLK],
                                            start=False, stop=True)
                            c_mg.inc(ins)
                            AC.wait_ge(c_mg.sem, c_mg.v)
                            _w(AC, c_vh, n_gu - 1)
                            c_sil.inc(AC.activation(
                                sg_sb[:, par * BLK:(par + 1) * BLK],
                                ps_g[:, par * 512:par * 512 + BLK], AF.Silu,
                                scale=gsc_sb[:, ich:ich + 1]))
                            VE.wait_ge(c_sil.sem, c_sil.v)
                            if band >= 2 and ic == 0 and tb == 0:
                                _w(VE, c_md, md_band[band - 2])
                            c_vh.inc(VE.scalar_tensor_tensor(
                                hact[:, fp * TB * OWN + ic * OWN + tb * BLK:
                                     fp * TB * OWN + ic * OWN + (tb + 1) * BLK],
                                ps_u[:, par * 512:par * 512 + BLK],
                                usc_sb[:, ich:ich + 1],
                                sg_sb[:, par * BLK:(par + 1) * BLK],
                                ALU.mult, ALU.mult))
                            n_gu += 1
                    mg_band[band] = c_mg.v
                    VE.wait_ge(s_dw.sem, dld)
                    for ic in range(TB):
                        ins = VE.scalar_tensor_tensor(
                            wdt_b[:, ic * D:(ic + 1) * D],
                            wdt_b[:, ic * D:(ic + 1) * D],
                            128.0, dsfull[:, :], ALU.subtract, ALU.mult)
                    c_dqd.inc(ins)
                    TE.wait_ge(c_dqd.sem, c_dqd.v)
                    TE.wait_ge(c_vh.sem, c_vh.v)
                    for tt in range(8):
                        for half in range(2):
                            dp = n_d % 2
                            _w(TE, c_vacc, n_d - 1)
                            for j in range(2):
                                h0 = half * 1024 + j * 512
                                for ic in range(TB):
                                    ins = TE.matmul(
                                        ps_d[:, dp * 1024 + j * 512:
                                             dp * 1024 + (j + 1) * 512],
                                        hact[:, fp * TB * OWN + ic * OWN + tt * 128:
                                             fp * TB * OWN + ic * OWN + (tt + 1) * 128],
                                        wdt_b[:, ic * D + h0:ic * D + h0 + 512],
                                        start=(ic == 0), stop=(ic == TB - 1))
                            c_md.inc(ins)
                            VE.wait_ge(c_md.sem, c_md.v)
                            dst = oacc[:, tt * D + half * 1024:
                                       tt * D + (half + 1) * 1024]
                            if band == 0:
                                c_vacc.inc(VE.tensor_copy(
                                    dst, ps_d[:, dp * 1024:(dp + 1) * 1024]))
                            else:
                                _w(VE, c_vacc, acc_v[(tt, half)])
                                c_vacc.inc(VE.tensor_add(
                                    dst, ps_d[:, dp * 1024:(dp + 1) * 1024], dst))
                            acc_v[(tt, half)] = c_vacc.v
                            n_d += 1
                    md_band[band] = c_md.v

                GP.wait_ge(c_vacc.sem, c_vacc.v)
                GP.wait_ge(c_cc.sem, 1)
                for tt in range(8):
                    s_fa[0].inc(GP.dma_start(oacc[:, tt * D:(tt + 1) * D],
                                             xow[tt * 128:(tt + 1) * 128, :],
                                             accum_op=ALU.add), 16)
                GP.wait_ge(s_fa[0].sem, s_fa[0].v)
                for tt in range(8):
                    s_fa[1].inc(GP.dma_start(oacc[:, tt * D:(tt + 1) * D],
                                             ccout[tt * 128:(tt + 1) * 128, :],
                                             accum_op=ALU.add), 16)
                SY.wait_ge(s_fa[0].sem, s_fa[0].v)
                SY.wait_ge(s_fa[1].sem, s_fa[1].v)
                for tt in range(8):
                    c_out.inc(SY.dma_start(out_own[tt * 128:(tt + 1) * 128, :],
                                           oacc[:, tt * D:(tt + 1) * D]), 16)
                SY.wait_ge(c_out.sem, c_out.v)
    return nc


def _get_nc():
    key = ("v1", DEBUG)
    if key not in _CACHE:
        _CACHE[key] = _build()
    return _CACHE[key]


def make_in_maps(inputs):
    f32 = lambda a: np.ascontiguousarray(np.asarray(a, dtype=np.float32))
    i32 = lambda a: np.ascontiguousarray(np.asarray(a, dtype=np.int32))
    x = f32(inputs["L0_attn_x"])
    starts = i32(inputs["L0_attn_lora_start_indices"]).reshape(1, NL)
    Wq = f32(inputs["L0_attn_Wq"])
    Wk = f32(inputs["L0_attn_Wk"])
    Wv = f32(inputs["L0_attn_Wv"])
    Wo = f32(inputs["L0_attn_Wo"])
    Aq = f32(inputs["L0_attn_Aq"]).reshape(LR, D)
    Ak = f32(inputs["L0_attn_Ak"]).reshape(LR, D)
    Av = f32(inputs["L0_attn_Av"]).reshape(LR, D)
    Ao = f32(inputs["L0_attn_Ao"]).reshape(LR, D)
    Bq = f32(inputs["L0_attn_Bq"])
    Bk = f32(inputs["L0_attn_Bk"])
    Bv = f32(inputs["L0_attn_Bv"])
    Bo = f32(inputs["L0_attn_Bo"])
    gwv = i32(inputs["L0_ffn_gate_w"])
    uwv = i32(inputs["L0_ffn_up_w"])
    dwv = i32(inputs["L0_ffn_down_w"])
    gs = f32(inputs["L0_ffn_gate_scale"]).reshape(II // 128, 128).T.copy()
    us = f32(inputs["L0_ffn_up_scale"]).reshape(II // 128, 128).T.copy()
    dsrow = f32(inputs["L0_ffn_down_scale"]).reshape(1, D)

    CC, SS, swp, idn, masks, onc, onr, aidx = _host_consts()
    in_maps = []
    for c in range(8):
        g, r = c // 4, c % 4
        osl = slice(OSL * r, OSL * (r + 1))
        own = slice(OWN * r, OWN * (r + 1))
        in_maps.append({
            "xg": x[g],
            "xow": np.ascontiguousarray(x[g][own]),
            "wq": np.ascontiguousarray(Wq[osl]),
            "wk": np.ascontiguousarray(Wk[osl]),
            "wv": np.ascontiguousarray(Wv[osl]),
            "wo": np.ascontiguousarray(Wo[:, osl]),
            "aq": Aq, "ak": Ak, "av": Av,
            "ao": np.ascontiguousarray(Ao[:, osl]),
            "bq": np.ascontiguousarray(Bq[:, osl, :]),
            "bk": np.ascontiguousarray(Bk[:, osl, :]),
            "bv": np.ascontiguousarray(Bv[:, osl, :]),
            "bo": Bo,
            "gw": gwv, "uw": uwv, "dw": dwv,
            "gsc": gs, "usc": us, "dsrow": dsrow,
            "starts": starts,
            "bmeta": np.array([[float(g)]], np.float32),
            "own0m": np.array([[OWN * r]], np.int32),
            "cc": CC, "ss": SS, "masks": masks,
            "swp": swp, "idn": idn, "onc": onc, "onr": onr, "aidx": aidx,
            "mner": np.full((1, 128), -128.0, ml_dtypes.bfloat16),
        })
    return in_maps


def kernel(**inputs):
    in_maps = make_in_maps(inputs)
    nc = _get_nc()
    res = run_bass_kernel_spmd(nc, in_maps, core_ids=list(range(8)))
    out = np.empty((BATCH, S, D), np.float32)
    for c in range(8):
        g, r = c // 4, c % 4
        out[g, OWN * r:OWN * (r + 1), :] = res.results[c]["out_own"]
    if DEBUG:
        kernel.last_results = res
    return out


# version 10
# speedup vs baseline: 1.0627x; 1.0424x over previous
import math
from contextlib import ExitStack

import numpy as np
import ml_dtypes

import concourse.bass as bass
import concourse.mybir as mybir
from concourse.bass_utils import run_bass_kernel_spmd

F32 = mybir.dt.float32
BF16 = mybir.dt.bfloat16
I32 = mybir.dt.int32
AF = mybir.ActivationFunctionType
ALU = mybir.AluOpType
AX = mybir.AxisListType

BATCH = 2
S = 4096
D = 2048
DH = 128
HPC = 4
OSL = HPC * DH
OWN = 1024
II = 8192
NL, R = 4, 16
LR = NL * R
NBLK = 8
BLK = 512
EPS = 1e-5
SM_SCALE = 1.0 / math.sqrt(DH)
SM_BIAS = -4.0
BAND = 256
NBAND = II // BAND
TB = BAND // 128

DEBUG = False
_CACHE = {}


class Ctr:
    __slots__ = ("sem", "v")

    def __init__(self, stack, nc, name):
        self.sem = stack.enter_context(nc.semaphore(name))
        self.v = 0

    def inc(self, ins, n=1):
        ins.then_inc(self.sem, n)
        self.v += n
        return self.v


def _w(eng, ctr, v):
    if v > 0:
        eng.wait_ge(ctr.sem, v)


def _host_consts():
    inv_freq = 10000.0 ** (-np.arange(0, DH, 2, dtype=np.float64) / DH)
    ang = np.arange(S, dtype=np.float64)[:, None] * inv_freq[None, :]
    cos = np.cos(ang).T
    sin = np.sin(ang).T
    CC = np.empty((DH, S), np.float32)
    SS = np.empty((DH, S), np.float32)
    CC[0::2] = cos
    CC[1::2] = cos
    SS[0::2] = -sin
    SS[1::2] = sin
    swp = np.zeros((DH, DH), ml_dtypes.bfloat16)
    for i in range(0, DH, 2):
        swp[i + 1, i] = 1
        swp[i, i + 1] = 1
    idn = np.eye(DH, dtype=ml_dtypes.bfloat16)
    masks = np.zeros((4, 128, BLK), ml_dtypes.bfloat16)
    for v in range(4):
        kk = np.arange(128)[:, None] + 128 * v
        qq = np.arange(BLK)[None, :]
        masks[v] = (kk <= qq).astype(ml_dtypes.bfloat16)
    ones_col = np.ones((128, 1), ml_dtypes.bfloat16)
    ones_row = np.ones((1, 128), ml_dtypes.bfloat16)
    aidx = (np.arange(LR) // R).astype(np.float32)[:, None]
    return CC, SS, swp, idn, masks, ones_col, ones_row, aidx


def _build():
    nc = bass.Bass("TRN2", target_bir_lowering=False)
    ap = lambda name, shape, dt, kind="ExternalInput": nc.dram_tensor(
        name, shape, dt, kind=kind).ap()

    xg = ap("xg", [S, D], F32)
    xow = ap("xow", [OWN, D], F32)
    wq_i = ap("wq", [OSL, D], F32)
    wk_i = ap("wk", [OSL, D], F32)
    wv_i = ap("wv", [OSL, D], F32)
    wo_i = ap("wo", [D, OSL], F32)
    aq_i = ap("aq", [LR, D], F32)
    ak_i = ap("ak", [LR, D], F32)
    av_i = ap("av", [LR, D], F32)
    ao_i = ap("ao", [LR, OSL], F32)
    bq_i = ap("bq", [NL, OSL, R], F32)
    bk_i = ap("bk", [NL, OSL, R], F32)
    bv_i = ap("bv", [NL, OSL, R], F32)
    bo_i = ap("bo", [NL, D, R], F32)
    gw_i = ap("gw", [II, D], I32)
    uw_i = ap("uw", [II, D], I32)
    dw_i = ap("dw", [D, II], I32)
    gsc_i = ap("gsc", [128, II // 128], F32)
    usc_i = ap("usc", [128, II // 128], F32)
    dsrow_i = ap("dsrow", [1, D], F32)
    starts_i = ap("starts", [1, NL], I32)
    bmeta_i = ap("bmeta", [1, 1], F32)
    own0_i = ap("own0m", [1, 1], I32)
    cc_i = ap("cc", [DH, S], F32)
    ss_i = ap("ss", [DH, S], F32)
    mask_i = ap("masks", [4, 128, BLK], BF16)
    swp_i = ap("swp", [DH, DH], BF16)
    idn_i = ap("idn", [DH, DH], BF16)
    onc_i = ap("onc", [128, 1], BF16)
    onr_i = ap("onr", [1, 128], BF16)
    mner_i = ap("mner", [1, 128], BF16)
    aidx_i = ap("aidx", [LR, 1], F32)

    out_own = ap("out_own", [OWN, D], F32, "ExternalOutput")
    dbg = {}
    if DEBUG:
        dbg["k"] = ap("dbg_k", [128, HPC * S], BF16, "ExternalOutput")
        dbg["q"] = ap("dbg_q", [128, HPC * S], BF16, "ExternalOutput")
        dbg["v"] = ap("dbg_v", [S, OSL], BF16, "ExternalOutput")
        dbg["attn"] = ap("dbg_attn", [128, HPC * S], BF16, "ExternalOutput")
        dbg["ccin"] = ap("dbg_ccin", [S, D], F32, "ExternalOutput")

    xn_all = ap("xn_all", [16, 128, S], BF16, "Internal")
    v_all = ap("v_all", [S, OSL], BF16, "Internal")
    wo_bf = ap("wo_bf", [D, OSL], BF16, "Internal")
    gw_bf = ap("gw_bf", [II, D], BF16, "Internal")
    uw_bf = ap("uw_bf", [II, D], BF16, "Internal")
    dw_bf = ap("dw_bf", [D, II], BF16, "Internal")
    ccin = ap("ccin", [S, D], F32, "Internal")
    ccout = ap("ccout", [OWN, D], F32, "Internal")

    with ExitStack() as st:
        sb = lambda name, shape, dt, **kw: st.enter_context(
            nc.sbuf_tensor(name, shape, dt, **kw))
        C = lambda name: Ctr(st, nc, name)

        mask_sb = sb("mask_sb", [128, 4 * BLK], BF16)
        swp_sb = sb("swp_sb", [DH, DH], BF16)
        idn_sb = sb("idn_sb", [DH, DH], BF16)
        onc_sb = sb("onc_sb", [128, 1], BF16)
        onr_sb = sb("onr_sb", [1, 128], BF16)
        aidx_sb = sb("aidx_sb", [LR, 1], F32)
        gsc_sb = sb("gsc_sb", [128, II // 128], F32)
        usc_sb = sb("usc_sb", [128, II // 128], F32)
        dsrow_bf = sb("dsrow_bf", [1, D], BF16)
        dsfull = sb("dsfull", [128, D], BF16)
        ind_sb = sb("ind_sb", [LR, 1], F32)
        starts_sb = sb("starts_sb", [1, NL], I32)
        startsf_sb = sb("startsf_sb", [1, NL], F32)
        bmeta_sb = sb("bmeta_sb", [1, 1], F32)
        own0_sb = sb("own0_sb", [1, 1], I32)
        idsc_sb = sb("idsc_sb", [1, 8], F32)
        idbf_sb = sb("idbf_sb", [1, 1], BF16)
        smbias_sb = sb("smbias_sb", [128, 1], F32)
        aot = sb("aot", [128, HPC * LR], BF16)
        bot = sb("bot", [LR, D], BF16)
        mner_sb = sb("mner_sb", [1, 128], BF16)

        c_lds = C("c_lds")
        c_ldg = C("c_ldg")
        s_ct = C("s_ct")
        s_x = [C("s_x0"), C("s_x1")]
        s_rp = C("s_rp")
        s_vs = [C("s_vs0"), C("s_vs1")]
        s_vg = C("s_vg")
        s_wo = C("s_wo")
        s_cc = [C("s_cc0"), C("s_cc1")]
        s_gw = C("s_gw")
        s_ga = C("s_ga")
        s_dw = C("s_dw")
        s_xo = [C("s_xo0"), C("s_xo1")]
        s_fa = [C("s_fa0"), C("s_fa1")]
        c_dqd = C("c_dqd")
        c_ffc = C("c_ffc")
        c_xcs = C("c_xcs")
        c_tr = C("c_tr")
        c_ev = C("c_ev")
        c_mmq = C("c_mmq")
        c_mlow = C("c_mlow")
        c_vlow = C("c_vlow")
        c_qraw = C("c_qraw")
        c_msw = C("c_msw")
        c_rope = C("c_rope")
        c_vev = C("c_vev")
        c_sq = C("c_sq")
        c_vst = C("c_vst")
        c_rstd = C("c_rstd")
        c_xn = C("c_xn")
        c_msc = C("c_msc")
        c_exp = C("c_exp")
        c_msk = C("c_msk")
        c_mav = C("c_mav")
        c_rcp = C("c_rcp")
        c_mbc = C("c_mbc")
        c_vbc = C("c_vbc")
        c_vat = C("c_vat")
        c_vlo = C("c_vlo")
        c_mo = C("c_mo")
        c_evo = C("c_evo")
        c_scc = C("c_scc")
        c_cc = C("c_cc")
        c_mg = C("c_mg")
        c_sil = C("c_sil")
        c_vh = C("c_vh")
        c_md = C("c_md")
        c_vacc = C("c_vacc")
        c_fin = C("c_fin")
        c_out = C("c_out")
        c_p0 = C("c_p0")
        vse = C("vse")
        ase = C("ase")

        blk0 = st.enter_context(nc.Block())

        @blk0.sync
        def _(_sy):
            SY = nc.sync
            TE = nc.tensor
            VE = nc.vector
            AC = nc.scalar
            GP = nc.gpsimd

            def vi(ins):
                return vse.inc(ins)

            def vw():
                VE.wait_ge(vse.sem, vse.v)

            def ai(ins):
                return ase.inc(ins)

            def aw():
                AC.wait_ge(ase.sem, ase.v)

            with ExitStack() as sA:
                sbA = lambda n, sh, dt, **kw: sA.enter_context(
                    nc.sbuf_tensor(n, sh, dt, **kw))
                ksb = sbA("ksb", [128, HPC * S], BF16)
                qsb = sbA("qsb", [128, HPC * S], BF16)

                with ExitStack() as sB:
                    sbB = lambda n, sh, dt, **kw: sB.enter_context(
                        nc.sbuf_tensor(n, sh, dt, **kw))
                    psB = lambda n, sh, dt: sB.enter_context(
                        nc.psum_tensor(n, sh, dt))
                    wqt = sbB("wqt", [128, 16 * BLK], BF16)
                    wkt = sbB("wkt", [128, 16 * BLK], BF16)
                    wvt = sbB("wvt", [128, 16 * BLK], BF16)
                    aqt = sbB("aqt", [128, 16 * LR], BF16)
                    akt = sbB("akt", [128, 16 * LR], BF16)
                    avt = sbB("avt", [128, 16 * LR], BF16)
                    bqt = sbB("bqt", [LR, OSL], BF16)
                    bkt = sbB("bkt", [LR, OSL], BF16)
                    bvt = sbB("bvt", [LR, OSL], BF16)
                    xt_sb = sbB("xt_sb", [128, 2 * D], F32)
                    stats = sbB("stats", [128, 64], F32)
                    xntm = sbB("xntm", [128, 2 * D], BF16)
                    xnfm = sbB("xnfm", [128, 16 * BLK], BF16)
                    ccb = sbB("ccb", [128, BLK], F32)
                    ssb = sbB("ssb", [128, BLK], F32)
                    lowq = sbB("lowq", [LR, BLK], BF16)
                    lowk = sbB("lowk", [LR, BLK], BF16)
                    lowv = sbB("lowv", [LR, BLK], BF16)
                    qraw = sbB("qraw", [128, 2 * BLK], BF16)
                    rtmp1 = sbB("rtmp1", [128, BLK], F32)
                    rtmp2 = sbB("rtmp2", [128, BLK], F32)
                    vstg = sbB("vstg", [128, 2 * OSL], BF16)
                    ps_tr = psB("ps_tr", [128, 2 * 1024], BF16)
                    ps_qk0 = psB("ps_qk0", [128, 512], F32)
                    ps_qk1 = psB("ps_qk1", [128, 512], F32)
                    ps_v0 = psB("ps_v0", [128, 512], F32)
                    ps_v1 = psB("ps_v1", [128, 512], F32)
                    ps_low = psB("ps_low", [128, 512], F32)
                    ps_swap = psB("ps_swap", [128, 512], F32)

                    for dst, src in [
                        (swp_sb[:, :], swp_i[:, :]), (idn_sb[:, :], idn_i[:, :]),
                        (onc_sb[:, :], onc_i[:, :]), (onr_sb[:, :], onr_i[:, :]),
                        (aidx_sb[:, :], aidx_i[:, :]), (gsc_sb[:, :], gsc_i[:, :]),
                        (usc_sb[:, :], usc_i[:, :]),
                        (starts_sb[:, :], starts_i[:, :]),
                        (bmeta_sb[:, :], bmeta_i[:, :]),
                        (own0_sb[:, :], own0_i[:, :]),
                        (mner_sb[:, :], mner_i[:, :]),
                    ]:
                        s_ct.inc(SY.dma_start(dst, src), 16)
                    for v in range(4):
                        s_ct.inc(SY.dma_start(
                            mask_sb[:, v * BLK:(v + 1) * BLK], mask_i[v, :, :]), 16)
                    ld_consts = s_ct.v

                    c_ldg.inc(GP.dma_start(wo_bf[:, :], wo_i[:, :]), 16)
                    ld_wcast = c_ldg.v

                    TE.wait_ge(s_ct.sem, ld_consts)
                    with ExitStack() as sP0:
                        astg = sP0.enter_context(
                            nc.sbuf_tensor("astg", [LR, D], BF16))
                        bstk = sP0.enter_context(
                            nc.sbuf_tensor("bstk", [128, 16 * LR], BF16))
                        for adr, dstT, wid in [(aq_i, aqt, D), (ak_i, akt, D),
                                               (av_i, avt, D), (ao_i, aot, OSL)]:
                            _w(GP, c_tr, c_tr.v)
                            c_ldg.inc(GP.dma_start(astg[:, 0:wid], adr[:, :]), 16)
                            TE.wait_ge(c_ldg.sem, c_ldg.v)
                            _w(TE, c_ev, c_ev.v)
                            nch = wid // 128
                            for dk in range(nch):
                                ins = TE.matmul(
                                    ps_tr[:, dk * LR:(dk + 1) * LR],
                                    astg[:, dk * 128:(dk + 1) * 128],
                                    idn_sb[0:LR, 0:LR],
                                    is_transpose=True, start=True, stop=True)
                            c_tr.inc(ins)
                            VE.wait_ge(c_tr.sem, c_tr.v)
                            c_ev.inc(VE.tensor_copy(dstT[:, 0:nch * LR],
                                                    ps_tr[:, 0:nch * LR]))
                        for bdr, dstT, nch in [(bq_i, bqt, OSL // 128),
                                               (bk_i, bkt, OSL // 128),
                                               (bv_i, bvt, OSL // 128),
                                               (bo_i, bot, D // 128)]:
                            _w(GP, c_tr, c_tr.v)
                            for ch in range(nch):
                                src3 = bdr[:, ch * 128:(ch + 1) * 128, :].rearrange(
                                    "a p r -> p a r")
                                c_ldg.inc(GP.dma_start(
                                    bstk[:, ch * LR:(ch + 1) * LR].rearrange(
                                        "p (a r) -> p a r", a=NL), src3), 16)
                            TE.wait_ge(c_ldg.sem, c_ldg.v)
                            for half in range((nch + 7) // 8):
                                _w(TE, c_ev, c_ev.v)
                                lo = half * 8
                                hi = min(lo + 8, nch)
                                for ch in range(lo, hi):
                                    ins = TE.matmul(
                                        ps_tr[0:LR,
                                              (ch - lo) * 128:(ch - lo + 1) * 128],
                                        bstk[:, ch * LR:(ch + 1) * LR],
                                        idn_sb[:, :],
                                        is_transpose=True, start=True, stop=True)
                                c_tr.inc(ins)
                                VE.wait_ge(c_tr.sem, c_tr.v)
                                c_ev.inc(VE.tensor_copy(
                                    dstT[0:LR, lo * 128:hi * 128],
                                    ps_tr[0:LR, 0:(hi - lo) * 128]))

                    VE.wait_ge(s_ct.sem, ld_consts)
                    VE.memset(smbias_sb[:, :], SM_BIAS)
                    vi(VE.tensor_copy(startsf_sb[:, :], starts_sb[:, :]))
                    vw()
                    vi(VE.tensor_single_scalar(idsc_sb[0:1, 0:NL],
                                               startsf_sb[0:1, :],
                                               bmeta_sb[0:1, 0:1], ALU.is_le))
                    vw()
                    vi(VE.reduce_sum(idsc_sb[0:1, 4:5], idsc_sb[0:1, 0:NL], AX.X))
                    vw()
                    vi(VE.tensor_scalar(idsc_sb[0:1, 5:6], idsc_sb[0:1, 4:5],
                                        -1.0, 0.0, ALU.add, ALU.max))
                    vw()
                    vi(VE.tensor_single_scalar(idsc_sb[0:1, 6:7], idsc_sb[0:1, 5:6],
                                               3.0, ALU.min))
                    vw()
                    c_p0.inc(VE.tensor_copy(idbf_sb[0:1, 0:1], idsc_sb[0:1, 6:7]))
                    TE.wait_ge(c_p0.sem, c_p0.v)
                    c_p0.inc(TE.matmul(ps_qk0[0:LR, 0:1], onr_sb[0:1, 0:LR],
                                       idbf_sb[0:1, 0:1], start=True, stop=True))
                    VE.wait_ge(c_p0.sem, c_p0.v)
                    vi(VE.tensor_tensor(ind_sb[:, :], ps_qk0[0:LR, 0:1],
                                        aidx_sb[:, :], ALU.is_equal))
                    vw()
                    for dstT, wid in [(bqt, OSL), (bkt, OSL), (bvt, OSL), (bot, D)]:
                        ins = VE.tensor_scalar(dstT[:, 0:wid], dstT[:, 0:wid],
                                               ind_sb[:, 0:1], None, ALU.mult)
                    c_p0.inc(ins)
                    p0_done = c_p0.v

                    wstg = xnfm
                    wtr_v = {}
                    n_wc = 0
                    for wsrc, dstT in [(wq_i, wqt), (wk_i, wkt), (wv_i, wvt)]:
                        dstv = dstT[:, :].rearrange("p (dk q) -> p dk q", dk=16)
                        for fc in range(4):
                            sl = n_wc % 2
                            if n_wc >= 2:
                                GP.wait_ge(c_tr.sem, wtr_v[n_wc - 2])
                            c_ldg.inc(GP.dma_start(
                                wstg[:, sl * D:(sl + 1) * D],
                                wsrc[fc * 128:(fc + 1) * 128, :]), 16)
                            ldv = c_ldg.v
                            TE.wait_ge(c_ldg.sem, ldv)
                            for grp in range(2):
                                _w(TE, c_ev, c_ev.v if n_wc == 0 and grp == 0
                                   else c_ev.v - 1)
                                psl = (grp % 2) * 1024
                                for j in range(8):
                                    dk = grp * 8 + j
                                    ins = TE.matmul(
                                        ps_tr[:, psl + j * 128:psl + (j + 1) * 128],
                                        wstg[:, sl * D + dk * 128:
                                             sl * D + (dk + 1) * 128],
                                        idn_sb[:, :], is_transpose=True,
                                        start=True, stop=True)
                                c_tr.inc(ins)
                                VE.wait_ge(c_tr.sem, c_tr.v)
                                c_ev.inc(VE.tensor_copy(
                                    dstv[:, grp * 8:(grp + 1) * 8,
                                         fc * 128:(fc + 1) * 128],
                                    ps_tr[:, psl:psl + 1024].rearrange(
                                        "p (c q) -> p c q", c=8)))
                            wtr_v[n_wc] = c_tr.v
                            n_wc += 1

                    GP.wait_ge(s_ct.sem, ld_consts)
                    c_ldg.inc(GP.dma_start(dsrow_bf[0:1, :], dsrow_i[0:1, :]), 16)
                    c_dsr = c_ldg.v
                    c_ffc.inc(GP.dma_start(gw_bf[:, :], gw_i[:, :]), 16)
                    c_ffc.inc(GP.dma_start(uw_bf[:, :], uw_i[:, :]), 16)
                    c_ffc.inc(GP.dma_start(dw_bf[:, :], dw_i[:, :]), 16)
                    for j in range(4):
                        if j == 0:
                            TE.wait_ge(c_ldg.sem, c_dsr)
                        TE.wait_ge(vse.sem, vse.v)
                        c_p0.inc(TE.matmul(
                            ps_qk0[:, 0:BLK], onr_sb[0:1, :],
                            dsrow_bf[0:1, j * BLK:(j + 1) * BLK],
                            start=True, stop=True))
                        VE.wait_ge(c_p0.sem, c_p0.v)
                        vi(VE.tensor_copy(dsfull[:, j * BLK:(j + 1) * BLK],
                                          ps_qk0[:, 0:BLK]))
                    vw()
                    c_p0.inc(VE.tensor_copy(dsrow_bf[0:1, 0:1],
                                            dsrow_bf[0:1, 0:1]))
                    p0_done = c_p0.v

                    p0_tr = c_tr.v
                    n_qk = 0
                    n_v = 0
                    n_trf = 0
                    v_st = []
                    mmq_blk = {}
                    store_blk = {}
                    ld_rope = 0
                    for b in range(NBLK):
                        xld = []
                        for tt in range(4):
                            gi = 4 * b + tt
                            _w(SY, c_xn, gi - 1)
                            s_x[gi % 2].inc(SY.dma_start(
                                xt_sb[:, (gi % 2) * D:(gi % 2 + 1) * D],
                                xg[b * BLK + tt * 128:b * BLK + (tt + 1) * 128, :]),
                                16)
                            xld.append((s_x[gi % 2], s_x[gi % 2].v))
                        _w(SY, c_rope, 8 * b)
                        s_rp.inc(SY.dma_start(ccb[:, :],
                                              cc_i[:, b * BLK:(b + 1) * BLK]), 16)
                        s_rp.inc(SY.dma_start(ssb[:, :],
                                              ss_i[:, b * BLK:(b + 1) * BLK]), 16)
                        ld_rope = s_rp.v
                        for tt in range(4):
                            gi = 4 * b + tt
                            tp = gi % 2
                            sp = (gi % 2) * 32 + 8
                            xt = xt_sb[:, tp * D:(tp + 1) * D]
                            AC.wait_ge(xld[tt][0].sem, xld[tt][1])
                            _w(AC, c_tr, p0_tr + 2 * gi - 2 if gi >= 2 else 0)
                            c_sq.inc(AC.activation(
                                xntm[:, tp * D:(tp + 1) * D], xt, AF.Square,
                                accum_out=stats[:, sp:sp + 1]))
                            VE.wait_ge(xld[tt][0].sem, xld[tt][1])
                            vi(VE.reduce_sum(stats[:, sp + 1:sp + 2], xt, AX.X))
                            VE.wait_ge(c_sq.sem, c_sq.v)
                            vw()
                            vi(VE.tensor_scalar(stats[:, sp + 2:sp + 3],
                                                stats[:, sp + 1:sp + 2],
                                                1.0 / D, None, ALU.mult))
                            vw()
                            vi(VE.tensor_scalar(stats[:, sp + 3:sp + 4],
                                                stats[:, sp:sp + 1],
                                                1.0 / D, EPS, ALU.mult, ALU.add))
                            vw()
                            vi(VE.tensor_tensor(stats[:, sp + 4:sp + 5],
                                                stats[:, sp + 2:sp + 3],
                                                stats[:, sp + 2:sp + 3], ALU.mult))
                            vw()
                            c_vst.inc(VE.tensor_sub(stats[:, sp + 3:sp + 4],
                                                    stats[:, sp + 3:sp + 4],
                                                    stats[:, sp + 4:sp + 5]))
                            AC.wait_ge(c_vst.sem, c_vst.v)
                            ai(AC.activation(stats[:, sp + 5:sp + 6],
                                             stats[:, sp + 3:sp + 4], AF.Ln))
                            aw()
                            c_rstd.inc(AC.activation(stats[:, sp + 6:sp + 7],
                                                     stats[:, sp + 5:sp + 6],
                                                     AF.Exp, scale=-0.5))
                            VE.wait_ge(c_rstd.sem, c_rstd.v)
                            c_xn.inc(VE.tensor_scalar(
                                xntm[:, tp * D:(tp + 1) * D], xt,
                                stats[:, sp + 2:sp + 3], stats[:, sp + 6:sp + 7],
                                ALU.subtract, ALU.mult))
                            TE.wait_ge(c_xn.sem, c_xn.v)
                            xnv = xnfm[:, :].rearrange("p (dk q) -> p dk q", dk=16)
                            for grp in range(2):
                                _w(TE, c_ev, c_ev.v if n_trf == 0 else c_ev.v - 1)
                                sl = (n_trf % 2) * 1024
                                for j in range(8):
                                    dk = grp * 8 + j
                                    ins = TE.matmul(
                                        ps_tr[:, sl + j * 128:sl + (j + 1) * 128],
                                        xntm[:, tp * D + dk * 128:
                                             tp * D + (dk + 1) * 128],
                                        idn_sb[:, :], is_transpose=True,
                                        start=True, stop=True)
                                c_tr.inc(ins)
                                n_trf += 1
                                VE.wait_ge(c_tr.sem, c_tr.v)
                                if b >= 1 and tt == 0 and grp == 0:
                                    _w(VE, c_mmq, mmq_blk[b - 1])
                                c_ev.inc(VE.tensor_copy(
                                    xnv[:, grp * 8:(grp + 1) * 8,
                                        tt * 128:(tt + 1) * 128],
                                    ps_tr[:, sl:sl + 1024].rearrange(
                                        "p (c q) -> p c q", c=8)))
                        ev_blk = c_ev.v
                        xb = lambda dk: xnfm[:, dk * BLK:(dk + 1) * BLK]

                        TE.wait_ge(c_ev.sem, ev_blk)
                        if b == 0:
                            TE.wait_ge(c_p0.sem, p0_done)
                        for at, lowsb in [(aqt, lowq), (akt, lowk), (avt, lowv)]:
                            _w(TE, c_vlow, c_vlow.v)
                            for dk in range(16):
                                ins = TE.matmul(ps_low[0:LR, 0:BLK],
                                                at[:, dk * LR:(dk + 1) * LR],
                                                xb(dk), start=(dk == 0),
                                                stop=(dk == 15))
                            c_mlow.inc(ins)
                            VE.wait_ge(c_mlow.sem, c_mlow.v)
                            c_vlow.inc(VE.tensor_copy(lowsb[:, :],
                                                      ps_low[0:LR, 0:BLK]))

                        for wt, low, bt, dstsb in [(wqt, lowq, bqt, qsb),
                                                   (wkt, lowk, bkt, ksb)]:
                            for hh in range(HPC):
                                u = n_qk
                                pst = ps_qk0 if (u % 2) == 0 else ps_qk1
                                _w(TE, c_qraw, u - 1)
                                TE.wait_ge(c_vlow.sem, c_vlow.v)
                                for dk in range(16):
                                    TE.matmul(pst[:, 0:BLK],
                                              wt[:, dk * BLK + hh * 128:
                                                 dk * BLK + (hh + 1) * 128],
                                              xb(dk), start=(dk == 0), stop=False)
                                c_mmq.inc(TE.matmul(
                                    pst[:, 0:BLK], bt[:, hh * 128:(hh + 1) * 128],
                                    low[:, :], start=False, stop=True))
                                VE.wait_ge(c_mmq.sem, c_mmq.v)
                                _w(VE, c_msw, u - 1)
                                c_qraw.inc(VE.tensor_copy(
                                    qraw[:, (u % 2) * BLK:(u % 2 + 1) * BLK],
                                    pst[:, 0:BLK]))
                                TE.wait_ge(c_qraw.sem, c_qraw.v)
                                _w(TE, c_rope, u)
                                c_msw.inc(TE.matmul(
                                    ps_swap[:, 0:BLK], swp_sb[:, :],
                                    qraw[:, (u % 2) * BLK:(u % 2 + 1) * BLK],
                                    start=True, stop=True))
                                VE.wait_ge(s_rp.sem, ld_rope)
                                VE.wait_ge(c_qraw.sem, c_qraw.v)
                                vi(VE.tensor_tensor(
                                    rtmp1[:, :],
                                    qraw[:, (u % 2) * BLK:(u % 2 + 1) * BLK],
                                    ccb[:, :], ALU.mult))
                                VE.wait_ge(c_msw.sem, c_msw.v)
                                vi(VE.tensor_tensor(rtmp2[:, :], ps_swap[:, 0:BLK],
                                                    ssb[:, :], ALU.mult))
                                vw()
                                c_rope.inc(VE.tensor_add(
                                    dstsb[:, hh * S + b * BLK:
                                          hh * S + (b + 1) * BLK],
                                    rtmp1[:, :], rtmp2[:, :]))
                                n_qk += 1

                        for tt in range(4):
                            pst = ps_v0 if (n_v % 2) == 0 else ps_v1
                            _w(TE, c_vev, n_v - 1)
                            for dk in range(16):
                                TE.matmul(pst[:, 0:OSL],
                                          xnfm[:, dk * BLK + tt * 128:
                                               dk * BLK + (tt + 1) * 128],
                                          wvt[:, dk * BLK:(dk + 1) * BLK],
                                          start=(dk == 0), stop=False)
                            c_mmq.inc(TE.matmul(pst[:, 0:OSL],
                                                lowv[:, tt * 128:(tt + 1) * 128],
                                                bvt[:, :], start=False, stop=True))
                            VE.wait_ge(c_mmq.sem, c_mmq.v)
                            if n_v >= 2:
                                _w(VE, s_vs[n_v % 2], s_vs[n_v % 2].v)
                            c_vev.inc(VE.tensor_copy(
                                vstg[:, (n_v % 2) * OSL:(n_v % 2 + 1) * OSL],
                                pst[:, 0:OSL]))
                            SY.wait_ge(c_vev.sem, c_vev.v)
                            kc = b * 4 + tt
                            s_vs[n_v % 2].inc(SY.dma_start(
                                v_all[kc * 128:(kc + 1) * 128, :],
                                vstg[:, (n_v % 2) * OSL:(n_v % 2 + 1) * OSL]), 16)
                            n_v += 1
                        mmq_blk[b] = c_mmq.v
                    p1_rope = c_rope.v

                with ExitStack() as sC:
                    sbC = lambda n, sh, dt, **kw: sC.enter_context(
                        nc.sbuf_tensor(n, sh, dt, **kw))
                    psC = lambda n, sh, dt: sC.enter_context(
                        nc.psum_tensor(n, sh, dt))
                    vsb = sbC("vsb", [128, HPC * S], BF16)
                    asb = sbC("asb", [128, HPC * S], BF16)
                    psb = sbC("psb", [128, 2 * BLK], BF16)
                    sums_f = sbC("sums_f", [1, BLK], F32)
                    rinvf = sbC("rinvf", [1, BLK], F32)
                    rinvb = sbC("rinvb", [1, BLK], BF16)
                    binv = sbC("binv", [128, BLK], BF16)
                    sP2 = ExitStack()
                    ps_sc = sP2.enter_context(
                        nc.psum_tensor("ps_sc", [128, 2 * 512], F32))
                    ps_attn = sP2.enter_context(
                        nc.psum_tensor("ps_attn", [128, 512], F32))
                    ps_sum = sP2.enter_context(
                        nc.psum_tensor("ps_sum", [128, 512], F32))
                    ps_bc = sP2.enter_context(
                        nc.psum_tensor("ps_bc", [128, 512], F32))

                    SY.wait_ge(s_vs[0].sem, s_vs[0].v)
                    SY.wait_ge(s_vs[1].sem, s_vs[1].v)
                    vsrc = v_all.rearrange("(kc p) d -> p kc d", p=128)
                    for h in range(HPC):
                        s_vg.inc(SY.dma_start(
                            vsb[:, h * S:(h + 1) * S].rearrange(
                                "p (kc d) -> p kc d", kc=32),
                            vsrc[:, :, h * 128:(h + 1) * 128]), 16)
                    ld_v = s_vg.v

                    TE.wait_ge(c_rope.sem, p1_rope)
                    TE.wait_ge(s_vg.sem, ld_v)
                    VE.wait_ge(s_ct.sem, ld_consts)
                    n_kc = 0
                    n_qb = 0
                    for u in range(HPC):
                        for qb in range(NBLK):
                            nk = 4 * qb + 4
                            pend = []

                            def consume(TE=TE, u=u, qb=qb, nk=nk):
                                par, ctr, val, kcc = pend.pop(0)
                                _w(TE, ctr, val)
                                if kcc == 0:
                                    _w(TE, c_vat, n_qb)
                                    _w(TE, c_rcp, n_qb)
                                TE.matmul(
                                    ps_attn[:, 0:BLK],
                                    vsb[:, u * S + kcc * 128:u * S + (kcc + 1) * 128],
                                    psb[:, par * BLK:(par + 1) * BLK],
                                    start=(kcc == 0), stop=(kcc == nk - 1))
                                c_mav.inc(TE.matmul(
                                    ps_sum[0:1, 0:BLK], onc_sb[:, 0:1],
                                    psb[:, par * BLK:(par + 1) * BLK],
                                    start=(kcc == 0), stop=(kcc == nk - 1)))

                            for kc in range(nk):
                                par = n_kc % 2
                                _w(TE, c_exp, n_kc - 1)
                                c_msc.inc(TE.matmul(
                                    ps_sc[:, par * 512:par * 512 + BLK],
                                    ksb[:, u * S + kc * 128:u * S + (kc + 1) * 128],
                                    qsb[:, u * S + qb * BLK:u * S + (qb + 1) * BLK],
                                    start=True, stop=True))
                                AC.wait_ge(c_msc.sem, c_msc.v)
                                _w(AC, c_mav, n_kc - 1)
                                c_exp.inc(AC.activation(
                                    psb[:, par * BLK:(par + 1) * BLK],
                                    ps_sc[:, par * 512:par * 512 + BLK],
                                    AF.Exp, bias=smbias_sb[:, 0:1],
                                    scale=SM_SCALE))
                                diag = kc - 4 * qb
                                if diag >= 0:
                                    VE.wait_ge(c_exp.sem, c_exp.v)
                                    c_msk.inc(VE.tensor_tensor(
                                        psb[:, par * BLK:(par + 1) * BLK],
                                        psb[:, par * BLK:(par + 1) * BLK],
                                        mask_sb[:, diag * BLK:(diag + 1) * BLK],
                                        ALU.mult))
                                    pend.append((par, c_msk, c_msk.v, kc))
                                else:
                                    pend.append((par, c_exp, c_exp.v, kc))
                                if len(pend) > 1:
                                    consume()
                                n_kc += 1
                            consume()
                            VE.wait_ge(c_mav.sem, c_mav.v)
                            vi(VE.tensor_copy(sums_f[0:1, :], ps_sum[0:1, 0:BLK]))
                            vw()
                            vi(VE.reciprocal(rinvf[0:1, :], sums_f[0:1, :]))
                            vw()
                            c_rcp.inc(VE.tensor_copy(rinvb[0:1, :], rinvf[0:1, :]))
                            TE.wait_ge(c_rcp.sem, c_rcp.v)
                            _w(TE, c_vbc, n_qb)
                            c_mbc.inc(TE.matmul(ps_bc[:, 0:BLK], onr_sb[0:1, :],
                                                rinvb[0:1, :],
                                                start=True, stop=True))
                            VE.wait_ge(c_mbc.sem, c_mbc.v)
                            c_vbc.inc(VE.tensor_copy(binv[:, :], ps_bc[:, 0:BLK]))
                            VE.wait_ge(c_vbc.sem, c_vbc.v)
                            c_vat.inc(VE.tensor_tensor(
                                asb[:, u * S + qb * BLK:u * S + (qb + 1) * BLK],
                                ps_attn[:, 0:BLK], binv[:, :], ALU.mult))
                            n_qb += 1
                    p2_done = c_vat.v
                    sP2.close()

                    if DEBUG:
                        SY.wait_ge(c_vat.sem, p2_done)
                        c_lds.inc(SY.dma_start(dbg["k"][:, :], ksb[:, :]), 16)
                        c_lds.inc(SY.dma_start(dbg["q"][:, :], qsb[:, :]), 16)
                        c_lds.inc(SY.dma_start(dbg["v"][:, :], v_all[:, :]), 16)
                        c_lds.inc(SY.dma_start(dbg["attn"][:, :], asb[:, :]), 16)

                    wot = sbC("wot", [128, HPC * D], BF16, is_transpose_dest=True)
                    lowo = sbC("lowo", [LR, S], BF16)
                    osb = sbC("osb", [128, 2 * D], F32)
                    for c in range(HPC):
                        s_wo.inc(SY.dma_start(
                            wot[:, c * D:(c + 1) * D],
                            wo_bf[:, c * 128:(c + 1) * 128], transpose=True), 16)
                    ld_wot = s_wo.v

                    with ExitStack() as sL:
                        ps_lo3 = sL.enter_context(
                            nc.psum_tensor("ps_lo3", [128, 512], F32))
                        TE.wait_ge(c_vat.sem, p2_done)
                        for blk in range(NBLK):
                            _w(TE, c_vlo, blk)
                            for c in range(HPC):
                                ins = TE.matmul(
                                    ps_lo3[0:LR, 0:BLK],
                                    aot[:, c * LR:(c + 1) * LR],
                                    asb[:, c * S + blk * BLK:
                                        c * S + (blk + 1) * BLK],
                                    start=(c == 0), stop=(c == HPC - 1))
                            c_mlow.inc(ins)
                            VE.wait_ge(c_mlow.sem, c_mlow.v)
                            c_vlo.inc(VE.tensor_copy(
                                lowo[:, blk * BLK:(blk + 1) * BLK],
                                ps_lo3[0:LR, 0:BLK]))
                    with ExitStack() as sO:
                        ps_o = sO.enter_context(
                            nc.psum_tensor("ps_o", [128, 2 * D], F32))
                        TE.wait_ge(s_wo.sem, ld_wot)
                        TE.wait_ge(c_vlo.sem, c_vlo.v)
                        for tt in range(S // 128):
                            par = tt % 2
                            op = par * D
                            _w(TE, c_evo, 2 * (tt - 1))
                            for j in range(4):
                                for c in range(HPC):
                                    TE.matmul(
                                        ps_o[:, op + j * 512:op + (j + 1) * 512],
                                        asb[:, c * S + tt * 128:
                                            c * S + (tt + 1) * 128],
                                        wot[:, c * D + j * 512:c * D + (j + 1) * 512],
                                        start=(c == 0), stop=False)
                                ins = TE.matmul(
                                    ps_o[:, op + j * 512:op + (j + 1) * 512],
                                    lowo[:, tt * 128:(tt + 1) * 128],
                                    bot[:, j * 512:(j + 1) * 512],
                                    start=False, stop=True)
                            c_mo.inc(ins)
                            VE.wait_ge(c_mo.sem, c_mo.v)
                            _w(VE, s_cc[par], s_cc[par].v)
                            c_evo.inc(VE.tensor_copy(osb[:, op:op + 1024],
                                                     ps_o[:, op:op + 1024]))
                            AC.wait_ge(c_mo.sem, c_mo.v)
                            _w(AC, s_cc[par], s_cc[par].v)
                            c_evo.inc(AC.copy(osb[:, op + 1024:op + D],
                                              ps_o[:, op + 1024:op + D]))
                            SY.wait_ge(c_evo.sem, c_evo.v)
                            s_cc[par].inc(SY.dma_start(
                                ccin[tt * 128:(tt + 1) * 128, :],
                                osb[:, op:op + D]), 16)

                    if DEBUG:
                        SY.wait_ge(s_cc[0].sem, s_cc[0].v)
                        SY.wait_ge(s_cc[1].sem, s_cc[1].v)
                        c_lds.inc(SY.dma_start(dbg["ccin"][:, :], ccin[:, :]), 16)

                    GP.wait_ge(s_cc[0].sem, s_cc[0].v)
                    GP.wait_ge(s_cc[1].sem, s_cc[1].v)
                    c_cc.inc(GP.collective_compute(
                        "ReduceScatter", ALU.add,
                        replica_groups=[[0, 1, 2, 3], [4, 5, 6, 7]],
                        ins=[ccin[:, :].opt()], outs=[ccout[:, :].opt()]), 1)

            with ExitStack() as sD:
                sbD = lambda n, sh, dt, **kw: sD.enter_context(
                    nc.sbuf_tensor(n, sh, dt, **kw))
                psD = lambda n, sh, dt: sD.enter_context(nc.psum_tensor(n, sh, dt))
                xnown = sbD("xnown", [128, 16 * OWN], BF16)
                wgt_b = sbD("wgt_b", [128, 2 * 16 * BAND], BF16,
                            is_transpose_dest=True)
                wut_b = sbD("wut_b", [128, 2 * 16 * BAND], BF16,
                            is_transpose_dest=True)
                wdt_b = sbD("wdt_b", [128, TB * D], BF16, is_transpose_dest=True)
                hact = sbD("hact", [128, 2 * TB * OWN], BF16)
                sg_sb = sbD("sg_sb", [128, 2 * BLK], BF16)
                oacc = sbD("oacc", [128, 8 * D], F32)
                xcs_hi = sbD("xcs_hi", [1, 2 * BLK], BF16)
                xcs_lo = sbD("xcs_lo", [1, 2 * BLK], BF16)
                xcs_tmp = sbD("xcs_tmp", [1, BLK], F32)

                with ExitStack() as sP4a:
                    xts = sP4a.enter_context(
                        nc.sbuf_tensor("xts", [128, 2 * D], F32))
                    xnt4 = sP4a.enter_context(
                        nc.sbuf_tensor("xnt4", [128, 2 * D], BF16))
                    st4 = sP4a.enter_context(
                        nc.sbuf_tensor("st4", [128, 64], F32))
                    ps_t4 = sP4a.enter_context(
                        nc.psum_tensor("ps_t4", [128, 2 * 1024], BF16))
                    TE.wait_ge(c_evo.sem, c_evo.v)
                    p4_tr0 = c_tr.v
                    n_t4 = 0
                    xo_ld = []
                    for ti in range(8):
                        _w(SY, c_xn, c_xn.v + ti - 1)
                        s_xo[ti % 2].inc(SY.dma_start(
                            xts[:, (ti % 2) * D:(ti % 2 + 1) * D],
                            xow[ti * 128:(ti + 1) * 128, :]), 16)
                        xo_ld.append((s_xo[ti % 2], s_xo[ti % 2].v))
                    xnv4 = xnown[:, :].rearrange("p (dk q) -> p dk q", dk=16)
                    for ti in range(8):
                        tp = ti % 2
                        sp = tp * 32 + 8
                        xt = xts[:, tp * D:(tp + 1) * D]
                        AC.wait_ge(xo_ld[ti][0].sem, xo_ld[ti][1])
                        _w(AC, c_tr, p4_tr0 + 2 * ti - 2 if ti >= 2 else 0)
                        c_sq.inc(AC.activation(
                            xnt4[:, tp * D:(tp + 1) * D], xt, AF.Square,
                            accum_out=st4[:, sp:sp + 1]))
                        VE.wait_ge(xo_ld[ti][0].sem, xo_ld[ti][1])
                        vi(VE.reduce_sum(st4[:, sp + 1:sp + 2], xt, AX.X))
                        VE.wait_ge(c_sq.sem, c_sq.v)
                        vw()
                        vi(VE.tensor_scalar(st4[:, sp + 2:sp + 3],
                                            st4[:, sp + 1:sp + 2],
                                            1.0 / D, None, ALU.mult))
                        vw()
                        vi(VE.tensor_scalar(st4[:, sp + 3:sp + 4],
                                            st4[:, sp:sp + 1],
                                            1.0 / D, EPS, ALU.mult, ALU.add))
                        vw()
                        vi(VE.tensor_tensor(st4[:, sp + 4:sp + 5],
                                            st4[:, sp + 2:sp + 3],
                                            st4[:, sp + 2:sp + 3], ALU.mult))
                        vw()
                        c_vst.inc(VE.tensor_sub(st4[:, sp + 3:sp + 4],
                                                st4[:, sp + 3:sp + 4],
                                                st4[:, sp + 4:sp + 5]))
                        AC.wait_ge(c_vst.sem, c_vst.v)
                        ai(AC.activation(st4[:, sp + 5:sp + 6],
                                         st4[:, sp + 3:sp + 4], AF.Ln))
                        aw()
                        c_rstd.inc(AC.activation(st4[:, sp + 6:sp + 7],
                                                 st4[:, sp + 5:sp + 6],
                                                 AF.Exp, scale=-0.5))
                        VE.wait_ge(c_rstd.sem, c_rstd.v)
                        c_xn.inc(VE.tensor_scalar(
                            xnt4[:, tp * D:(tp + 1) * D], xt,
                            st4[:, sp + 2:sp + 3], st4[:, sp + 6:sp + 7],
                            ALU.subtract, ALU.mult))
                        TE.wait_ge(c_xn.sem, c_xn.v)
                        for grp in range(2):
                            _w(TE, c_ev, c_ev.v if n_t4 == 0 else c_ev.v - 1)
                            sl = (n_t4 % 2) * 1024
                            for j in range(8):
                                dk = grp * 8 + j
                                ins = TE.matmul(
                                    ps_t4[:, sl + j * 128:sl + (j + 1) * 128],
                                    xnt4[:, tp * D + dk * 128:
                                         tp * D + (dk + 1) * 128],
                                    idn_sb[:, :], is_transpose=True,
                                    start=True, stop=True)
                            c_tr.inc(ins)
                            n_t4 += 1
                            VE.wait_ge(c_tr.sem, c_tr.v)
                            c_ev.inc(VE.tensor_copy(
                                xnv4[:, grp * 8:(grp + 1) * 8,
                                     ti * 128:(ti + 1) * 128],
                                ps_t4[:, sl:sl + 1024].rearrange(
                                    "p (c q) -> p c q", c=8)))
                    TE.wait_ge(c_ev.sem, c_ev.v)
                    ps_cs = sP4a.enter_context(
                        nc.psum_tensor("ps_cs", [128, 2 * 512], F32))
                    for tb in range(2):
                        cs = ps_cs[0:1, tb * 512:tb * 512 + BLK]
                        for dk in range(16):
                            ins = TE.matmul(cs, onc_sb[:, 0:1],
                                            xnown[:, dk * OWN + tb * BLK:
                                                  dk * OWN + (tb + 1) * BLK],
                                            start=(dk == 0), stop=(dk == 15))
                        c_tr.inc(ins)
                        VE.wait_ge(c_tr.sem, c_tr.v)
                        vi(VE.tensor_copy(xcs_hi[0:1, tb * BLK:(tb + 1) * BLK],
                                          cs))
                        vw()
                        vi(VE.tensor_sub(xcs_tmp[0:1, :], cs,
                                         xcs_hi[0:1, tb * BLK:(tb + 1) * BLK]))
                        vw()
                        c_xcs.inc(VE.tensor_copy(
                            xcs_lo[0:1, tb * BLK:(tb + 1) * BLK],
                            xcs_tmp[0:1, :]))
                ps_g = psD("ps_g", [128, 2 * 512], F32)
                ps_u = psD("ps_u", [128, 2 * 512], F32)
                ps_d = psD("ps_d", [128, 2 * 1024], F32)

                n_gu = 0
                n_d = 0
                mg_band = {}
                md_band = {}
                acc_v = {}
                SY.wait_ge(c_ffc.sem, c_ffc.v)
                TE.wait_ge(c_xcs.sem, c_xcs.v)
                for band in range(NBAND):
                    fp = band % 2
                    if band >= 2:
                        _w(SY, c_mg, mg_band[band - 2])
                        _w(AC, c_mg, mg_band[band - 2])
                    wsrc = band * BAND
                    for dk in range(8):
                        s_gw.inc(SY.dma_start(
                            wgt_b[:, fp * 16 * BAND + dk * BAND:
                                  fp * 16 * BAND + (dk + 1) * BAND],
                            gw_bf[wsrc:wsrc + BAND, dk * 128:(dk + 1) * 128],
                            transpose=True), 16)
                        s_gw.inc(SY.dma_start(
                            wut_b[:, fp * 16 * BAND + dk * BAND:
                                  fp * 16 * BAND + (dk + 1) * BAND],
                            uw_bf[wsrc:wsrc + BAND, dk * 128:(dk + 1) * 128],
                            transpose=True), 16)
                    for dk in range(8, 16):
                        s_ga.inc(SY.dma_start(
                            wgt_b[:, fp * 16 * BAND + dk * BAND:
                                  fp * 16 * BAND + (dk + 1) * BAND],
                            gw_bf[wsrc:wsrc + BAND, dk * 128:(dk + 1) * 128],
                            transpose=True), 16)
                        s_ga.inc(SY.dma_start(
                            wut_b[:, fp * 16 * BAND + dk * BAND:
                                  fp * 16 * BAND + (dk + 1) * BAND],
                            uw_bf[wsrc:wsrc + BAND, dk * 128:(dk + 1) * 128],
                            transpose=True), 16)
                    gld = s_gw.v
                    gld_a = s_ga.v
                    if band >= 1:
                        _w(SY, c_md, md_band[band - 1])
                    for ic in range(TB):
                        s_dw.inc(SY.dma_start(
                            wdt_b[:, ic * D:(ic + 1) * D],
                            dw_bf[:, wsrc + ic * 128:wsrc + (ic + 1) * 128],
                            transpose=True), 16)
                    dld = s_dw.v

                    for ic in range(TB):
                        for tb in range(2):
                            par = n_gu % 2
                            _w(TE, c_sil, n_gu - 1)
                            _w(TE, c_vh, n_gu - 1)
                            TE.wait_ge(s_gw.sem, gld)
                            TE.wait_ge(s_ga.sem, gld_a)
                            ich = band * TB + ic
                            for dk in range(16):
                                TE.matmul(
                                    ps_g[:, par * 512:par * 512 + BLK],
                                    wgt_b[:, fp * 16 * BAND + dk * BAND + ic * 128:
                                          fp * 16 * BAND + dk * BAND + (ic + 1) * 128],
                                    xnown[:, dk * OWN + tb * BLK:
                                          dk * OWN + (tb + 1) * BLK],
                                    start=(dk == 0), stop=False)
                            TE.matmul(ps_g[:, par * 512:par * 512 + BLK],
                                      mner_sb[0:1, :],
                                      xcs_hi[0:1, tb * BLK:(tb + 1) * BLK],
                                      start=False, stop=False)
                            TE.matmul(ps_g[:, par * 512:par * 512 + BLK],
                                      mner_sb[0:1, :],
                                      xcs_lo[0:1, tb * BLK:(tb + 1) * BLK],
                                      start=False, stop=True)
                            for dk in range(16):
                                TE.matmul(
                                    ps_u[:, par * 512:par * 512 + BLK],
                                    wut_b[:, fp * 16 * BAND + dk * BAND + ic * 128:
                                          fp * 16 * BAND + dk * BAND + (ic + 1) * 128],
                                    xnown[:, dk * OWN + tb * BLK:
                                          dk * OWN + (tb + 1) * BLK],
                                    start=(dk == 0), stop=False)
                            TE.matmul(ps_u[:, par * 512:par * 512 + BLK],
                                      mner_sb[0:1, :],
                                      xcs_hi[0:1, tb * BLK:(tb + 1) * BLK],
                                      start=False, stop=False)
                            ins = TE.matmul(ps_u[:, par * 512:par * 512 + BLK],
                                            mner_sb[0:1, :],
                                            xcs_lo[0:1, tb * BLK:(tb + 1) * BLK],
                                            start=False, stop=True)
                            c_mg.inc(ins)
                            AC.wait_ge(c_mg.sem, c_mg.v)
                            _w(AC, c_vh, n_gu - 1)
                            c_sil.inc(AC.activation(
                                sg_sb[:, par * BLK:(par + 1) * BLK],
                                ps_g[:, par * 512:par * 512 + BLK], AF.Silu,
                                scale=gsc_sb[:, ich:ich + 1]))
                            VE.wait_ge(c_sil.sem, c_sil.v)
                            if band >= 2 and ic == 0 and tb == 0:
                                _w(VE, c_md, md_band[band - 2])
                            c_vh.inc(VE.scalar_tensor_tensor(
                                hact[:, fp * TB * OWN + ic * OWN + tb * BLK:
                                     fp * TB * OWN + ic * OWN + (tb + 1) * BLK],
                                ps_u[:, par * 512:par * 512 + BLK],
                                usc_sb[:, ich:ich + 1],
                                sg_sb[:, par * BLK:(par + 1) * BLK],
                                ALU.mult, ALU.mult))
                            n_gu += 1
                    mg_band[band] = c_mg.v
                    VE.wait_ge(s_dw.sem, dld)
                    for ic in range(TB):
                        ins = VE.scalar_tensor_tensor(
                            wdt_b[:, ic * D:(ic + 1) * D],
                            wdt_b[:, ic * D:(ic + 1) * D],
                            128.0, dsfull[:, :], ALU.subtract, ALU.mult)
                    c_dqd.inc(ins)
                    TE.wait_ge(c_dqd.sem, c_dqd.v)
                    TE.wait_ge(c_vh.sem, c_vh.v)
                    for tt in range(8):
                        for half in range(2):
                            dp = n_d % 2
                            _w(TE, c_vacc, n_d - 1)
                            for j in range(2):
                                h0 = half * 1024 + j * 512
                                for ic in range(TB):
                                    ins = TE.matmul(
                                        ps_d[:, dp * 1024 + j * 512:
                                             dp * 1024 + (j + 1) * 512],
                                        hact[:, fp * TB * OWN + ic * OWN + tt * 128:
                                             fp * TB * OWN + ic * OWN + (tt + 1) * 128],
                                        wdt_b[:, ic * D + h0:ic * D + h0 + 512],
                                        start=(ic == 0), stop=(ic == TB - 1))
                            c_md.inc(ins)
                            VE.wait_ge(c_md.sem, c_md.v)
                            dst = oacc[:, tt * D + half * 1024:
                                       tt * D + (half + 1) * 1024]
                            if band == 0:
                                c_vacc.inc(VE.tensor_copy(
                                    dst, ps_d[:, dp * 1024:(dp + 1) * 1024]))
                            else:
                                _w(VE, c_vacc, acc_v[(tt, half)])
                                c_vacc.inc(VE.tensor_add(
                                    dst, ps_d[:, dp * 1024:(dp + 1) * 1024], dst))
                            acc_v[(tt, half)] = c_vacc.v
                            n_d += 1
                    md_band[band] = c_md.v

                GP.wait_ge(c_vacc.sem, c_vacc.v)
                GP.wait_ge(c_cc.sem, 1)
                for tt in range(8):
                    s_fa[0].inc(GP.dma_start(oacc[:, tt * D:(tt + 1) * D],
                                             xow[tt * 128:(tt + 1) * 128, :],
                                             accum_op=ALU.add), 16)
                GP.wait_ge(s_fa[0].sem, s_fa[0].v)
                for tt in range(8):
                    s_fa[1].inc(GP.dma_start(oacc[:, tt * D:(tt + 1) * D],
                                             ccout[tt * 128:(tt + 1) * 128, :],
                                             accum_op=ALU.add), 16)
                SY.wait_ge(s_fa[0].sem, s_fa[0].v)
                SY.wait_ge(s_fa[1].sem, s_fa[1].v)
                for tt in range(8):
                    c_out.inc(SY.dma_start(out_own[tt * 128:(tt + 1) * 128, :],
                                           oacc[:, tt * D:(tt + 1) * D]), 16)
                SY.wait_ge(c_out.sem, c_out.v)
    return nc


def _get_nc():
    key = ("v1", DEBUG)
    if key not in _CACHE:
        _CACHE[key] = _build()
    return _CACHE[key]


def make_in_maps(inputs):
    f32 = lambda a: np.ascontiguousarray(np.asarray(a, dtype=np.float32))
    i32 = lambda a: np.ascontiguousarray(np.asarray(a, dtype=np.int32))
    x = f32(inputs["L0_attn_x"])
    starts = i32(inputs["L0_attn_lora_start_indices"]).reshape(1, NL)
    Wq = f32(inputs["L0_attn_Wq"])
    Wk = f32(inputs["L0_attn_Wk"])
    Wv = f32(inputs["L0_attn_Wv"])
    Wo = f32(inputs["L0_attn_Wo"])
    Aq = f32(inputs["L0_attn_Aq"]).reshape(LR, D)
    Ak = f32(inputs["L0_attn_Ak"]).reshape(LR, D)
    Av = f32(inputs["L0_attn_Av"]).reshape(LR, D)
    Ao = f32(inputs["L0_attn_Ao"]).reshape(LR, D)
    Bq = f32(inputs["L0_attn_Bq"])
    Bk = f32(inputs["L0_attn_Bk"])
    Bv = f32(inputs["L0_attn_Bv"])
    Bo = f32(inputs["L0_attn_Bo"])
    gwv = i32(inputs["L0_ffn_gate_w"])
    uwv = i32(inputs["L0_ffn_up_w"])
    dwv = i32(inputs["L0_ffn_down_w"])
    gs = f32(inputs["L0_ffn_gate_scale"]).reshape(II // 128, 128).T.copy()
    us = f32(inputs["L0_ffn_up_scale"]).reshape(II // 128, 128).T.copy()
    dsrow = f32(inputs["L0_ffn_down_scale"]).reshape(1, D)

    CC, SS, swp, idn, masks, onc, onr, aidx = _host_consts()
    in_maps = []
    for c in range(8):
        g, r = c // 4, c % 4
        osl = slice(OSL * r, OSL * (r + 1))
        own = slice(OWN * r, OWN * (r + 1))
        in_maps.append({
            "xg": x[g],
            "xow": np.ascontiguousarray(x[g][own]),
            "wq": np.ascontiguousarray(Wq[osl]),
            "wk": np.ascontiguousarray(Wk[osl]),
            "wv": np.ascontiguousarray(Wv[osl]),
            "wo": np.ascontiguousarray(Wo[:, osl]),
            "aq": Aq, "ak": Ak, "av": Av,
            "ao": np.ascontiguousarray(Ao[:, osl]),
            "bq": np.ascontiguousarray(Bq[:, osl, :]),
            "bk": np.ascontiguousarray(Bk[:, osl, :]),
            "bv": np.ascontiguousarray(Bv[:, osl, :]),
            "bo": Bo,
            "gw": gwv, "uw": uwv, "dw": dwv,
            "gsc": gs, "usc": us, "dsrow": dsrow,
            "starts": starts,
            "bmeta": np.array([[float(g)]], np.float32),
            "own0m": np.array([[OWN * r]], np.int32),
            "cc": CC, "ss": SS, "masks": masks,
            "swp": swp, "idn": idn, "onc": onc, "onr": onr, "aidx": aidx,
            "mner": np.full((1, 128), -128.0, ml_dtypes.bfloat16),
        })
    return in_maps


def kernel(**inputs):
    in_maps = make_in_maps(inputs)
    nc = _get_nc()
    res = run_bass_kernel_spmd(nc, in_maps, core_ids=list(range(8)))
    out = np.empty((BATCH, S, D), np.float32)
    for c in range(8):
        g, r = c // 4, c % 4
        out[g, OWN * r:OWN * (r + 1), :] = res.results[c]["out_own"]
    if DEBUG:
        kernel.last_results = res
    return out


# version 16
# speedup vs baseline: 1.1696x; 1.1006x over previous
import math
from contextlib import ExitStack

import numpy as np
import ml_dtypes

import concourse.bass as bass
import concourse.mybir as mybir
from concourse.bass_utils import run_bass_kernel_spmd

F32 = mybir.dt.float32
BF16 = mybir.dt.bfloat16
I32 = mybir.dt.int32
AF = mybir.ActivationFunctionType
ALU = mybir.AluOpType
AX = mybir.AxisListType

BATCH = 2
S = 4096
D = 2048
DH = 128
HPC = 4
OSL = HPC * DH
OWN = 1024
II = 8192
NL, R = 4, 16
LR = NL * R
NBLK = 8
BLK = 512
EPS = 1e-5
SM_SCALE = 1.0 / math.sqrt(DH)
SM_BIAS = -4.0
BAND = 256
NBAND = II // BAND
TB = BAND // 128

DEBUG = False
_CACHE = {}


class Ctr:
    __slots__ = ("sem", "v")

    def __init__(self, stack, nc, name):
        self.sem = stack.enter_context(nc.semaphore(name))
        self.v = 0

    def inc(self, ins, n=1):
        ins.then_inc(self.sem, n)
        self.v += n
        return self.v


def _w(eng, ctr, v):
    if v > 0:
        eng.wait_ge(ctr.sem, v)


def _host_consts():
    inv_freq = 10000.0 ** (-np.arange(0, DH, 2, dtype=np.float64) / DH)
    ang = np.arange(S, dtype=np.float64)[:, None] * inv_freq[None, :]
    cos = np.cos(ang).T
    sin = np.sin(ang).T
    CC = np.empty((DH, S), np.float32)
    SS = np.empty((DH, S), np.float32)
    CC[0::2] = cos
    CC[1::2] = cos
    SS[0::2] = -sin
    SS[1::2] = sin
    swp = np.zeros((DH, DH), ml_dtypes.bfloat16)
    for i in range(0, DH, 2):
        swp[i + 1, i] = 1
        swp[i, i + 1] = 1
    idn = np.eye(DH, dtype=ml_dtypes.bfloat16)
    masks = np.zeros((4, 128, BLK), ml_dtypes.bfloat16)
    for v in range(4):
        kk = np.arange(128)[:, None] + 128 * v
        qq = np.arange(BLK)[None, :]
        masks[v] = (kk <= qq).astype(ml_dtypes.bfloat16)
    ones_col = np.ones((128, 1), ml_dtypes.bfloat16)
    ones_row = np.ones((1, 128), ml_dtypes.bfloat16)
    aidx = (np.arange(LR) // R).astype(np.float32)[:, None]
    return CC, SS, swp, idn, masks, ones_col, ones_row, aidx


def _build():
    nc = bass.Bass("TRN2", target_bir_lowering=False)
    ap = lambda name, shape, dt, kind="ExternalInput": nc.dram_tensor(
        name, shape, dt, kind=kind).ap()

    xg = ap("xg", [S, D], F32)
    xow = ap("xow", [OWN, D], F32)
    wq_i = ap("wq", [OSL, D], F32)
    wk_i = ap("wk", [OSL, D], F32)
    wv_i = ap("wv", [OSL, D], F32)
    wo_i = ap("wo", [D, OSL], F32)
    aq_i = ap("aq", [LR, D], F32)
    ak_i = ap("ak", [LR, D], F32)
    av_i = ap("av", [LR, D], F32)
    ao_i = ap("ao", [LR, OSL], F32)
    bq_i = ap("bq", [NL, OSL, R], F32)
    bk_i = ap("bk", [NL, OSL, R], F32)
    bv_i = ap("bv", [NL, OSL, R], F32)
    bo_i = ap("bo", [NL, D, R], F32)
    gw_i = ap("gw", [II, D], I32)
    uw_i = ap("uw", [II, D], I32)
    dw_i = ap("dw", [D, II], I32)
    gsc_i = ap("gsc", [128, II // 128], F32)
    usc_i = ap("usc", [128, II // 128], F32)
    dsrow_i = ap("dsrow", [1, D], F32)
    starts_i = ap("starts", [1, NL], I32)
    bmeta_i = ap("bmeta", [1, 1], F32)
    own0_i = ap("own0m", [1, 1], I32)
    cc_i = ap("cc", [DH, S], F32)
    ss_i = ap("ss", [DH, S], F32)
    mask_i = ap("masks", [4, 128, BLK], BF16)
    swp_i = ap("swp", [DH, DH], BF16)
    idn_i = ap("idn", [DH, DH], BF16)
    onc_i = ap("onc", [128, 1], BF16)
    onr_i = ap("onr", [1, 128], BF16)
    mner_i = ap("mner", [1, 128], BF16)
    aidx_i = ap("aidx", [LR, 1], F32)

    out_own = ap("out_own", [OWN, D], F32, "ExternalOutput")
    dbg = {}
    if DEBUG:
        dbg["k"] = ap("dbg_k", [128, HPC * S], BF16, "ExternalOutput")
        dbg["q"] = ap("dbg_q", [128, HPC * S], BF16, "ExternalOutput")
        dbg["v"] = ap("dbg_v", [S, OSL], BF16, "ExternalOutput")
        dbg["attn"] = ap("dbg_attn", [128, HPC * S], BF16, "ExternalOutput")
        dbg["ccin"] = ap("dbg_ccin", [S, D], F32, "ExternalOutput")

    xn_all = ap("xn_all", [16, 128, S], BF16, "Internal")
    v_all = ap("v_all", [S, OSL], BF16, "Internal")
    wo_bf = ap("wo_bf", [D, OSL], BF16, "Internal")
    gw_bf = ap("gw_bf", [II, D], BF16, "Internal")
    uw_bf = ap("uw_bf", [II, D], BF16, "Internal")
    dw_bf = ap("dw_bf", [D, II], BF16, "Internal")
    ccin = ap("ccin", [S, D], F32, "Internal")
    ccout = ap("ccout", [OWN, D], F32, "Internal")

    with ExitStack() as st:
        sb = lambda name, shape, dt, **kw: st.enter_context(
            nc.sbuf_tensor(name, shape, dt, **kw))
        C = lambda name: Ctr(st, nc, name)

        swp_sb = sb("swp_sb", [DH, DH], BF16)
        idn_sb = sb("idn_sb", [DH, DH], BF16)
        onc_sb = sb("onc_sb", [128, 1], BF16)
        onr_sb = sb("onr_sb", [1, 128], BF16)
        aidx_sb = sb("aidx_sb", [LR, 1], F32)
        gsc_sb = sb("gsc_sb", [128, II // 128], F32)
        usc_sb = sb("usc_sb", [128, II // 128], F32)
        dsrow_bf = sb("dsrow_bf", [1, D], BF16)
        dsfull = sb("dsfull", [128, D], BF16)
        ind_sb = sb("ind_sb", [LR, 1], F32)
        starts_sb = sb("starts_sb", [1, NL], I32)
        startsf_sb = sb("startsf_sb", [1, NL], F32)
        bmeta_sb = sb("bmeta_sb", [1, 1], F32)
        own0_sb = sb("own0_sb", [1, 1], I32)
        idsc_sb = sb("idsc_sb", [1, 8], F32)
        idbf_sb = sb("idbf_sb", [1, 1], BF16)
        smbias_sb = sb("smbias_sb", [128, 1], F32)
        eps_sb = sb("eps_sb", [128, 1], F32)
        aot = sb("aot", [128, HPC * LR], BF16)
        bot = sb("bot", [LR, D], BF16)
        mner_sb = sb("mner_sb", [1, 128], BF16)

        c_lds = C("c_lds")
        c_ldg = C("c_ldg")
        s_ct = C("s_ct")
        s_x = [C("s_x0"), C("s_x1"), C("s_x2")]
        s_rp = C("s_rp")
        s_vs = [C("s_vs0"), C("s_vs1")]
        s_vg = C("s_vg")
        s_wo = C("s_wo")
        s_cc = [C("s_cc0"), C("s_cc1")]
        s_gw = C("s_gw")
        s_ga = C("s_ga")
        s_dw = C("s_dw")
        s_xo = [C("s_xo0"), C("s_xo1")]
        s_fa = [C("s_fa0"), C("s_fa1")]
        c_dqd = C("c_dqd")
        c_ffc = C("c_ffc")
        c_xcs = C("c_xcs")
        c_tr = C("c_tr")
        c_ev = C("c_ev")
        c_mmq = C("c_mmq")
        c_mlow = C("c_mlow")
        c_vlow = C("c_vlow")
        c_qraw = C("c_qraw")
        c_msw = C("c_msw")
        c_rope = C("c_rope")
        c_vev = C("c_vev")
        c_sq = C("c_sq")
        c_vst = C("c_vst")
        c_rstd = C("c_rstd")
        c_xn = C("c_xn")
        c_msc = C("c_msc")
        c_exp = C("c_exp")
        c_msk = C("c_msk")
        c_mav = C("c_mav")
        c_rcp = C("c_rcp")
        c_mbc = C("c_mbc")
        c_vbc = C("c_vbc")
        c_vat = C("c_vat")
        c_vlo = C("c_vlo")
        c_mo = C("c_mo")
        c_evo = C("c_evo")
        c_scc = C("c_scc")
        c_cc = C("c_cc")
        c_mg = C("c_mg")
        c_sil = C("c_sil")
        c_vh = C("c_vh")
        c_md = C("c_md")
        c_vacc = C("c_vacc")
        c_fin = C("c_fin")
        c_out = C("c_out")
        c_p0 = C("c_p0")
        vse = C("vse")
        ase = C("ase")

        blk0 = st.enter_context(nc.Block())

        @blk0.sync
        def _(_sy):
            SY = nc.sync
            TE = nc.tensor
            VE = nc.vector
            AC = nc.scalar
            GP = nc.gpsimd

            def vi(ins):
                return vse.inc(ins)

            def vw():
                VE.wait_ge(vse.sem, vse.v)

            def ai(ins):
                return ase.inc(ins)

            def aw():
                AC.wait_ge(ase.sem, ase.v)

            with ExitStack() as sA:
                sbA = lambda n, sh, dt, **kw: sA.enter_context(
                    nc.sbuf_tensor(n, sh, dt, **kw))
                ksb = sbA("ksb", [128, HPC * S], BF16)
                qsb = sbA("qsb", [128, HPC * S], BF16)

                with ExitStack() as sB:
                    sbB = lambda n, sh, dt, **kw: sB.enter_context(
                        nc.sbuf_tensor(n, sh, dt, **kw))
                    psB = lambda n, sh, dt: sB.enter_context(
                        nc.psum_tensor(n, sh, dt))
                    wqt = sbB("wqt", [128, 16 * BLK], BF16)
                    wkt = sbB("wkt", [128, 16 * BLK], BF16)
                    wvt = sbB("wvt", [128, 16 * BLK], BF16)
                    aqt = sbB("aqt", [128, 16 * LR], BF16)
                    akt = sbB("akt", [128, 16 * LR], BF16)
                    avt = sbB("avt", [128, 16 * LR], BF16)
                    bqt = sbB("bqt", [LR, OSL], BF16)
                    bkt = sbB("bkt", [LR, OSL], BF16)
                    bvt = sbB("bvt", [LR, OSL], BF16)
                    xt_sb = sbB("xt_sb", [128, 3 * D], F32)
                    stats = sbB("stats", [128, 32], F32)
                    xntm = sbB("xntm", [128, 2 * D], BF16)
                    xnfm = sbB("xnfm", [128, 16 * BLK], BF16)
                    ccb = sbB("ccb", [128, BLK], F32)
                    ssb = sbB("ssb", [128, BLK], F32)
                    lowq = sbB("lowq", [LR, BLK], BF16)
                    lowk = sbB("lowk", [LR, BLK], BF16)
                    lowv = sbB("lowv", [LR, BLK], BF16)
                    qraw = sbB("qraw", [128, 2 * BLK], BF16)
                    rtmp2 = sbB("rtmp2", [128, BLK], F32)
                    vstg = sbB("vstg", [128, 2 * OSL], BF16)
                    ps_tr = psB("ps_tr", [128, 2 * 1024], BF16)
                    ps_qk0 = psB("ps_qk0", [128, 512], F32)
                    ps_qk1 = psB("ps_qk1", [128, 512], F32)
                    ps_v0 = psB("ps_v0", [128, 512], F32)
                    ps_v1 = psB("ps_v1", [128, 512], F32)
                    ps_low = psB("ps_low", [128, 512], F32)
                    ps_swap = psB("ps_swap", [128, 512], F32)

                    for dst, src in [
                        (swp_sb[:, :], swp_i[:, :]), (idn_sb[:, :], idn_i[:, :]),
                        (onc_sb[:, :], onc_i[:, :]), (onr_sb[:, :], onr_i[:, :]),
                        (aidx_sb[:, :], aidx_i[:, :]), (gsc_sb[:, :], gsc_i[:, :]),
                        (usc_sb[:, :], usc_i[:, :]),
                        (starts_sb[:, :], starts_i[:, :]),
                        (bmeta_sb[:, :], bmeta_i[:, :]),
                        (own0_sb[:, :], own0_i[:, :]),
                        (mner_sb[:, :], mner_i[:, :]),
                    ]:
                        s_ct.inc(SY.dma_start(dst, src), 16)
                    ld_consts = s_ct.v

                    c_ldg.inc(GP.dma_start(wo_bf[:, :], wo_i[:, :]), 16)
                    ld_wcast = c_ldg.v

                    TE.wait_ge(s_ct.sem, ld_consts)
                    with ExitStack() as sP0:
                        astg = sP0.enter_context(
                            nc.sbuf_tensor("astg", [LR, D], BF16))
                        bstk = sP0.enter_context(
                            nc.sbuf_tensor("bstk", [128, 16 * LR], BF16))
                        for adr, dstT, wid in [(aq_i, aqt, D), (ak_i, akt, D),
                                               (av_i, avt, D), (ao_i, aot, OSL)]:
                            _w(GP, c_tr, c_tr.v)
                            c_ldg.inc(GP.dma_start(astg[:, 0:wid], adr[:, :]), 16)
                            TE.wait_ge(c_ldg.sem, c_ldg.v)
                            _w(TE, c_ev, c_ev.v)
                            nch = wid // 128
                            for dk in range(nch):
                                ins = TE.matmul(
                                    ps_tr[:, dk * LR:(dk + 1) * LR],
                                    astg[:, dk * 128:(dk + 1) * 128],
                                    idn_sb[0:LR, 0:LR],
                                    is_transpose=True, start=True, stop=True)
                            c_tr.inc(ins)
                            VE.wait_ge(c_tr.sem, c_tr.v)
                            c_ev.inc(VE.tensor_copy(dstT[:, 0:nch * LR],
                                                    ps_tr[:, 0:nch * LR]))
                        for bdr, dstT, nch in [(bq_i, bqt, OSL // 128),
                                               (bk_i, bkt, OSL // 128),
                                               (bv_i, bvt, OSL // 128),
                                               (bo_i, bot, D // 128)]:
                            _w(GP, c_tr, c_tr.v)
                            for ch in range(nch):
                                src3 = bdr[:, ch * 128:(ch + 1) * 128, :].rearrange(
                                    "a p r -> p a r")
                                c_ldg.inc(GP.dma_start(
                                    bstk[:, ch * LR:(ch + 1) * LR].rearrange(
                                        "p (a r) -> p a r", a=NL), src3), 16)
                            TE.wait_ge(c_ldg.sem, c_ldg.v)
                            for half in range((nch + 7) // 8):
                                _w(TE, c_ev, c_ev.v)
                                lo = half * 8
                                hi = min(lo + 8, nch)
                                for ch in range(lo, hi):
                                    ins = TE.matmul(
                                        ps_tr[0:LR,
                                              (ch - lo) * 128:(ch - lo + 1) * 128],
                                        bstk[:, ch * LR:(ch + 1) * LR],
                                        idn_sb[:, :],
                                        is_transpose=True, start=True, stop=True)
                                c_tr.inc(ins)
                                VE.wait_ge(c_tr.sem, c_tr.v)
                                c_ev.inc(VE.tensor_copy(
                                    dstT[0:LR, lo * 128:hi * 128],
                                    ps_tr[0:LR, 0:(hi - lo) * 128]))

                    VE.wait_ge(s_ct.sem, ld_consts)
                    VE.memset(smbias_sb[:, :], SM_BIAS)
                    vi(VE.memset(eps_sb[:, :], EPS))
                    eps_mark = vse.v
                    vi(VE.tensor_copy(startsf_sb[:, :], starts_sb[:, :]))
                    vw()
                    vi(VE.tensor_single_scalar(idsc_sb[0:1, 0:NL],
                                               startsf_sb[0:1, :],
                                               bmeta_sb[0:1, 0:1], ALU.is_le))
                    vw()
                    vi(VE.reduce_sum(idsc_sb[0:1, 4:5], idsc_sb[0:1, 0:NL], AX.X))
                    vw()
                    vi(VE.tensor_scalar(idsc_sb[0:1, 5:6], idsc_sb[0:1, 4:5],
                                        -1.0, 0.0, ALU.add, ALU.max))
                    vw()
                    vi(VE.tensor_single_scalar(idsc_sb[0:1, 6:7], idsc_sb[0:1, 5:6],
                                               3.0, ALU.min))
                    vw()
                    c_p0.inc(VE.tensor_copy(idbf_sb[0:1, 0:1], idsc_sb[0:1, 6:7]))
                    TE.wait_ge(c_p0.sem, c_p0.v)
                    c_p0.inc(TE.matmul(ps_qk0[0:LR, 0:1], onr_sb[0:1, 0:LR],
                                       idbf_sb[0:1, 0:1], start=True, stop=True))
                    VE.wait_ge(c_p0.sem, c_p0.v)
                    vi(VE.tensor_tensor(ind_sb[:, :], ps_qk0[0:LR, 0:1],
                                        aidx_sb[:, :], ALU.is_equal))
                    vw()
                    for dstT, wid in [(bqt, OSL), (bkt, OSL), (bvt, OSL), (bot, D)]:
                        ins = VE.tensor_scalar(dstT[:, 0:wid], dstT[:, 0:wid],
                                               ind_sb[:, 0:1], None, ALU.mult)
                    c_p0.inc(ins)
                    p0_done = c_p0.v

                    wstg = xnfm
                    wtr_v = {}
                    n_wc = 0
                    for wsrc, dstT in [(wq_i, wqt), (wk_i, wkt), (wv_i, wvt)]:
                        dstv = dstT[:, :].rearrange("p (dk q) -> p dk q", dk=16)
                        for fc in range(4):
                            sl = n_wc % 2
                            if n_wc >= 2:
                                GP.wait_ge(c_tr.sem, wtr_v[n_wc - 2])
                            c_ldg.inc(GP.dma_start(
                                wstg[:, sl * D:(sl + 1) * D],
                                wsrc[fc * 128:(fc + 1) * 128, :]), 16)
                            ldv = c_ldg.v
                            TE.wait_ge(c_ldg.sem, ldv)
                            for grp in range(2):
                                _w(TE, c_ev, c_ev.v if n_wc == 0 and grp == 0
                                   else c_ev.v - 1)
                                psl = (grp % 2) * 1024
                                for j in range(8):
                                    dk = grp * 8 + j
                                    ins = TE.matmul(
                                        ps_tr[:, psl + j * 128:psl + (j + 1) * 128],
                                        wstg[:, sl * D + dk * 128:
                                             sl * D + (dk + 1) * 128],
                                        idn_sb[:, :], is_transpose=True,
                                        start=True, stop=True)
                                c_tr.inc(ins)
                                VE.wait_ge(c_tr.sem, c_tr.v)
                                c_ev.inc(VE.tensor_copy(
                                    dstv[:, grp * 8:(grp + 1) * 8,
                                         fc * 128:(fc + 1) * 128],
                                    ps_tr[:, psl:psl + 1024].rearrange(
                                        "p (c q) -> p c q", c=8)))
                            wtr_v[n_wc] = c_tr.v
                            n_wc += 1

                    GP.wait_ge(s_ct.sem, ld_consts)
                    c_ldg.inc(GP.dma_start(dsrow_bf[0:1, :], dsrow_i[0:1, :]), 16)
                    c_dsr = c_ldg.v
                    NCH = 8
                    RCH = II // NCH
                    for k in range(NCH):
                        _w(GP, c_xn, 4 * k)
                        r0 = k * RCH
                        c_ffc.inc(GP.dma_start(gw_bf[r0:r0 + RCH, :],
                                               gw_i[r0:r0 + RCH, :]), 16)
                        c_ffc.inc(GP.dma_start(uw_bf[r0:r0 + RCH, :],
                                               uw_i[r0:r0 + RCH, :]), 16)
                        c_ffc.inc(GP.dma_start(dw_bf[:, r0:r0 + RCH],
                                               dw_i[:, r0:r0 + RCH]), 16)
                    for j in range(4):
                        if j == 0:
                            TE.wait_ge(c_ldg.sem, c_dsr)
                        TE.wait_ge(vse.sem, vse.v)
                        c_p0.inc(TE.matmul(
                            ps_qk0[:, 0:BLK], onr_sb[0:1, :],
                            dsrow_bf[0:1, j * BLK:(j + 1) * BLK],
                            start=True, stop=True))
                        VE.wait_ge(c_p0.sem, c_p0.v)
                        vi(VE.tensor_copy(dsfull[:, j * BLK:(j + 1) * BLK],
                                          ps_qk0[:, 0:BLK]))
                    vw()
                    c_p0.inc(VE.tensor_copy(dsrow_bf[0:1, 0:1],
                                            dsrow_bf[0:1, 0:1]))
                    p0_done = c_p0.v

                    p0_tr = c_tr.v
                    n_qk = 0
                    n_v = 0
                    n_trf = 0
                    v_st = []
                    mmq_blk = {}
                    store_blk = {}
                    ld_rope = 0
                    for b in range(NBLK):
                        xld = []
                        for tt in range(4):
                            gi = 4 * b + tt
                            _w(SY, c_xn, gi - 2)
                            s_x[gi % 3].inc(SY.dma_start(
                                xt_sb[:, (gi % 3) * D:(gi % 3 + 1) * D],
                                xg[b * BLK + tt * 128:b * BLK + (tt + 1) * 128, :]),
                                16)
                            xld.append((s_x[gi % 3], s_x[gi % 3].v))
                        _w(SY, c_rope, 8 * b)
                        s_rp.inc(SY.dma_start(ccb[:, :],
                                              cc_i[:, b * BLK:(b + 1) * BLK]), 16)
                        s_rp.inc(SY.dma_start(ssb[:, :],
                                              ss_i[:, b * BLK:(b + 1) * BLK]), 16)
                        ld_rope = s_rp.v
                        for tt in range(4):
                            gi = 4 * b + tt
                            tp = gi % 2
                            sp = (gi % 4) * 8
                            st_ = lambda c: stats[:, sp + c:sp + c + 1]
                            xt = xt_sb[:, (gi % 3) * D:(gi % 3 + 1) * D]
                            xnslot = xntm[:, tp * D:(tp + 1) * D]
                            AC.wait_ge(xld[tt][0].sem, xld[tt][1])
                            if gi == 0:
                                AC.wait_ge(vse.sem, eps_mark)
                            _w(AC, c_tr, p0_tr + 2 * gi - 2 if gi >= 2 else 0)
                            c_sq.inc(AC.activation(xnslot, xt, AF.Square,
                                                   accum_out=st_(0)))
                            ai(AC.activation(xnslot, xt, AF.Identity,
                                             scale=-1.0 / D, accum_out=st_(1)))
                            aw()
                            ai(AC.activation(st_(2), st_(1), AF.Square))
                            aw()
                            ai(AC.activation(st_(3), st_(2), AF.Identity,
                                             scale=-1.0,
                                             bias=eps_sb[:, 0:1]))
                            AC.wait_ge(c_sq.sem, c_sq.v)
                            aw()
                            ai(AC.activation(st_(4), st_(0), AF.Identity,
                                             scale=1.0 / D, bias=st_(3)))
                            aw()
                            ai(AC.activation(st_(5), st_(4), AF.Ln))
                            aw()
                            ai(AC.activation(st_(6), st_(5), AF.Exp, scale=-0.5))
                            aw()
                            ai(AC.activation(st_(7), st_(1), AF.Identity,
                                             scale=st_(6)))
                            aw()
                            c_xn.inc(AC.activation(xnslot, xt, AF.Identity,
                                                   scale=st_(6), bias=st_(7)))
                            TE.wait_ge(c_xn.sem, c_xn.v)
                            xnv = xnfm[:, :].rearrange("p (dk q) -> p dk q", dk=16)
                            for grp in range(2):
                                _w(TE, c_ev, c_ev.v if n_trf == 0 else c_ev.v - 1)
                                sl = (n_trf % 2) * 1024
                                for j in range(8):
                                    dk = grp * 8 + j
                                    ins = TE.matmul(
                                        ps_tr[:, sl + j * 128:sl + (j + 1) * 128],
                                        xntm[:, tp * D + dk * 128:
                                             tp * D + (dk + 1) * 128],
                                        idn_sb[:, :], is_transpose=True,
                                        start=True, stop=True)
                                c_tr.inc(ins)
                                n_trf += 1
                                VE.wait_ge(c_tr.sem, c_tr.v)
                                if b >= 1 and tt == 0 and grp == 0:
                                    _w(VE, c_mmq, mmq_blk[b - 1])
                                c_ev.inc(VE.tensor_copy(
                                    xnv[:, grp * 8:(grp + 1) * 8,
                                        tt * 128:(tt + 1) * 128],
                                    ps_tr[:, sl:sl + 1024].rearrange(
                                        "p (c q) -> p c q", c=8)))
                        ev_blk = c_ev.v
                        xb = lambda dk: xnfm[:, dk * BLK:(dk + 1) * BLK]

                        TE.wait_ge(c_ev.sem, ev_blk)
                        if b == 0:
                            TE.wait_ge(c_p0.sem, p0_done)
                        for at, lowsb in [(aqt, lowq), (akt, lowk), (avt, lowv)]:
                            _w(TE, c_vlow, c_vlow.v)
                            for dk in range(16):
                                ins = TE.matmul(ps_low[0:LR, 0:BLK],
                                                at[:, dk * LR:(dk + 1) * LR],
                                                xb(dk), start=(dk == 0),
                                                stop=(dk == 15))
                            c_mlow.inc(ins)
                            VE.wait_ge(c_mlow.sem, c_mlow.v)
                            c_vlow.inc(VE.tensor_copy(lowsb[:, :],
                                                      ps_low[0:LR, 0:BLK]))

                        for wt, low, bt, dstsb in [(wqt, lowq, bqt, qsb),
                                                   (wkt, lowk, bkt, ksb)]:
                            for hh in range(HPC):
                                u = n_qk
                                pst = ps_qk0 if (u % 2) == 0 else ps_qk1
                                _w(TE, c_qraw, u - 1)
                                TE.wait_ge(c_vlow.sem, c_vlow.v)
                                for dk in range(16):
                                    TE.matmul(pst[:, 0:BLK],
                                              wt[:, dk * BLK + hh * 128:
                                                 dk * BLK + (hh + 1) * 128],
                                              xb(dk), start=(dk == 0), stop=False)
                                c_mmq.inc(TE.matmul(
                                    pst[:, 0:BLK], bt[:, hh * 128:(hh + 1) * 128],
                                    low[:, :], start=False, stop=True))
                                VE.wait_ge(c_mmq.sem, c_mmq.v)
                                _w(VE, c_msw, u - 1)
                                c_qraw.inc(VE.tensor_copy(
                                    qraw[:, (u % 2) * BLK:(u % 2 + 1) * BLK],
                                    pst[:, 0:BLK]))
                                TE.wait_ge(c_qraw.sem, c_qraw.v)
                                _w(TE, c_rope, u)
                                c_msw.inc(TE.matmul(
                                    ps_swap[:, 0:BLK], swp_sb[:, :],
                                    qraw[:, (u % 2) * BLK:(u % 2 + 1) * BLK],
                                    start=True, stop=True))
                                VE.wait_ge(s_rp.sem, ld_rope)
                                VE.wait_ge(c_qraw.sem, c_qraw.v)
                                dst = dstsb[:, hh * S + b * BLK:
                                            hh * S + (b + 1) * BLK]
                                vi(VE.tensor_tensor(
                                    dst, qraw[:, (u % 2) * BLK:(u % 2 + 1) * BLK],
                                    ccb[:, :], ALU.mult))
                                VE.wait_ge(c_msw.sem, c_msw.v)
                                vi(VE.tensor_tensor(rtmp2[:, :], ps_swap[:, 0:BLK],
                                                    ssb[:, :], ALU.mult))
                                vw()
                                c_rope.inc(VE.tensor_add(dst, dst, rtmp2[:, :]))
                                n_qk += 1

                        for tt in range(4):
                            pst = ps_v0 if (n_v % 2) == 0 else ps_v1
                            _w(TE, c_vev, n_v - 1)
                            for dk in range(16):
                                TE.matmul(pst[:, 0:OSL],
                                          xnfm[:, dk * BLK + tt * 128:
                                               dk * BLK + (tt + 1) * 128],
                                          wvt[:, dk * BLK:(dk + 1) * BLK],
                                          start=(dk == 0), stop=False)
                            c_mmq.inc(TE.matmul(pst[:, 0:OSL],
                                                lowv[:, tt * 128:(tt + 1) * 128],
                                                bvt[:, :], start=False, stop=True))
                            VE.wait_ge(c_mmq.sem, c_mmq.v)
                            if n_v >= 2:
                                _w(VE, s_vs[n_v % 2], s_vs[n_v % 2].v)
                            c_vev.inc(VE.tensor_copy(
                                vstg[:, (n_v % 2) * OSL:(n_v % 2 + 1) * OSL],
                                pst[:, 0:OSL]))
                            SY.wait_ge(c_vev.sem, c_vev.v)
                            kc = b * 4 + tt
                            s_vs[n_v % 2].inc(SY.dma_start(
                                v_all[kc * 128:(kc + 1) * 128, :],
                                vstg[:, (n_v % 2) * OSL:(n_v % 2 + 1) * OSL]), 16)
                            n_v += 1
                        mmq_blk[b] = c_mmq.v
                    p1_rope = c_rope.v

                with ExitStack() as sC:
                    sbC = lambda n, sh, dt, **kw: sC.enter_context(
                        nc.sbuf_tensor(n, sh, dt, **kw))
                    psC = lambda n, sh, dt: sC.enter_context(
                        nc.psum_tensor(n, sh, dt))
                    vsb = sbC("vsb", [128, HPC * S], BF16)
                    asb = sbC("asb", [128, HPC * S], BF16)
                    psb = sbC("psb", [128, 2 * BLK], BF16)
                    mask_sb = sbC("mask_sb", [128, 4 * BLK], BF16)
                    sums_f = sbC("sums_f", [1, BLK], F32)
                    rinvf = sbC("rinvf", [1, BLK], F32)
                    rinvb = sbC("rinvb", [1, BLK], BF16)
                    binv = sbC("binv", [128, BLK], BF16)
                    sP2 = ExitStack()
                    ps_sc = sP2.enter_context(
                        nc.psum_tensor("ps_sc", [128, 2 * 512], F32))
                    ps_attn = sP2.enter_context(
                        nc.psum_tensor("ps_attn", [128, 512], F32))
                    ps_sum = sP2.enter_context(
                        nc.psum_tensor("ps_sum", [128, 512], F32))
                    ps_bc = sP2.enter_context(
                        nc.psum_tensor("ps_bc", [128, 512], F32))

                    SY.wait_ge(s_vs[0].sem, s_vs[0].v)
                    SY.wait_ge(s_vs[1].sem, s_vs[1].v)
                    vsrc = v_all.rearrange("(kc p) d -> p kc d", p=128)
                    for h in range(HPC):
                        s_vg.inc(SY.dma_start(
                            vsb[:, h * S:(h + 1) * S].rearrange(
                                "p (kc d) -> p kc d", kc=32),
                            vsrc[:, :, h * 128:(h + 1) * 128]), 16)
                    for v in range(4):
                        s_vg.inc(SY.dma_start(
                            mask_sb[:, v * BLK:(v + 1) * BLK], mask_i[v, :, :]), 16)
                    ld_v = s_vg.v

                    TE.wait_ge(c_rope.sem, p1_rope)
                    TE.wait_ge(s_vg.sem, ld_v)
                    VE.wait_ge(s_vg.sem, ld_v)
                    n_kc = 0
                    n_qb = 0
                    for u in range(HPC):
                        for qb in range(NBLK):
                            nk = 4 * qb + 4
                            pend = []

                            def consume(TE=TE, u=u, qb=qb, nk=nk):
                                par, ctr, val, kcc = pend.pop(0)
                                _w(TE, ctr, val)
                                if kcc == 0:
                                    _w(TE, c_vat, n_qb)
                                    _w(TE, c_rcp, n_qb)
                                TE.matmul(
                                    ps_attn[:, 0:BLK],
                                    vsb[:, u * S + kcc * 128:u * S + (kcc + 1) * 128],
                                    psb[:, par * BLK:(par + 1) * BLK],
                                    start=(kcc == 0), stop=(kcc == nk - 1))
                                c_mav.inc(TE.matmul(
                                    ps_sum[0:1, 0:BLK], onc_sb[:, 0:1],
                                    psb[:, par * BLK:(par + 1) * BLK],
                                    start=(kcc == 0), stop=(kcc == nk - 1)))

                            for kc in range(nk):
                                par = n_kc % 2
                                _w(TE, c_exp, n_kc - 1)
                                c_msc.inc(TE.matmul(
                                    ps_sc[:, par * 512:par * 512 + BLK],
                                    ksb[:, u * S + kc * 128:u * S + (kc + 1) * 128],
                                    qsb[:, u * S + qb * BLK:u * S + (qb + 1) * BLK],
                                    start=True, stop=True))
                                AC.wait_ge(c_msc.sem, c_msc.v)
                                _w(AC, c_mav, n_kc - 1)
                                c_exp.inc(AC.activation(
                                    psb[:, par * BLK:(par + 1) * BLK],
                                    ps_sc[:, par * 512:par * 512 + BLK],
                                    AF.Exp, bias=smbias_sb[:, 0:1],
                                    scale=SM_SCALE))
                                diag = kc - 4 * qb
                                if diag >= 0:
                                    VE.wait_ge(c_exp.sem, c_exp.v)
                                    c_msk.inc(VE.tensor_tensor(
                                        psb[:, par * BLK:(par + 1) * BLK],
                                        psb[:, par * BLK:(par + 1) * BLK],
                                        mask_sb[:, diag * BLK:(diag + 1) * BLK],
                                        ALU.mult))
                                    pend.append((par, c_msk, c_msk.v, kc))
                                else:
                                    pend.append((par, c_exp, c_exp.v, kc))
                                if len(pend) > 1:
                                    consume()
                                n_kc += 1
                            consume()
                            VE.wait_ge(c_mav.sem, c_mav.v)
                            vi(VE.tensor_copy(sums_f[0:1, :], ps_sum[0:1, 0:BLK]))
                            vw()
                            vi(VE.reciprocal(rinvf[0:1, :], sums_f[0:1, :]))
                            vw()
                            c_rcp.inc(VE.tensor_copy(rinvb[0:1, :], rinvf[0:1, :]))
                            TE.wait_ge(c_rcp.sem, c_rcp.v)
                            _w(TE, c_vbc, n_qb)
                            c_mbc.inc(TE.matmul(ps_bc[:, 0:BLK], onr_sb[0:1, :],
                                                rinvb[0:1, :],
                                                start=True, stop=True))
                            VE.wait_ge(c_mbc.sem, c_mbc.v)
                            c_vbc.inc(VE.tensor_copy(binv[:, :], ps_bc[:, 0:BLK]))
                            VE.wait_ge(c_vbc.sem, c_vbc.v)
                            c_vat.inc(VE.tensor_tensor(
                                asb[:, u * S + qb * BLK:u * S + (qb + 1) * BLK],
                                ps_attn[:, 0:BLK], binv[:, :], ALU.mult))
                            n_qb += 1
                    p2_done = c_vat.v
                    sP2.close()

                    if DEBUG:
                        SY.wait_ge(c_vat.sem, p2_done)
                        c_lds.inc(SY.dma_start(dbg["k"][:, :], ksb[:, :]), 16)
                        c_lds.inc(SY.dma_start(dbg["q"][:, :], qsb[:, :]), 16)
                        c_lds.inc(SY.dma_start(dbg["v"][:, :], v_all[:, :]), 16)
                        c_lds.inc(SY.dma_start(dbg["attn"][:, :], asb[:, :]), 16)

                    wot = sbC("wot", [128, HPC * D], BF16, is_transpose_dest=True)
                    lowo = sbC("lowo", [LR, S], BF16)
                    osb = sbC("osb", [128, 2 * D], F32)
                    for c in range(HPC):
                        s_wo.inc(SY.dma_start(
                            wot[:, c * D:(c + 1) * D],
                            wo_bf[:, c * 128:(c + 1) * 128], transpose=True), 16)
                    ld_wot = s_wo.v

                    with ExitStack() as sL:
                        ps_lo3 = sL.enter_context(
                            nc.psum_tensor("ps_lo3", [128, 512], F32))
                        TE.wait_ge(c_vat.sem, p2_done)
                        for blk in range(NBLK):
                            _w(TE, c_vlo, blk)
                            for c in range(HPC):
                                ins = TE.matmul(
                                    ps_lo3[0:LR, 0:BLK],
                                    aot[:, c * LR:(c + 1) * LR],
                                    asb[:, c * S + blk * BLK:
                                        c * S + (blk + 1) * BLK],
                                    start=(c == 0), stop=(c == HPC - 1))
                            c_mlow.inc(ins)
                            VE.wait_ge(c_mlow.sem, c_mlow.v)
                            c_vlo.inc(VE.tensor_copy(
                                lowo[:, blk * BLK:(blk + 1) * BLK],
                                ps_lo3[0:LR, 0:BLK]))
                    with ExitStack() as sO:
                        ps_o = sO.enter_context(
                            nc.psum_tensor("ps_o", [128, 2 * D], F32))
                        TE.wait_ge(s_wo.sem, ld_wot)
                        TE.wait_ge(c_vlo.sem, c_vlo.v)
                        for tt in range(S // 128):
                            par = tt % 2
                            op = par * D
                            _w(TE, c_evo, 2 * (tt - 1))
                            for j in range(4):
                                for c in range(HPC):
                                    TE.matmul(
                                        ps_o[:, op + j * 512:op + (j + 1) * 512],
                                        asb[:, c * S + tt * 128:
                                            c * S + (tt + 1) * 128],
                                        wot[:, c * D + j * 512:c * D + (j + 1) * 512],
                                        start=(c == 0), stop=False)
                                ins = TE.matmul(
                                    ps_o[:, op + j * 512:op + (j + 1) * 512],
                                    lowo[:, tt * 128:(tt + 1) * 128],
                                    bot[:, j * 512:(j + 1) * 512],
                                    start=False, stop=True)
                            c_mo.inc(ins)
                            VE.wait_ge(c_mo.sem, c_mo.v)
                            _w(VE, s_cc[par], s_cc[par].v)
                            c_evo.inc(VE.tensor_copy(osb[:, op:op + 1024],
                                                     ps_o[:, op:op + 1024]))
                            AC.wait_ge(c_mo.sem, c_mo.v)
                            _w(AC, s_cc[par], s_cc[par].v)
                            c_evo.inc(AC.copy(osb[:, op + 1024:op + D],
                                              ps_o[:, op + 1024:op + D]))
                            SY.wait_ge(c_evo.sem, c_evo.v)
                            s_cc[par].inc(SY.dma_start(
                                ccin[tt * 128:(tt + 1) * 128, :],
                                osb[:, op:op + D]), 16)

                    if DEBUG:
                        SY.wait_ge(s_cc[0].sem, s_cc[0].v)
                        SY.wait_ge(s_cc[1].sem, s_cc[1].v)
                        c_lds.inc(SY.dma_start(dbg["ccin"][:, :], ccin[:, :]), 16)

                    GP.wait_ge(s_cc[0].sem, s_cc[0].v)
                    GP.wait_ge(s_cc[1].sem, s_cc[1].v)
                    c_cc.inc(GP.collective_compute(
                        "ReduceScatter", ALU.add,
                        replica_groups=[[0, 1, 2, 3], [4, 5, 6, 7]],
                        ins=[ccin[:, :].opt()], outs=[ccout[:, :].opt()]), 1)

            with ExitStack() as sD:
                sbD = lambda n, sh, dt, **kw: sD.enter_context(
                    nc.sbuf_tensor(n, sh, dt, **kw))
                psD = lambda n, sh, dt: sD.enter_context(nc.psum_tensor(n, sh, dt))
                xnown = sbD("xnown", [128, 16 * OWN], BF16)
                wgt_b = sbD("wgt_b", [128, 2 * 16 * BAND], BF16,
                            is_transpose_dest=True)
                wut_b = sbD("wut_b", [128, 2 * 16 * BAND], BF16,
                            is_transpose_dest=True)
                wdt_b = sbD("wdt_b", [128, 2 * TB * D], BF16,
                            is_transpose_dest=True)
                hact = sbD("hact", [128, 2 * TB * OWN], BF16)
                sg_sb = sbD("sg_sb", [128, 2 * BLK], BF16)
                oacc = sbD("oacc", [128, 8 * D], F32)
                xcs_hi = sbD("xcs_hi", [1, 2 * BLK], BF16)
                xcs_lo = sbD("xcs_lo", [1, 2 * BLK], BF16)
                xcs_tmp = sbD("xcs_tmp", [1, BLK], F32)

                p3_mo = c_mo.v
                p3_evo = c_evo.v
                gld_band = {}
                uld_band = {}
                dld_band = {}
                mg_band = {}
                md_band = {}

                def issue_band(band):
                    fp = band % 2
                    wsrc = band * BAND
                    if band == 0:
                        for q in (SY, AC):
                            q.wait_ge(c_mo.sem, p3_mo)
                            q.wait_ge(c_evo.sem, p3_evo)
                            q.wait_ge(c_ffc.sem, c_ffc.v)
                    if band >= 2:
                        _w(SY, c_mg, mg_band[band - 2])
                        _w(AC, c_mg, mg_band[band - 2])
                        _w(SY, c_md, md_band[band - 2])
                    for dk in range(16):
                        s_gw.inc(SY.dma_start(
                            wgt_b[:, fp * 16 * BAND + dk * BAND:
                                  fp * 16 * BAND + (dk + 1) * BAND],
                            gw_bf[wsrc:wsrc + BAND, dk * 128:(dk + 1) * 128],
                            transpose=True), 16)
                        s_ga.inc(AC.dma_start(
                            wut_b[:, fp * 16 * BAND + dk * BAND:
                                  fp * 16 * BAND + (dk + 1) * BAND],
                            uw_bf[wsrc:wsrc + BAND, dk * 128:(dk + 1) * 128],
                            transpose=True), 16)
                    gld_band[band] = s_gw.v
                    uld_band[band] = s_ga.v
                    for ic in range(TB):
                        s_dw.inc(SY.dma_start(
                            wdt_b[:, fp * TB * D + ic * D:
                                  fp * TB * D + (ic + 1) * D],
                            dw_bf[:, wsrc + ic * 128:wsrc + (ic + 1) * 128],
                            transpose=True), 16)
                    dld_band[band] = s_dw.v

                with ExitStack() as sP4a:
                    xts = sP4a.enter_context(
                        nc.sbuf_tensor("xts", [128, 2 * D], F32))
                    xnt4 = sP4a.enter_context(
                        nc.sbuf_tensor("xnt4", [128, 2 * D], BF16))
                    st4 = sP4a.enter_context(
                        nc.sbuf_tensor("st4", [128, 32], F32))
                    ps_t4 = sP4a.enter_context(
                        nc.psum_tensor("ps_t4", [128, 2 * 1024], BF16))
                    TE.wait_ge(c_evo.sem, c_evo.v)
                    p4_tr0 = c_tr.v
                    n_t4 = 0
                    xo_ld = []
                    SY.wait_ge(c_mo.sem, p3_mo)
                    SY.wait_ge(c_evo.sem, p3_evo)
                    for ti in range(8):
                        _w(SY, c_xn, c_xn.v + ti - 1)
                        s_xo[ti % 2].inc(SY.dma_start(
                            xts[:, (ti % 2) * D:(ti % 2 + 1) * D],
                            xow[ti * 128:(ti + 1) * 128, :]), 16)
                        xo_ld.append((s_xo[ti % 2], s_xo[ti % 2].v))
                    issue_band(0)
                    issue_band(1)
                    xnv4 = xnown[:, :].rearrange("p (dk q) -> p dk q", dk=16)
                    for ti in range(8):
                        tp = ti % 2
                        sp = (ti % 4) * 8
                        st_ = lambda c: st4[:, sp + c:sp + c + 1]
                        xt = xts[:, tp * D:(tp + 1) * D]
                        xnslot = xnt4[:, tp * D:(tp + 1) * D]
                        AC.wait_ge(xo_ld[ti][0].sem, xo_ld[ti][1])
                        _w(AC, c_tr, p4_tr0 + 2 * ti - 2 if ti >= 2 else 0)
                        c_sq.inc(AC.activation(xnslot, xt, AF.Square,
                                               accum_out=st_(0)))
                        ai(AC.activation(xnslot, xt, AF.Identity,
                                         scale=-1.0 / D, accum_out=st_(1)))
                        aw()
                        ai(AC.activation(st_(2), st_(1), AF.Square))
                        aw()
                        ai(AC.activation(st_(3), st_(2), AF.Identity,
                                         scale=-1.0, bias=eps_sb[:, 0:1]))
                        AC.wait_ge(c_sq.sem, c_sq.v)
                        aw()
                        ai(AC.activation(st_(4), st_(0), AF.Identity,
                                         scale=1.0 / D, bias=st_(3)))
                        aw()
                        ai(AC.activation(st_(5), st_(4), AF.Ln))
                        aw()
                        ai(AC.activation(st_(6), st_(5), AF.Exp, scale=-0.5))
                        aw()
                        ai(AC.activation(st_(7), st_(1), AF.Identity,
                                         scale=st_(6)))
                        aw()
                        c_xn.inc(AC.activation(xnslot, xt, AF.Identity,
                                               scale=st_(6), bias=st_(7)))
                        TE.wait_ge(c_xn.sem, c_xn.v)
                        for grp in range(2):
                            _w(TE, c_ev, c_ev.v if n_t4 == 0 else c_ev.v - 1)
                            sl = (n_t4 % 2) * 1024
                            for j in range(8):
                                dk = grp * 8 + j
                                ins = TE.matmul(
                                    ps_t4[:, sl + j * 128:sl + (j + 1) * 128],
                                    xnt4[:, tp * D + dk * 128:
                                         tp * D + (dk + 1) * 128],
                                    idn_sb[:, :], is_transpose=True,
                                    start=True, stop=True)
                            c_tr.inc(ins)
                            n_t4 += 1
                            VE.wait_ge(c_tr.sem, c_tr.v)
                            c_ev.inc(VE.tensor_copy(
                                xnv4[:, grp * 8:(grp + 1) * 8,
                                     ti * 128:(ti + 1) * 128],
                                ps_t4[:, sl:sl + 1024].rearrange(
                                    "p (c q) -> p c q", c=8)))
                    TE.wait_ge(c_ev.sem, c_ev.v)
                    ps_cs = sP4a.enter_context(
                        nc.psum_tensor("ps_cs", [128, 2 * 512], F32))
                    for tb in range(2):
                        cs = ps_cs[0:1, tb * 512:tb * 512 + BLK]
                        for dk in range(16):
                            ins = TE.matmul(cs, onc_sb[:, 0:1],
                                            xnown[:, dk * OWN + tb * BLK:
                                                  dk * OWN + (tb + 1) * BLK],
                                            start=(dk == 0), stop=(dk == 15))
                        c_tr.inc(ins)
                        VE.wait_ge(c_tr.sem, c_tr.v)
                        vi(VE.tensor_copy(xcs_hi[0:1, tb * BLK:(tb + 1) * BLK],
                                          cs))
                        vw()
                        vi(VE.tensor_sub(xcs_tmp[0:1, :], cs,
                                         xcs_hi[0:1, tb * BLK:(tb + 1) * BLK]))
                        vw()
                        c_xcs.inc(VE.tensor_copy(
                            xcs_lo[0:1, tb * BLK:(tb + 1) * BLK],
                            xcs_tmp[0:1, :]))
                ps_g = psD("ps_g", [128, 2 * 512], F32)
                ps_u = psD("ps_u", [128, 2 * 512], F32)
                ps_d = psD("ps_d", [128, 2 * 1024], F32)

                n_gu = 0
                n_d = 0
                acc_v = {}
                TE.wait_ge(c_xcs.sem, c_xcs.v)
                for band in range(NBAND):
                    fp = band % 2
                    TE.wait_ge(s_gw.sem, gld_band[band])
                    TE.wait_ge(s_ga.sem, uld_band[band])
                    for ic in range(TB):
                        for tb in range(2):
                            par = n_gu % 2
                            _w(TE, c_sil, n_gu - 1)
                            _w(TE, c_vh, n_gu - 1)
                            ich = band * TB + ic
                            for dk in range(16):
                                TE.matmul(
                                    ps_g[:, par * 512:par * 512 + BLK],
                                    wgt_b[:, fp * 16 * BAND + dk * BAND + ic * 128:
                                          fp * 16 * BAND + dk * BAND + (ic + 1) * 128],
                                    xnown[:, dk * OWN + tb * BLK:
                                          dk * OWN + (tb + 1) * BLK],
                                    start=(dk == 0), stop=False)
                            TE.matmul(ps_g[:, par * 512:par * 512 + BLK],
                                      mner_sb[0:1, :],
                                      xcs_hi[0:1, tb * BLK:(tb + 1) * BLK],
                                      start=False, stop=False)
                            TE.matmul(ps_g[:, par * 512:par * 512 + BLK],
                                      mner_sb[0:1, :],
                                      xcs_lo[0:1, tb * BLK:(tb + 1) * BLK],
                                      start=False, stop=True)
                            for dk in range(16):
                                TE.matmul(
                                    ps_u[:, par * 512:par * 512 + BLK],
                                    wut_b[:, fp * 16 * BAND + dk * BAND + ic * 128:
                                          fp * 16 * BAND + dk * BAND + (ic + 1) * 128],
                                    xnown[:, dk * OWN + tb * BLK:
                                          dk * OWN + (tb + 1) * BLK],
                                    start=(dk == 0), stop=False)
                            TE.matmul(ps_u[:, par * 512:par * 512 + BLK],
                                      mner_sb[0:1, :],
                                      xcs_hi[0:1, tb * BLK:(tb + 1) * BLK],
                                      start=False, stop=False)
                            ins = TE.matmul(ps_u[:, par * 512:par * 512 + BLK],
                                            mner_sb[0:1, :],
                                            xcs_lo[0:1, tb * BLK:(tb + 1) * BLK],
                                            start=False, stop=True)
                            c_mg.inc(ins)
                            AC.wait_ge(c_mg.sem, c_mg.v)
                            _w(AC, c_vh, n_gu - 1)
                            c_sil.inc(AC.activation(
                                sg_sb[:, par * BLK:(par + 1) * BLK],
                                ps_g[:, par * 512:par * 512 + BLK], AF.Silu,
                                scale=gsc_sb[:, ich:ich + 1]))
                            VE.wait_ge(c_sil.sem, c_sil.v)
                            if band >= 2 and ic == 0 and tb == 0:
                                _w(VE, c_md, md_band[band - 2])
                            c_vh.inc(VE.scalar_tensor_tensor(
                                hact[:, fp * TB * OWN + ic * OWN + tb * BLK:
                                     fp * TB * OWN + ic * OWN + (tb + 1) * BLK],
                                ps_u[:, par * 512:par * 512 + BLK],
                                usc_sb[:, ich:ich + 1],
                                sg_sb[:, par * BLK:(par + 1) * BLK],
                                ALU.mult, ALU.mult))
                            n_gu += 1
                    mg_band[band] = c_mg.v
                    VE.wait_ge(s_dw.sem, dld_band[band])
                    for ic in range(TB):
                        ins = VE.scalar_tensor_tensor(
                            wdt_b[:, fp * TB * D + ic * D:
                                  fp * TB * D + (ic + 1) * D],
                            wdt_b[:, fp * TB * D + ic * D:
                                  fp * TB * D + (ic + 1) * D],
                            128.0, dsfull[:, :], ALU.subtract, ALU.mult)
                    c_dqd.inc(ins)
                    TE.wait_ge(c_dqd.sem, c_dqd.v)
                    TE.wait_ge(c_vh.sem, c_vh.v)
                    for tt in range(8):
                        for half in range(2):
                            dp = n_d % 2
                            _w(TE, c_vacc, n_d - 1)
                            for j in range(2):
                                h0 = half * 1024 + j * 512
                                for ic in range(TB):
                                    ins = TE.matmul(
                                        ps_d[:, dp * 1024 + j * 512:
                                             dp * 1024 + (j + 1) * 512],
                                        hact[:, fp * TB * OWN + ic * OWN + tt * 128:
                                             fp * TB * OWN + ic * OWN + (tt + 1) * 128],
                                        wdt_b[:, fp * TB * D + ic * D + h0:
                                              fp * TB * D + ic * D + h0 + 512],
                                        start=(ic == 0), stop=(ic == TB - 1))
                            c_md.inc(ins)
                            VE.wait_ge(c_md.sem, c_md.v)
                            dst = oacc[:, tt * D + half * 1024:
                                       tt * D + (half + 1) * 1024]
                            if band == 0:
                                c_vacc.inc(VE.tensor_copy(
                                    dst, ps_d[:, dp * 1024:(dp + 1) * 1024]))
                            else:
                                _w(VE, c_vacc, acc_v[(tt, half)])
                                c_vacc.inc(VE.tensor_add(
                                    dst, ps_d[:, dp * 1024:(dp + 1) * 1024], dst))
                            acc_v[(tt, half)] = c_vacc.v
                            n_d += 1
                    md_band[band] = c_md.v
                    if band + 2 < NBAND:
                        issue_band(band + 2)

                GP.wait_ge(c_vacc.sem, c_vacc.v)
                GP.wait_ge(c_cc.sem, 1)
                for tt in range(8):
                    s_fa[0].inc(GP.dma_start(oacc[:, tt * D:(tt + 1) * D],
                                             xow[tt * 128:(tt + 1) * 128, :],
                                             accum_op=ALU.add), 16)
                GP.wait_ge(s_fa[0].sem, s_fa[0].v)
                for tt in range(8):
                    s_fa[1].inc(GP.dma_start(oacc[:, tt * D:(tt + 1) * D],
                                             ccout[tt * 128:(tt + 1) * 128, :],
                                             accum_op=ALU.add), 16)
                SY.wait_ge(s_fa[0].sem, s_fa[0].v)
                SY.wait_ge(s_fa[1].sem, s_fa[1].v)
                for tt in range(8):
                    c_out.inc(SY.dma_start(out_own[tt * 128:(tt + 1) * 128, :],
                                           oacc[:, tt * D:(tt + 1) * D]), 16)
                SY.wait_ge(c_out.sem, c_out.v)
    return nc


def _get_nc():
    key = ("v1", DEBUG)
    if key not in _CACHE:
        _CACHE[key] = _build()
    return _CACHE[key]


def make_in_maps(inputs):
    f32 = lambda a: np.ascontiguousarray(np.asarray(a, dtype=np.float32))
    i32 = lambda a: np.ascontiguousarray(np.asarray(a, dtype=np.int32))
    x = f32(inputs["L0_attn_x"])
    starts = i32(inputs["L0_attn_lora_start_indices"]).reshape(1, NL)
    Wq = f32(inputs["L0_attn_Wq"])
    Wk = f32(inputs["L0_attn_Wk"])
    Wv = f32(inputs["L0_attn_Wv"])
    Wo = f32(inputs["L0_attn_Wo"])
    Aq = f32(inputs["L0_attn_Aq"]).reshape(LR, D)
    Ak = f32(inputs["L0_attn_Ak"]).reshape(LR, D)
    Av = f32(inputs["L0_attn_Av"]).reshape(LR, D)
    Ao = f32(inputs["L0_attn_Ao"]).reshape(LR, D)
    Bq = f32(inputs["L0_attn_Bq"])
    Bk = f32(inputs["L0_attn_Bk"])
    Bv = f32(inputs["L0_attn_Bv"])
    Bo = f32(inputs["L0_attn_Bo"])
    gwv = i32(inputs["L0_ffn_gate_w"])
    uwv = i32(inputs["L0_ffn_up_w"])
    dwv = i32(inputs["L0_ffn_down_w"])
    gs = f32(inputs["L0_ffn_gate_scale"]).reshape(II // 128, 128).T.copy()
    us = f32(inputs["L0_ffn_up_scale"]).reshape(II // 128, 128).T.copy()
    dsrow = f32(inputs["L0_ffn_down_scale"]).reshape(1, D)

    CC, SS, swp, idn, masks, onc, onr, aidx = _host_consts()
    in_maps = []
    for c in range(8):
        g, r = c // 4, c % 4
        osl = slice(OSL * r, OSL * (r + 1))
        own = slice(OWN * r, OWN * (r + 1))
        in_maps.append({
            "xg": x[g],
            "xow": np.ascontiguousarray(x[g][own]),
            "wq": np.ascontiguousarray(Wq[osl]),
            "wk": np.ascontiguousarray(Wk[osl]),
            "wv": np.ascontiguousarray(Wv[osl]),
            "wo": np.ascontiguousarray(Wo[:, osl]),
            "aq": Aq, "ak": Ak, "av": Av,
            "ao": np.ascontiguousarray(Ao[:, osl]),
            "bq": np.ascontiguousarray(Bq[:, osl, :]),
            "bk": np.ascontiguousarray(Bk[:, osl, :]),
            "bv": np.ascontiguousarray(Bv[:, osl, :]),
            "bo": Bo,
            "gw": gwv, "uw": uwv, "dw": dwv,
            "gsc": gs, "usc": us, "dsrow": dsrow,
            "starts": starts,
            "bmeta": np.array([[float(g)]], np.float32),
            "own0m": np.array([[OWN * r]], np.int32),
            "cc": CC, "ss": SS, "masks": masks,
            "swp": swp, "idn": idn, "onc": onc, "onr": onr, "aidx": aidx,
            "mner": np.full((1, 128), -128.0, ml_dtypes.bfloat16),
        })
    return in_maps


def kernel(**inputs):
    in_maps = make_in_maps(inputs)
    nc = _get_nc()
    res = run_bass_kernel_spmd(nc, in_maps, core_ids=list(range(8)))
    out = np.empty((BATCH, S, D), np.float32)
    for c in range(8):
        g, r = c // 4, c % 4
        out[g, OWN * r:OWN * (r + 1), :] = res.results[c]["out_own"]
    if DEBUG:
        kernel.last_results = res
    return out


# version 17
# speedup vs baseline: 1.1775x; 1.0067x over previous
import math
from contextlib import ExitStack

import numpy as np
import ml_dtypes

import concourse.bass as bass
import concourse.mybir as mybir
from concourse.bass_utils import run_bass_kernel_spmd

F32 = mybir.dt.float32
BF16 = mybir.dt.bfloat16
I32 = mybir.dt.int32
AF = mybir.ActivationFunctionType
ALU = mybir.AluOpType
AX = mybir.AxisListType

BATCH = 2
S = 4096
D = 2048
DH = 128
HPC = 4
OSL = HPC * DH
OWN = 1024
II = 8192
NL, R = 4, 16
LR = NL * R
NBLK = 8
BLK = 512
EPS = 1e-5
SM_SCALE = 1.0 / math.sqrt(DH)
SM_BIAS = -4.0
BAND = 256
NBAND = II // BAND
TB = BAND // 128

DEBUG = False
_CACHE = {}


class Ctr:
    __slots__ = ("sem", "v")

    def __init__(self, stack, nc, name):
        self.sem = stack.enter_context(nc.semaphore(name))
        self.v = 0

    def inc(self, ins, n=1):
        ins.then_inc(self.sem, n)
        self.v += n
        return self.v


def _w(eng, ctr, v):
    if v > 0:
        eng.wait_ge(ctr.sem, v)


def _host_consts():
    inv_freq = 10000.0 ** (-np.arange(0, DH, 2, dtype=np.float64) / DH)
    ang = np.arange(S, dtype=np.float64)[:, None] * inv_freq[None, :]
    cos = np.cos(ang).T
    sin = np.sin(ang).T
    CC = np.empty((DH, S), np.float32)
    SS = np.empty((DH, S), np.float32)
    CC[0::2] = cos
    CC[1::2] = cos
    SS[0::2] = -sin
    SS[1::2] = sin
    swp = np.zeros((DH, DH), ml_dtypes.bfloat16)
    for i in range(0, DH, 2):
        swp[i + 1, i] = 1
        swp[i, i + 1] = 1
    idn = np.eye(DH, dtype=ml_dtypes.bfloat16)
    masks = np.zeros((4, 128, BLK), ml_dtypes.bfloat16)
    for v in range(4):
        kk = np.arange(128)[:, None] + 128 * v
        qq = np.arange(BLK)[None, :]
        masks[v] = (kk <= qq).astype(ml_dtypes.bfloat16)
    ones_col = np.ones((128, 1), ml_dtypes.bfloat16)
    ones_row = np.ones((1, 128), ml_dtypes.bfloat16)
    aidx = (np.arange(LR) // R).astype(np.float32)[:, None]
    return CC, SS, swp, idn, masks, ones_col, ones_row, aidx


def _build():
    nc = bass.Bass("TRN2", target_bir_lowering=False)
    ap = lambda name, shape, dt, kind="ExternalInput": nc.dram_tensor(
        name, shape, dt, kind=kind).ap()

    xg = ap("xg", [S, D], F32)
    xow = ap("xow", [OWN, D], F32)
    wq_i = ap("wq", [OSL, D], F32)
    wk_i = ap("wk", [OSL, D], F32)
    wv_i = ap("wv", [OSL, D], F32)
    wo_i = ap("wo", [D, OSL], F32)
    aq_i = ap("aq", [LR, D], F32)
    ak_i = ap("ak", [LR, D], F32)
    av_i = ap("av", [LR, D], F32)
    ao_i = ap("ao", [LR, OSL], F32)
    bq_i = ap("bq", [NL, OSL, R], F32)
    bk_i = ap("bk", [NL, OSL, R], F32)
    bv_i = ap("bv", [NL, OSL, R], F32)
    bo_i = ap("bo", [NL, D, R], F32)
    gw_i = ap("gw", [II, D], I32)
    uw_i = ap("uw", [II, D], I32)
    dw_i = ap("dw", [D, II], I32)
    gsc_i = ap("gsc", [128, II // 128], F32)
    usc_i = ap("usc", [128, II // 128], F32)
    dsrow_i = ap("dsrow", [1, D], F32)
    starts_i = ap("starts", [1, NL], I32)
    bmeta_i = ap("bmeta", [1, 1], F32)
    own0_i = ap("own0m", [1, 1], I32)
    cc_i = ap("cc", [DH, S], F32)
    ss_i = ap("ss", [DH, S], F32)
    mask_i = ap("masks", [4, 128, BLK], BF16)
    swp_i = ap("swp", [DH, DH], BF16)
    idn_i = ap("idn", [DH, DH], BF16)
    onc_i = ap("onc", [128, 1], BF16)
    onr_i = ap("onr", [1, 128], BF16)
    mner_i = ap("mner", [1, 128], BF16)
    aidx_i = ap("aidx", [LR, 1], F32)

    out_own = ap("out_own", [OWN, D], F32, "ExternalOutput")
    dbg = {}
    if DEBUG:
        dbg["k"] = ap("dbg_k", [128, HPC * S], BF16, "ExternalOutput")
        dbg["q"] = ap("dbg_q", [128, HPC * S], BF16, "ExternalOutput")
        dbg["v"] = ap("dbg_v", [S, OSL], BF16, "ExternalOutput")
        dbg["attn"] = ap("dbg_attn", [128, HPC * S], BF16, "ExternalOutput")
        dbg["ccin"] = ap("dbg_ccin", [S, D], F32, "ExternalOutput")

    xn_all = ap("xn_all", [16, 128, S], BF16, "Internal")
    v_all = ap("v_all", [S, OSL], BF16, "Internal")
    wo_bf = ap("wo_bf", [D, OSL], BF16, "Internal")
    gw_bf = ap("gw_bf", [II, D], BF16, "Internal")
    uw_bf = ap("uw_bf", [II, D], BF16, "Internal")
    dw_bf = ap("dw_bf", [D, II], BF16, "Internal")
    ccin = ap("ccin", [S, D], F32, "Internal")
    ccout = ap("ccout", [OWN, D], F32, "Internal")

    with ExitStack() as st:
        sb = lambda name, shape, dt, **kw: st.enter_context(
            nc.sbuf_tensor(name, shape, dt, **kw))
        C = lambda name: Ctr(st, nc, name)

        swp_sb = sb("swp_sb", [DH, DH], BF16)
        idn_sb = sb("idn_sb", [DH, DH], BF16)
        onc_sb = sb("onc_sb", [128, 1], BF16)
        onr_sb = sb("onr_sb", [1, 128], BF16)
        aidx_sb = sb("aidx_sb", [LR, 1], F32)
        gsc_sb = sb("gsc_sb", [128, II // 128], F32)
        usc_sb = sb("usc_sb", [128, II // 128], F32)
        dsrow_bf = sb("dsrow_bf", [1, D], BF16)
        dsfull = sb("dsfull", [128, D], BF16)
        ind_sb = sb("ind_sb", [LR, 1], F32)
        starts_sb = sb("starts_sb", [1, NL], I32)
        startsf_sb = sb("startsf_sb", [1, NL], F32)
        bmeta_sb = sb("bmeta_sb", [1, 1], F32)
        own0_sb = sb("own0_sb", [1, 1], I32)
        idsc_sb = sb("idsc_sb", [1, 8], F32)
        idbf_sb = sb("idbf_sb", [1, 1], BF16)
        smbias_sb = sb("smbias_sb", [128, 1], F32)
        eps_sb = sb("eps_sb", [128, 1], F32)
        aot = sb("aot", [128, HPC * LR], BF16)
        bot = sb("bot", [LR, D], BF16)
        mner_sb = sb("mner_sb", [1, 128], BF16)

        c_lds = C("c_lds")
        c_ldg = C("c_ldg")
        s_ct = C("s_ct")
        s_x = [C("s_x0"), C("s_x1"), C("s_x2")]
        s_rp = C("s_rp")
        s_vs = [C("s_vs0"), C("s_vs1")]
        s_vg = C("s_vg")
        s_wo = C("s_wo")
        s_cc = [C("s_cc0"), C("s_cc1")]
        s_gw = C("s_gw")
        s_ga = C("s_ga")
        s_dw = C("s_dw")
        s_xo = [C("s_xo0"), C("s_xo1")]
        s_fa = [C("s_fa0"), C("s_fa1")]
        c_dqd = C("c_dqd")
        c_ffc = C("c_ffc")
        c_xcs = C("c_xcs")
        c_tr = C("c_tr")
        c_ev = C("c_ev")
        c_mmq = C("c_mmq")
        c_mlow = C("c_mlow")
        c_vlow = C("c_vlow")
        c_qraw = C("c_qraw")
        c_msw = C("c_msw")
        c_rope = C("c_rope")
        c_vev = C("c_vev")
        c_sq = C("c_sq")
        c_vst = C("c_vst")
        c_rstd = C("c_rstd")
        c_xn = C("c_xn")
        c_msc = C("c_msc")
        c_exp = C("c_exp")
        c_msk = C("c_msk")
        c_mav = C("c_mav")
        c_rcp = C("c_rcp")
        c_mbc = C("c_mbc")
        c_vbc = C("c_vbc")
        c_vat = C("c_vat")
        c_vlo = C("c_vlo")
        c_mo = C("c_mo")
        c_evo = C("c_evo")
        c_scc = C("c_scc")
        c_cc = C("c_cc")
        c_mg = C("c_mg")
        c_sil = C("c_sil")
        c_vh = C("c_vh")
        c_md = C("c_md")
        c_vacc = C("c_vacc")
        c_fin = C("c_fin")
        c_out = C("c_out")
        c_p0 = C("c_p0")
        vse = C("vse")
        ase = C("ase")

        blk0 = st.enter_context(nc.Block())

        @blk0.sync
        def _(_sy):
            SY = nc.sync
            TE = nc.tensor
            VE = nc.vector
            AC = nc.scalar
            GP = nc.gpsimd

            def vi(ins):
                return vse.inc(ins)

            def vw():
                VE.wait_ge(vse.sem, vse.v)

            def ai(ins):
                return ase.inc(ins)

            def aw():
                AC.wait_ge(ase.sem, ase.v)

            with ExitStack() as sA:
                sbA = lambda n, sh, dt, **kw: sA.enter_context(
                    nc.sbuf_tensor(n, sh, dt, **kw))
                ksb = sbA("ksb", [128, HPC * S], BF16)
                qsb = sbA("qsb", [128, HPC * S], BF16)

                with ExitStack() as sB:
                    sbB = lambda n, sh, dt, **kw: sB.enter_context(
                        nc.sbuf_tensor(n, sh, dt, **kw))
                    psB = lambda n, sh, dt: sB.enter_context(
                        nc.psum_tensor(n, sh, dt))
                    wqt = sbB("wqt", [128, 16 * BLK], BF16)
                    wkt = sbB("wkt", [128, 16 * BLK], BF16)
                    wvt = sbB("wvt", [128, 16 * BLK], BF16)
                    aqt = sbB("aqt", [128, 16 * LR], BF16)
                    akt = sbB("akt", [128, 16 * LR], BF16)
                    avt = sbB("avt", [128, 16 * LR], BF16)
                    bqt = sbB("bqt", [LR, OSL], BF16)
                    bkt = sbB("bkt", [LR, OSL], BF16)
                    bvt = sbB("bvt", [LR, OSL], BF16)
                    xt_sb = sbB("xt_sb", [128, 3 * D], F32)
                    stats = sbB("stats", [128, 32], F32)
                    xntm = sbB("xntm", [128, 2 * D], BF16)
                    xnfm = sbB("xnfm", [128, 16 * BLK], BF16)
                    ccb = sbB("ccb", [128, BLK], F32)
                    ssb = sbB("ssb", [128, BLK], F32)
                    lowq = sbB("lowq", [LR, BLK], BF16)
                    lowk = sbB("lowk", [LR, BLK], BF16)
                    lowv = sbB("lowv", [LR, BLK], BF16)
                    qraw = sbB("qraw", [128, 2 * BLK], BF16)
                    rtmp2 = sbB("rtmp2", [128, BLK], F32)
                    vstg = sbB("vstg", [128, 2 * OSL], BF16)
                    ps_tr = psB("ps_tr", [128, 2 * 1024], BF16)
                    ps_qk0 = psB("ps_qk0", [128, 512], F32)
                    ps_qk1 = psB("ps_qk1", [128, 512], F32)
                    ps_v0 = psB("ps_v0", [128, 512], F32)
                    ps_v1 = psB("ps_v1", [128, 512], F32)
                    ps_low = psB("ps_low", [128, 512], F32)
                    ps_swap = psB("ps_swap", [128, 512], F32)

                    for dst, src in [
                        (swp_sb[:, :], swp_i[:, :]), (idn_sb[:, :], idn_i[:, :]),
                        (onc_sb[:, :], onc_i[:, :]), (onr_sb[:, :], onr_i[:, :]),
                        (aidx_sb[:, :], aidx_i[:, :]), (gsc_sb[:, :], gsc_i[:, :]),
                        (usc_sb[:, :], usc_i[:, :]),
                        (starts_sb[:, :], starts_i[:, :]),
                        (bmeta_sb[:, :], bmeta_i[:, :]),
                        (own0_sb[:, :], own0_i[:, :]),
                        (mner_sb[:, :], mner_i[:, :]),
                    ]:
                        s_ct.inc(SY.dma_start(dst, src), 16)
                    ld_consts = s_ct.v

                    c_ldg.inc(GP.dma_start(wo_bf[:, :], wo_i[:, :]), 16)
                    ld_wcast = c_ldg.v

                    TE.wait_ge(s_ct.sem, ld_consts)
                    with ExitStack() as sP0:
                        astg = sP0.enter_context(
                            nc.sbuf_tensor("astg", [LR, D], BF16))
                        bstk = sP0.enter_context(
                            nc.sbuf_tensor("bstk", [128, 16 * LR], BF16))
                        for adr, dstT, wid in [(aq_i, aqt, D), (ak_i, akt, D),
                                               (av_i, avt, D), (ao_i, aot, OSL)]:
                            _w(GP, c_tr, c_tr.v)
                            c_ldg.inc(GP.dma_start(astg[:, 0:wid], adr[:, :]), 16)
                            TE.wait_ge(c_ldg.sem, c_ldg.v)
                            _w(TE, c_ev, c_ev.v)
                            nch = wid // 128
                            for dk in range(nch):
                                ins = TE.matmul(
                                    ps_tr[:, dk * LR:(dk + 1) * LR],
                                    astg[:, dk * 128:(dk + 1) * 128],
                                    idn_sb[0:LR, 0:LR],
                                    is_transpose=True, start=True, stop=True)
                            c_tr.inc(ins)
                            VE.wait_ge(c_tr.sem, c_tr.v)
                            c_ev.inc(VE.tensor_copy(dstT[:, 0:nch * LR],
                                                    ps_tr[:, 0:nch * LR]))
                        for bdr, dstT, nch in [(bq_i, bqt, OSL // 128),
                                               (bk_i, bkt, OSL // 128),
                                               (bv_i, bvt, OSL // 128),
                                               (bo_i, bot, D // 128)]:
                            _w(GP, c_tr, c_tr.v)
                            for ch in range(nch):
                                src3 = bdr[:, ch * 128:(ch + 1) * 128, :].rearrange(
                                    "a p r -> p a r")
                                c_ldg.inc(GP.dma_start(
                                    bstk[:, ch * LR:(ch + 1) * LR].rearrange(
                                        "p (a r) -> p a r", a=NL), src3), 16)
                            TE.wait_ge(c_ldg.sem, c_ldg.v)
                            for half in range((nch + 7) // 8):
                                _w(TE, c_ev, c_ev.v)
                                lo = half * 8
                                hi = min(lo + 8, nch)
                                for ch in range(lo, hi):
                                    ins = TE.matmul(
                                        ps_tr[0:LR,
                                              (ch - lo) * 128:(ch - lo + 1) * 128],
                                        bstk[:, ch * LR:(ch + 1) * LR],
                                        idn_sb[:, :],
                                        is_transpose=True, start=True, stop=True)
                                c_tr.inc(ins)
                                VE.wait_ge(c_tr.sem, c_tr.v)
                                c_ev.inc(VE.tensor_copy(
                                    dstT[0:LR, lo * 128:hi * 128],
                                    ps_tr[0:LR, 0:(hi - lo) * 128]))

                    VE.wait_ge(s_ct.sem, ld_consts)
                    VE.memset(smbias_sb[:, :], SM_BIAS)
                    vi(VE.memset(eps_sb[:, :], EPS))
                    eps_mark = vse.v
                    vi(VE.tensor_copy(startsf_sb[:, :], starts_sb[:, :]))
                    vw()
                    vi(VE.tensor_single_scalar(idsc_sb[0:1, 0:NL],
                                               startsf_sb[0:1, :],
                                               bmeta_sb[0:1, 0:1], ALU.is_le))
                    vw()
                    vi(VE.reduce_sum(idsc_sb[0:1, 4:5], idsc_sb[0:1, 0:NL], AX.X))
                    vw()
                    vi(VE.tensor_scalar(idsc_sb[0:1, 5:6], idsc_sb[0:1, 4:5],
                                        -1.0, 0.0, ALU.add, ALU.max))
                    vw()
                    vi(VE.tensor_single_scalar(idsc_sb[0:1, 6:7], idsc_sb[0:1, 5:6],
                                               3.0, ALU.min))
                    vw()
                    c_p0.inc(VE.tensor_copy(idbf_sb[0:1, 0:1], idsc_sb[0:1, 6:7]))
                    TE.wait_ge(c_p0.sem, c_p0.v)
                    c_p0.inc(TE.matmul(ps_qk0[0:LR, 0:1], onr_sb[0:1, 0:LR],
                                       idbf_sb[0:1, 0:1], start=True, stop=True))
                    VE.wait_ge(c_p0.sem, c_p0.v)
                    vi(VE.tensor_tensor(ind_sb[:, :], ps_qk0[0:LR, 0:1],
                                        aidx_sb[:, :], ALU.is_equal))
                    vw()
                    for dstT, wid in [(bqt, OSL), (bkt, OSL), (bvt, OSL), (bot, D)]:
                        ins = VE.tensor_scalar(dstT[:, 0:wid], dstT[:, 0:wid],
                                               ind_sb[:, 0:1], None, ALU.mult)
                    c_p0.inc(ins)
                    p0_done = c_p0.v

                    wstg = xnfm
                    wtr_v = {}
                    n_wc = 0
                    for wsrc, dstT in [(wq_i, wqt), (wk_i, wkt), (wv_i, wvt)]:
                        dstv = dstT[:, :].rearrange("p (dk q) -> p dk q", dk=16)
                        for fc in range(4):
                            sl = n_wc % 2
                            if n_wc >= 2:
                                GP.wait_ge(c_tr.sem, wtr_v[n_wc - 2])
                            c_ldg.inc(GP.dma_start(
                                wstg[:, sl * D:(sl + 1) * D],
                                wsrc[fc * 128:(fc + 1) * 128, :]), 16)
                            ldv = c_ldg.v
                            TE.wait_ge(c_ldg.sem, ldv)
                            for grp in range(2):
                                _w(TE, c_ev, c_ev.v if n_wc == 0 and grp == 0
                                   else c_ev.v - 1)
                                psl = (grp % 2) * 1024
                                for j in range(8):
                                    dk = grp * 8 + j
                                    ins = TE.matmul(
                                        ps_tr[:, psl + j * 128:psl + (j + 1) * 128],
                                        wstg[:, sl * D + dk * 128:
                                             sl * D + (dk + 1) * 128],
                                        idn_sb[:, :], is_transpose=True,
                                        start=True, stop=True)
                                c_tr.inc(ins)
                                VE.wait_ge(c_tr.sem, c_tr.v)
                                c_ev.inc(VE.tensor_copy(
                                    dstv[:, grp * 8:(grp + 1) * 8,
                                         fc * 128:(fc + 1) * 128],
                                    ps_tr[:, psl:psl + 1024].rearrange(
                                        "p (c q) -> p c q", c=8)))
                            wtr_v[n_wc] = c_tr.v
                            n_wc += 1

                    GP.wait_ge(s_ct.sem, ld_consts)
                    c_ldg.inc(GP.dma_start(dsrow_bf[0:1, :], dsrow_i[0:1, :]), 16)
                    c_dsr = c_ldg.v
                    NCH = 8
                    RCH = II // NCH
                    for k in range(NCH):
                        _w(GP, c_xn, 4 * k)
                        r0 = k * RCH
                        c_ffc.inc(GP.dma_start(gw_bf[r0:r0 + RCH, :],
                                               gw_i[r0:r0 + RCH, :]), 16)
                        c_ffc.inc(GP.dma_start(uw_bf[r0:r0 + RCH, :],
                                               uw_i[r0:r0 + RCH, :]), 16)
                        c_ffc.inc(GP.dma_start(dw_bf[:, r0:r0 + RCH],
                                               dw_i[:, r0:r0 + RCH]), 16)
                    for j in range(4):
                        if j == 0:
                            TE.wait_ge(c_ldg.sem, c_dsr)
                        TE.wait_ge(vse.sem, vse.v)
                        c_p0.inc(TE.matmul(
                            ps_qk0[:, 0:BLK], onr_sb[0:1, :],
                            dsrow_bf[0:1, j * BLK:(j + 1) * BLK],
                            start=True, stop=True))
                        VE.wait_ge(c_p0.sem, c_p0.v)
                        vi(VE.tensor_copy(dsfull[:, j * BLK:(j + 1) * BLK],
                                          ps_qk0[:, 0:BLK]))
                    vw()
                    c_p0.inc(VE.tensor_copy(dsrow_bf[0:1, 0:1],
                                            dsrow_bf[0:1, 0:1]))
                    p0_done = c_p0.v

                    p0_tr = c_tr.v
                    n_qk = 0
                    n_v = 0
                    n_trf = 0
                    v_st = []
                    mmq_blk = {}
                    store_blk = {}
                    ld_rope = 0
                    for b in range(NBLK):
                        xld = []
                        for tt in range(4):
                            gi = 4 * b + tt
                            _w(SY, c_xn, gi - 2)
                            s_x[gi % 3].inc(SY.dma_start(
                                xt_sb[:, (gi % 3) * D:(gi % 3 + 1) * D],
                                xg[b * BLK + tt * 128:b * BLK + (tt + 1) * 128, :]),
                                16)
                            xld.append((s_x[gi % 3], s_x[gi % 3].v))
                        _w(SY, c_rope, 8 * b)
                        s_rp.inc(SY.dma_start(ccb[:, :],
                                              cc_i[:, b * BLK:(b + 1) * BLK]), 16)
                        s_rp.inc(SY.dma_start(ssb[:, :],
                                              ss_i[:, b * BLK:(b + 1) * BLK]), 16)
                        ld_rope = s_rp.v
                        for tt in range(4):
                            gi = 4 * b + tt
                            tp = gi % 2
                            sp = (gi % 4) * 8
                            st_ = lambda c: stats[:, sp + c:sp + c + 1]
                            xt = xt_sb[:, (gi % 3) * D:(gi % 3 + 1) * D]
                            xnslot = xntm[:, tp * D:(tp + 1) * D]
                            AC.wait_ge(xld[tt][0].sem, xld[tt][1])
                            if gi == 0:
                                AC.wait_ge(vse.sem, eps_mark)
                            _w(AC, c_tr, p0_tr + 2 * gi - 2 if gi >= 2 else 0)
                            c_sq.inc(AC.activation(xnslot, xt, AF.Square,
                                                   accum_out=st_(0)))
                            ai(AC.activation(xnslot, xt, AF.Identity,
                                             scale=-1.0 / D, accum_out=st_(1)))
                            aw()
                            ai(AC.activation(st_(2), st_(1), AF.Square))
                            aw()
                            ai(AC.activation(st_(3), st_(2), AF.Identity,
                                             scale=-1.0,
                                             bias=eps_sb[:, 0:1]))
                            AC.wait_ge(c_sq.sem, c_sq.v)
                            aw()
                            ai(AC.activation(st_(4), st_(0), AF.Identity,
                                             scale=1.0 / D, bias=st_(3)))
                            aw()
                            ai(AC.activation(st_(5), st_(4), AF.Ln))
                            aw()
                            ai(AC.activation(st_(6), st_(5), AF.Exp, scale=-0.5))
                            aw()
                            ai(AC.activation(st_(7), st_(1), AF.Identity,
                                             scale=st_(6)))
                            aw()
                            c_xn.inc(AC.activation(xnslot, xt, AF.Identity,
                                                   scale=st_(6), bias=st_(7)))
                            TE.wait_ge(c_xn.sem, c_xn.v)
                            xnv = xnfm[:, :].rearrange("p (dk q) -> p dk q", dk=16)
                            for grp in range(2):
                                _w(TE, c_ev, c_ev.v if n_trf == 0 else c_ev.v - 1)
                                sl = (n_trf % 2) * 1024
                                for j in range(8):
                                    dk = grp * 8 + j
                                    ins = TE.matmul(
                                        ps_tr[:, sl + j * 128:sl + (j + 1) * 128],
                                        xntm[:, tp * D + dk * 128:
                                             tp * D + (dk + 1) * 128],
                                        idn_sb[:, :], is_transpose=True,
                                        start=True, stop=True)
                                c_tr.inc(ins)
                                n_trf += 1
                                VE.wait_ge(c_tr.sem, c_tr.v)
                                if b >= 1 and tt == 0 and grp == 0:
                                    _w(VE, c_mmq, mmq_blk[b - 1])
                                c_ev.inc(VE.tensor_copy(
                                    xnv[:, grp * 8:(grp + 1) * 8,
                                        tt * 128:(tt + 1) * 128],
                                    ps_tr[:, sl:sl + 1024].rearrange(
                                        "p (c q) -> p c q", c=8)))
                        ev_blk = c_ev.v
                        xb = lambda dk: xnfm[:, dk * BLK:(dk + 1) * BLK]

                        TE.wait_ge(c_ev.sem, ev_blk)
                        if b == 0:
                            TE.wait_ge(c_p0.sem, p0_done)
                        for at, lowsb in [(aqt, lowq), (akt, lowk), (avt, lowv)]:
                            _w(TE, c_vlow, c_vlow.v)
                            for dk in range(16):
                                ins = TE.matmul(ps_low[0:LR, 0:BLK],
                                                at[:, dk * LR:(dk + 1) * LR],
                                                xb(dk), start=(dk == 0),
                                                stop=(dk == 15))
                            c_mlow.inc(ins)
                            VE.wait_ge(c_mlow.sem, c_mlow.v)
                            c_vlow.inc(VE.tensor_copy(lowsb[:, :],
                                                      ps_low[0:LR, 0:BLK]))

                        for wt, low, bt, dstsb in [(wqt, lowq, bqt, qsb),
                                                   (wkt, lowk, bkt, ksb)]:
                            for hh in range(HPC):
                                u = n_qk
                                pst = ps_qk0 if (u % 2) == 0 else ps_qk1
                                _w(TE, c_qraw, u - 1)
                                TE.wait_ge(c_vlow.sem, c_vlow.v)
                                for dk in range(16):
                                    TE.matmul(pst[:, 0:BLK],
                                              wt[:, dk * BLK + hh * 128:
                                                 dk * BLK + (hh + 1) * 128],
                                              xb(dk), start=(dk == 0), stop=False)
                                c_mmq.inc(TE.matmul(
                                    pst[:, 0:BLK], bt[:, hh * 128:(hh + 1) * 128],
                                    low[:, :], start=False, stop=True))
                                VE.wait_ge(c_mmq.sem, c_mmq.v)
                                _w(VE, c_msw, u - 1)
                                c_qraw.inc(VE.tensor_copy(
                                    qraw[:, (u % 2) * BLK:(u % 2 + 1) * BLK],
                                    pst[:, 0:BLK]))
                                TE.wait_ge(c_qraw.sem, c_qraw.v)
                                _w(TE, c_rope, u)
                                c_msw.inc(TE.matmul(
                                    ps_swap[:, 0:BLK], swp_sb[:, :],
                                    qraw[:, (u % 2) * BLK:(u % 2 + 1) * BLK],
                                    start=True, stop=True))
                                VE.wait_ge(s_rp.sem, ld_rope)
                                VE.wait_ge(c_qraw.sem, c_qraw.v)
                                dst = dstsb[:, hh * S + b * BLK:
                                            hh * S + (b + 1) * BLK]
                                vi(VE.tensor_tensor(
                                    dst, qraw[:, (u % 2) * BLK:(u % 2 + 1) * BLK],
                                    ccb[:, :], ALU.mult))
                                VE.wait_ge(c_msw.sem, c_msw.v)
                                vi(VE.tensor_tensor(rtmp2[:, :], ps_swap[:, 0:BLK],
                                                    ssb[:, :], ALU.mult))
                                vw()
                                c_rope.inc(VE.tensor_add(dst, dst, rtmp2[:, :]))
                                n_qk += 1

                        for tt in range(4):
                            pst = ps_v0 if (n_v % 2) == 0 else ps_v1
                            _w(TE, c_vev, n_v - 1)
                            for dk in range(16):
                                TE.matmul(pst[:, 0:OSL],
                                          xnfm[:, dk * BLK + tt * 128:
                                               dk * BLK + (tt + 1) * 128],
                                          wvt[:, dk * BLK:(dk + 1) * BLK],
                                          start=(dk == 0), stop=False)
                            c_mmq.inc(TE.matmul(pst[:, 0:OSL],
                                                lowv[:, tt * 128:(tt + 1) * 128],
                                                bvt[:, :], start=False, stop=True))
                            VE.wait_ge(c_mmq.sem, c_mmq.v)
                            if n_v >= 2:
                                _w(VE, s_vs[n_v % 2], s_vs[n_v % 2].v)
                            c_vev.inc(VE.tensor_copy(
                                vstg[:, (n_v % 2) * OSL:(n_v % 2 + 1) * OSL],
                                pst[:, 0:OSL]))
                            SY.wait_ge(c_vev.sem, c_vev.v)
                            kc = b * 4 + tt
                            s_vs[n_v % 2].inc(SY.dma_start(
                                v_all[kc * 128:(kc + 1) * 128, :],
                                vstg[:, (n_v % 2) * OSL:(n_v % 2 + 1) * OSL]), 16)
                            n_v += 1
                        mmq_blk[b] = c_mmq.v
                    p1_rope = c_rope.v

                with ExitStack() as sC:
                    sbC = lambda n, sh, dt, **kw: sC.enter_context(
                        nc.sbuf_tensor(n, sh, dt, **kw))
                    psC = lambda n, sh, dt: sC.enter_context(
                        nc.psum_tensor(n, sh, dt))
                    vsb = sbC("vsb", [128, HPC * S], BF16)
                    asb = sbC("asb", [128, HPC * S], BF16)
                    psb = sbC("psb", [128, 2 * BLK], BF16)
                    mask_sb = sbC("mask_sb", [128, 4 * BLK], BF16)
                    sums_f = sbC("sums_f", [1, BLK], F32)
                    rinvf = sbC("rinvf", [1, BLK], F32)
                    rinvb = sbC("rinvb", [1, BLK], BF16)
                    binv = sbC("binv", [128, BLK], BF16)
                    sP2 = ExitStack()
                    ps_sc = sP2.enter_context(
                        nc.psum_tensor("ps_sc", [128, 2 * 512], F32))
                    ps_attn = sP2.enter_context(
                        nc.psum_tensor("ps_attn", [128, 512], F32))
                    ps_sum = sP2.enter_context(
                        nc.psum_tensor("ps_sum", [128, 512], F32))
                    ps_bc = sP2.enter_context(
                        nc.psum_tensor("ps_bc", [128, 512], F32))

                    SY.wait_ge(s_vs[0].sem, s_vs[0].v)
                    SY.wait_ge(s_vs[1].sem, s_vs[1].v)
                    vsrc = v_all.rearrange("(kc p) d -> p kc d", p=128)
                    for h in range(HPC):
                        s_vg.inc(SY.dma_start(
                            vsb[:, h * S:(h + 1) * S].rearrange(
                                "p (kc d) -> p kc d", kc=32),
                            vsrc[:, :, h * 128:(h + 1) * 128]), 16)
                    for v in range(4):
                        s_vg.inc(SY.dma_start(
                            mask_sb[:, v * BLK:(v + 1) * BLK], mask_i[v, :, :]), 16)
                    ld_v = s_vg.v

                    TE.wait_ge(c_rope.sem, p1_rope)
                    TE.wait_ge(s_vg.sem, ld_v)
                    VE.wait_ge(s_vg.sem, ld_v)
                    n_kc = 0
                    n_qb = 0
                    for u in range(HPC):
                        for qb in range(NBLK):
                            nk = 4 * qb + 4
                            pend = []

                            def consume(TE=TE, u=u, qb=qb, nk=nk):
                                par, ctr, val, kcc = pend.pop(0)
                                _w(TE, ctr, val)
                                if kcc == 0:
                                    _w(TE, c_vat, n_qb)
                                    _w(TE, c_rcp, n_qb)
                                TE.matmul(
                                    ps_attn[:, 0:BLK],
                                    vsb[:, u * S + kcc * 128:u * S + (kcc + 1) * 128],
                                    psb[:, par * BLK:(par + 1) * BLK],
                                    start=(kcc == 0), stop=(kcc == nk - 1))
                                c_mav.inc(TE.matmul(
                                    ps_sum[0:1, 0:BLK], onc_sb[:, 0:1],
                                    psb[:, par * BLK:(par + 1) * BLK],
                                    start=(kcc == 0), stop=(kcc == nk - 1)))

                            for kc in range(nk):
                                par = n_kc % 2
                                _w(TE, c_exp, n_kc - 1)
                                c_msc.inc(TE.matmul(
                                    ps_sc[:, par * 512:par * 512 + BLK],
                                    ksb[:, u * S + kc * 128:u * S + (kc + 1) * 128],
                                    qsb[:, u * S + qb * BLK:u * S + (qb + 1) * BLK],
                                    start=True, stop=True))
                                AC.wait_ge(c_msc.sem, c_msc.v)
                                _w(AC, c_mav, n_kc - 1)
                                c_exp.inc(AC.activation(
                                    psb[:, par * BLK:(par + 1) * BLK],
                                    ps_sc[:, par * 512:par * 512 + BLK],
                                    AF.Exp, bias=smbias_sb[:, 0:1],
                                    scale=SM_SCALE))
                                diag = kc - 4 * qb
                                if diag >= 0:
                                    VE.wait_ge(c_exp.sem, c_exp.v)
                                    c_msk.inc(VE.tensor_tensor(
                                        psb[:, par * BLK:(par + 1) * BLK],
                                        psb[:, par * BLK:(par + 1) * BLK],
                                        mask_sb[:, diag * BLK:(diag + 1) * BLK],
                                        ALU.mult))
                                    pend.append((par, c_msk, c_msk.v, kc))
                                else:
                                    pend.append((par, c_exp, c_exp.v, kc))
                                if len(pend) > 1:
                                    consume()
                                n_kc += 1
                            consume()
                            VE.wait_ge(c_mav.sem, c_mav.v)
                            vi(VE.tensor_copy(sums_f[0:1, :], ps_sum[0:1, 0:BLK]))
                            vw()
                            vi(VE.reciprocal(rinvf[0:1, :], sums_f[0:1, :]))
                            vw()
                            c_rcp.inc(VE.tensor_copy(rinvb[0:1, :], rinvf[0:1, :]))
                            TE.wait_ge(c_rcp.sem, c_rcp.v)
                            _w(TE, c_vbc, n_qb)
                            c_mbc.inc(TE.matmul(ps_bc[:, 0:BLK], onr_sb[0:1, :],
                                                rinvb[0:1, :],
                                                start=True, stop=True))
                            VE.wait_ge(c_mbc.sem, c_mbc.v)
                            c_vbc.inc(VE.tensor_copy(binv[:, :], ps_bc[:, 0:BLK]))
                            VE.wait_ge(c_vbc.sem, c_vbc.v)
                            c_vat.inc(VE.tensor_tensor(
                                asb[:, u * S + qb * BLK:u * S + (qb + 1) * BLK],
                                ps_attn[:, 0:BLK], binv[:, :], ALU.mult))
                            n_qb += 1
                    p2_done = c_vat.v
                    sP2.close()

                    if DEBUG:
                        SY.wait_ge(c_vat.sem, p2_done)
                        c_lds.inc(SY.dma_start(dbg["k"][:, :], ksb[:, :]), 16)
                        c_lds.inc(SY.dma_start(dbg["q"][:, :], qsb[:, :]), 16)
                        c_lds.inc(SY.dma_start(dbg["v"][:, :], v_all[:, :]), 16)
                        c_lds.inc(SY.dma_start(dbg["attn"][:, :], asb[:, :]), 16)

                    wot = sbC("wot", [128, HPC * D], BF16, is_transpose_dest=True)
                    lowo = sbC("lowo", [LR, S], BF16)
                    osb = sbC("osb", [128, 2 * D], F32)
                    for c in range(HPC):
                        s_wo.inc(SY.dma_start(
                            wot[:, c * D:(c + 1) * D],
                            wo_bf[:, c * 128:(c + 1) * 128], transpose=True), 16)
                    ld_wot = s_wo.v

                    with ExitStack() as sL:
                        ps_lo3 = sL.enter_context(
                            nc.psum_tensor("ps_lo3", [128, 512], F32))
                        TE.wait_ge(c_vat.sem, p2_done)
                        for blk in range(NBLK):
                            _w(TE, c_vlo, blk)
                            for c in range(HPC):
                                ins = TE.matmul(
                                    ps_lo3[0:LR, 0:BLK],
                                    aot[:, c * LR:(c + 1) * LR],
                                    asb[:, c * S + blk * BLK:
                                        c * S + (blk + 1) * BLK],
                                    start=(c == 0), stop=(c == HPC - 1))
                            c_mlow.inc(ins)
                            VE.wait_ge(c_mlow.sem, c_mlow.v)
                            c_vlo.inc(VE.tensor_copy(
                                lowo[:, blk * BLK:(blk + 1) * BLK],
                                ps_lo3[0:LR, 0:BLK]))
                    with ExitStack() as sO:
                        ps_o = sO.enter_context(
                            nc.psum_tensor("ps_o", [128, 2 * D], F32))
                        TE.wait_ge(s_wo.sem, ld_wot)
                        TE.wait_ge(c_vlo.sem, c_vlo.v)
                        tt_order = [t for t in range(S // 128) if t % 8 < 4] + \
                                   [t for t in range(S // 128) if t % 8 >= 4]
                        cc_h0 = None
                        for nt, tt in enumerate(tt_order):
                            par = nt % 2
                            op = par * D
                            _w(TE, c_evo, 2 * (nt - 1))
                            for j in range(4):
                                for c in range(HPC):
                                    TE.matmul(
                                        ps_o[:, op + j * 512:op + (j + 1) * 512],
                                        asb[:, c * S + tt * 128:
                                            c * S + (tt + 1) * 128],
                                        wot[:, c * D + j * 512:c * D + (j + 1) * 512],
                                        start=(c == 0), stop=False)
                                ins = TE.matmul(
                                    ps_o[:, op + j * 512:op + (j + 1) * 512],
                                    lowo[:, tt * 128:(tt + 1) * 128],
                                    bot[:, j * 512:(j + 1) * 512],
                                    start=False, stop=True)
                            c_mo.inc(ins)
                            VE.wait_ge(c_mo.sem, c_mo.v)
                            _w(VE, s_cc[par], s_cc[par].v)
                            c_evo.inc(VE.tensor_copy(osb[:, op:op + 1024],
                                                     ps_o[:, op:op + 1024]))
                            AC.wait_ge(c_mo.sem, c_mo.v)
                            _w(AC, s_cc[par], s_cc[par].v)
                            c_evo.inc(AC.copy(osb[:, op + 1024:op + D],
                                              ps_o[:, op + 1024:op + D]))
                            SY.wait_ge(c_evo.sem, c_evo.v)
                            hh2 = (tt % 8) // 4
                            row0 = hh2 * 2048 + (tt // 8) * 512 + (tt % 4) * 128
                            s_cc[par].inc(SY.dma_start(
                                ccin[row0:row0 + 128, :],
                                osb[:, op:op + D]), 16)
                            if nt == 15:
                                cc_h0 = (s_cc[0].v, s_cc[1].v)

                    GP.wait_ge(s_cc[0].sem, cc_h0[0])
                    GP.wait_ge(s_cc[1].sem, cc_h0[1])
                    c_cc.inc(GP.collective_compute(
                        "ReduceScatter", ALU.add,
                        replica_groups=[[0, 1, 2, 3], [4, 5, 6, 7]],
                        ins=[ccin[0:2048, :].opt()],
                        outs=[ccout[0:512, :].opt()]), 1)
                    GP.wait_ge(s_cc[0].sem, s_cc[0].v)
                    GP.wait_ge(s_cc[1].sem, s_cc[1].v)
                    c_cc.inc(GP.collective_compute(
                        "ReduceScatter", ALU.add,
                        replica_groups=[[0, 1, 2, 3], [4, 5, 6, 7]],
                        ins=[ccin[2048:4096, :].opt()],
                        outs=[ccout[512:1024, :].opt()]), 1)

            with ExitStack() as sD:
                sbD = lambda n, sh, dt, **kw: sD.enter_context(
                    nc.sbuf_tensor(n, sh, dt, **kw))
                psD = lambda n, sh, dt: sD.enter_context(nc.psum_tensor(n, sh, dt))
                xnown = sbD("xnown", [128, 16 * OWN], BF16)
                wgt_b = sbD("wgt_b", [128, 2 * 16 * BAND], BF16,
                            is_transpose_dest=True)
                wut_b = sbD("wut_b", [128, 2 * 16 * BAND], BF16,
                            is_transpose_dest=True)
                wdt_b = sbD("wdt_b", [128, 2 * TB * D], BF16,
                            is_transpose_dest=True)
                hact = sbD("hact", [128, 2 * TB * OWN], BF16)
                sg_sb = sbD("sg_sb", [128, 2 * BLK], BF16)
                oacc = sbD("oacc", [128, 8 * D], F32)
                xcs_hi = sbD("xcs_hi", [1, 2 * BLK], BF16)
                xcs_lo = sbD("xcs_lo", [1, 2 * BLK], BF16)
                xcs_tmp = sbD("xcs_tmp", [1, BLK], F32)

                p3_mo = c_mo.v
                p3_evo = c_evo.v
                gld_band = {}
                uld_band = {}
                dld_band = {}
                mg_band = {}
                md_band = {}

                def issue_band(band):
                    fp = band % 2
                    wsrc = band * BAND
                    if band == 0:
                        for q in (SY, AC):
                            q.wait_ge(c_mo.sem, p3_mo)
                            q.wait_ge(c_evo.sem, p3_evo)
                            q.wait_ge(c_ffc.sem, c_ffc.v)
                    if band >= 2:
                        _w(SY, c_mg, mg_band[band - 2])
                        _w(AC, c_mg, mg_band[band - 2])
                        _w(SY, c_md, md_band[band - 2])
                    for dk in range(16):
                        s_gw.inc(SY.dma_start(
                            wgt_b[:, fp * 16 * BAND + dk * BAND:
                                  fp * 16 * BAND + (dk + 1) * BAND],
                            gw_bf[wsrc:wsrc + BAND, dk * 128:(dk + 1) * 128],
                            transpose=True), 16)
                        s_ga.inc(AC.dma_start(
                            wut_b[:, fp * 16 * BAND + dk * BAND:
                                  fp * 16 * BAND + (dk + 1) * BAND],
                            uw_bf[wsrc:wsrc + BAND, dk * 128:(dk + 1) * 128],
                            transpose=True), 16)
                    gld_band[band] = s_gw.v
                    uld_band[band] = s_ga.v
                    for ic in range(TB):
                        s_dw.inc(SY.dma_start(
                            wdt_b[:, fp * TB * D + ic * D:
                                  fp * TB * D + (ic + 1) * D],
                            dw_bf[:, wsrc + ic * 128:wsrc + (ic + 1) * 128],
                            transpose=True), 16)
                    dld_band[band] = s_dw.v

                with ExitStack() as sP4a:
                    xts = sP4a.enter_context(
                        nc.sbuf_tensor("xts", [128, 2 * D], F32))
                    xnt4 = sP4a.enter_context(
                        nc.sbuf_tensor("xnt4", [128, 2 * D], BF16))
                    st4 = sP4a.enter_context(
                        nc.sbuf_tensor("st4", [128, 32], F32))
                    ps_t4 = sP4a.enter_context(
                        nc.psum_tensor("ps_t4", [128, 2 * 1024], BF16))
                    TE.wait_ge(c_evo.sem, c_evo.v)
                    p4_tr0 = c_tr.v
                    n_t4 = 0
                    xo_ld = []
                    SY.wait_ge(c_mo.sem, p3_mo)
                    SY.wait_ge(c_evo.sem, p3_evo)
                    for ti in range(8):
                        _w(SY, c_xn, c_xn.v + ti - 1)
                        s_xo[ti % 2].inc(SY.dma_start(
                            xts[:, (ti % 2) * D:(ti % 2 + 1) * D],
                            xow[ti * 128:(ti + 1) * 128, :]), 16)
                        xo_ld.append((s_xo[ti % 2], s_xo[ti % 2].v))
                    issue_band(0)
                    issue_band(1)
                    xnv4 = xnown[:, :].rearrange("p (dk q) -> p dk q", dk=16)
                    for ti in range(8):
                        tp = ti % 2
                        sp = (ti % 4) * 8
                        st_ = lambda c: st4[:, sp + c:sp + c + 1]
                        xt = xts[:, tp * D:(tp + 1) * D]
                        xnslot = xnt4[:, tp * D:(tp + 1) * D]
                        AC.wait_ge(xo_ld[ti][0].sem, xo_ld[ti][1])
                        _w(AC, c_tr, p4_tr0 + 2 * ti - 2 if ti >= 2 else 0)
                        c_sq.inc(AC.activation(xnslot, xt, AF.Square,
                                               accum_out=st_(0)))
                        ai(AC.activation(xnslot, xt, AF.Identity,
                                         scale=-1.0 / D, accum_out=st_(1)))
                        aw()
                        ai(AC.activation(st_(2), st_(1), AF.Square))
                        aw()
                        ai(AC.activation(st_(3), st_(2), AF.Identity,
                                         scale=-1.0, bias=eps_sb[:, 0:1]))
                        AC.wait_ge(c_sq.sem, c_sq.v)
                        aw()
                        ai(AC.activation(st_(4), st_(0), AF.Identity,
                                         scale=1.0 / D, bias=st_(3)))
                        aw()
                        ai(AC.activation(st_(5), st_(4), AF.Ln))
                        aw()
                        ai(AC.activation(st_(6), st_(5), AF.Exp, scale=-0.5))
                        aw()
                        ai(AC.activation(st_(7), st_(1), AF.Identity,
                                         scale=st_(6)))
                        aw()
                        c_xn.inc(AC.activation(xnslot, xt, AF.Identity,
                                               scale=st_(6), bias=st_(7)))
                        TE.wait_ge(c_xn.sem, c_xn.v)
                        for grp in range(2):
                            _w(TE, c_ev, c_ev.v if n_t4 == 0 else c_ev.v - 1)
                            sl = (n_t4 % 2) * 1024
                            for j in range(8):
                                dk = grp * 8 + j
                                ins = TE.matmul(
                                    ps_t4[:, sl + j * 128:sl + (j + 1) * 128],
                                    xnt4[:, tp * D + dk * 128:
                                         tp * D + (dk + 1) * 128],
                                    idn_sb[:, :], is_transpose=True,
                                    start=True, stop=True)
                            c_tr.inc(ins)
                            n_t4 += 1
                            VE.wait_ge(c_tr.sem, c_tr.v)
                            c_ev.inc(VE.tensor_copy(
                                xnv4[:, grp * 8:(grp + 1) * 8,
                                     ti * 128:(ti + 1) * 128],
                                ps_t4[:, sl:sl + 1024].rearrange(
                                    "p (c q) -> p c q", c=8)))
                    TE.wait_ge(c_ev.sem, c_ev.v)
                    ps_cs = sP4a.enter_context(
                        nc.psum_tensor("ps_cs", [128, 2 * 512], F32))
                    for tb in range(2):
                        cs = ps_cs[0:1, tb * 512:tb * 512 + BLK]
                        for dk in range(16):
                            ins = TE.matmul(cs, onc_sb[:, 0:1],
                                            xnown[:, dk * OWN + tb * BLK:
                                                  dk * OWN + (tb + 1) * BLK],
                                            start=(dk == 0), stop=(dk == 15))
                        c_tr.inc(ins)
                        VE.wait_ge(c_tr.sem, c_tr.v)
                        vi(VE.tensor_copy(xcs_hi[0:1, tb * BLK:(tb + 1) * BLK],
                                          cs))
                        vw()
                        vi(VE.tensor_sub(xcs_tmp[0:1, :], cs,
                                         xcs_hi[0:1, tb * BLK:(tb + 1) * BLK]))
                        vw()
                        c_xcs.inc(VE.tensor_copy(
                            xcs_lo[0:1, tb * BLK:(tb + 1) * BLK],
                            xcs_tmp[0:1, :]))
                ps_g = psD("ps_g", [128, 2 * 512], F32)
                ps_u = psD("ps_u", [128, 2 * 512], F32)
                ps_d = psD("ps_d", [128, 2 * 1024], F32)

                n_gu = 0
                n_d = 0
                acc_v = {}
                TE.wait_ge(c_xcs.sem, c_xcs.v)
                for band in range(NBAND):
                    fp = band % 2
                    TE.wait_ge(s_gw.sem, gld_band[band])
                    TE.wait_ge(s_ga.sem, uld_band[band])
                    for ic in range(TB):
                        for tb in range(2):
                            par = n_gu % 2
                            _w(TE, c_sil, n_gu - 1)
                            _w(TE, c_vh, n_gu - 1)
                            ich = band * TB + ic
                            for dk in range(16):
                                TE.matmul(
                                    ps_g[:, par * 512:par * 512 + BLK],
                                    wgt_b[:, fp * 16 * BAND + dk * BAND + ic * 128:
                                          fp * 16 * BAND + dk * BAND + (ic + 1) * 128],
                                    xnown[:, dk * OWN + tb * BLK:
                                          dk * OWN + (tb + 1) * BLK],
                                    start=(dk == 0), stop=False)
                            TE.matmul(ps_g[:, par * 512:par * 512 + BLK],
                                      mner_sb[0:1, :],
                                      xcs_hi[0:1, tb * BLK:(tb + 1) * BLK],
                                      start=False, stop=False)
                            TE.matmul(ps_g[:, par * 512:par * 512 + BLK],
                                      mner_sb[0:1, :],
                                      xcs_lo[0:1, tb * BLK:(tb + 1) * BLK],
                                      start=False, stop=True)
                            for dk in range(16):
                                TE.matmul(
                                    ps_u[:, par * 512:par * 512 + BLK],
                                    wut_b[:, fp * 16 * BAND + dk * BAND + ic * 128:
                                          fp * 16 * BAND + dk * BAND + (ic + 1) * 128],
                                    xnown[:, dk * OWN + tb * BLK:
                                          dk * OWN + (tb + 1) * BLK],
                                    start=(dk == 0), stop=False)
                            TE.matmul(ps_u[:, par * 512:par * 512 + BLK],
                                      mner_sb[0:1, :],
                                      xcs_hi[0:1, tb * BLK:(tb + 1) * BLK],
                                      start=False, stop=False)
                            ins = TE.matmul(ps_u[:, par * 512:par * 512 + BLK],
                                            mner_sb[0:1, :],
                                            xcs_lo[0:1, tb * BLK:(tb + 1) * BLK],
                                            start=False, stop=True)
                            c_mg.inc(ins)
                            AC.wait_ge(c_mg.sem, c_mg.v)
                            _w(AC, c_vh, n_gu - 1)
                            c_sil.inc(AC.activation(
                                sg_sb[:, par * BLK:(par + 1) * BLK],
                                ps_g[:, par * 512:par * 512 + BLK], AF.Silu,
                                scale=gsc_sb[:, ich:ich + 1]))
                            VE.wait_ge(c_sil.sem, c_sil.v)
                            if band >= 2 and ic == 0 and tb == 0:
                                _w(VE, c_md, md_band[band - 2])
                            c_vh.inc(VE.scalar_tensor_tensor(
                                hact[:, fp * TB * OWN + ic * OWN + tb * BLK:
                                     fp * TB * OWN + ic * OWN + (tb + 1) * BLK],
                                ps_u[:, par * 512:par * 512 + BLK],
                                usc_sb[:, ich:ich + 1],
                                sg_sb[:, par * BLK:(par + 1) * BLK],
                                ALU.mult, ALU.mult))
                            n_gu += 1
                    mg_band[band] = c_mg.v
                    VE.wait_ge(s_dw.sem, dld_band[band])
                    for ic in range(TB):
                        ins = VE.scalar_tensor_tensor(
                            wdt_b[:, fp * TB * D + ic * D:
                                  fp * TB * D + (ic + 1) * D],
                            wdt_b[:, fp * TB * D + ic * D:
                                  fp * TB * D + (ic + 1) * D],
                            128.0, dsfull[:, :], ALU.subtract, ALU.mult)
                    c_dqd.inc(ins)
                    TE.wait_ge(c_dqd.sem, c_dqd.v)
                    TE.wait_ge(c_vh.sem, c_vh.v)
                    for tt in range(8):
                        for half in range(2):
                            dp = n_d % 2
                            _w(TE, c_vacc, n_d - 1)
                            for j in range(2):
                                h0 = half * 1024 + j * 512
                                for ic in range(TB):
                                    ins = TE.matmul(
                                        ps_d[:, dp * 1024 + j * 512:
                                             dp * 1024 + (j + 1) * 512],
                                        hact[:, fp * TB * OWN + ic * OWN + tt * 128:
                                             fp * TB * OWN + ic * OWN + (tt + 1) * 128],
                                        wdt_b[:, fp * TB * D + ic * D + h0:
                                              fp * TB * D + ic * D + h0 + 512],
                                        start=(ic == 0), stop=(ic == TB - 1))
                            c_md.inc(ins)
                            VE.wait_ge(c_md.sem, c_md.v)
                            dst = oacc[:, tt * D + half * 1024:
                                       tt * D + (half + 1) * 1024]
                            if band == 0:
                                c_vacc.inc(VE.tensor_copy(
                                    dst, ps_d[:, dp * 1024:(dp + 1) * 1024]))
                            else:
                                _w(VE, c_vacc, acc_v[(tt, half)])
                                c_vacc.inc(VE.tensor_add(
                                    dst, ps_d[:, dp * 1024:(dp + 1) * 1024], dst))
                            acc_v[(tt, half)] = c_vacc.v
                            n_d += 1
                    md_band[band] = c_md.v
                    if band + 2 < NBAND:
                        issue_band(band + 2)

                GP.wait_ge(c_vacc.sem, c_vacc.v)
                GP.wait_ge(c_cc.sem, 2)
                for tt in range(8):
                    s_fa[0].inc(GP.dma_start(oacc[:, tt * D:(tt + 1) * D],
                                             xow[tt * 128:(tt + 1) * 128, :],
                                             accum_op=ALU.add), 16)
                GP.wait_ge(s_fa[0].sem, s_fa[0].v)
                for tt in range(8):
                    s_fa[1].inc(GP.dma_start(oacc[:, tt * D:(tt + 1) * D],
                                             ccout[tt * 128:(tt + 1) * 128, :],
                                             accum_op=ALU.add), 16)
                SY.wait_ge(s_fa[0].sem, s_fa[0].v)
                SY.wait_ge(s_fa[1].sem, s_fa[1].v)
                for tt in range(8):
                    c_out.inc(SY.dma_start(out_own[tt * 128:(tt + 1) * 128, :],
                                           oacc[:, tt * D:(tt + 1) * D]), 16)
                SY.wait_ge(c_out.sem, c_out.v)
    return nc


def _get_nc():
    key = ("v1", DEBUG)
    if key not in _CACHE:
        _CACHE[key] = _build()
    return _CACHE[key]


def make_in_maps(inputs):
    f32 = lambda a: np.ascontiguousarray(np.asarray(a, dtype=np.float32))
    i32 = lambda a: np.ascontiguousarray(np.asarray(a, dtype=np.int32))
    x = f32(inputs["L0_attn_x"])
    starts = i32(inputs["L0_attn_lora_start_indices"]).reshape(1, NL)
    Wq = f32(inputs["L0_attn_Wq"])
    Wk = f32(inputs["L0_attn_Wk"])
    Wv = f32(inputs["L0_attn_Wv"])
    Wo = f32(inputs["L0_attn_Wo"])
    Aq = f32(inputs["L0_attn_Aq"]).reshape(LR, D)
    Ak = f32(inputs["L0_attn_Ak"]).reshape(LR, D)
    Av = f32(inputs["L0_attn_Av"]).reshape(LR, D)
    Ao = f32(inputs["L0_attn_Ao"]).reshape(LR, D)
    Bq = f32(inputs["L0_attn_Bq"])
    Bk = f32(inputs["L0_attn_Bk"])
    Bv = f32(inputs["L0_attn_Bv"])
    Bo = f32(inputs["L0_attn_Bo"])
    gwv = i32(inputs["L0_ffn_gate_w"])
    uwv = i32(inputs["L0_ffn_up_w"])
    dwv = i32(inputs["L0_ffn_down_w"])
    gs = f32(inputs["L0_ffn_gate_scale"]).reshape(II // 128, 128).T.copy()
    us = f32(inputs["L0_ffn_up_scale"]).reshape(II // 128, 128).T.copy()
    dsrow = f32(inputs["L0_ffn_down_scale"]).reshape(1, D)

    CC, SS, swp, idn, masks, onc, onr, aidx = _host_consts()
    in_maps = []
    for c in range(8):
        g, r = c // 4, c % 4
        osl = slice(OSL * r, OSL * (r + 1))
        own = slice(OWN * r, OWN * (r + 1))
        in_maps.append({
            "xg": x[g],
            "xow": np.ascontiguousarray(x[g][own]),
            "wq": np.ascontiguousarray(Wq[osl]),
            "wk": np.ascontiguousarray(Wk[osl]),
            "wv": np.ascontiguousarray(Wv[osl]),
            "wo": np.ascontiguousarray(Wo[:, osl]),
            "aq": Aq, "ak": Ak, "av": Av,
            "ao": np.ascontiguousarray(Ao[:, osl]),
            "bq": np.ascontiguousarray(Bq[:, osl, :]),
            "bk": np.ascontiguousarray(Bk[:, osl, :]),
            "bv": np.ascontiguousarray(Bv[:, osl, :]),
            "bo": Bo,
            "gw": gwv, "uw": uwv, "dw": dwv,
            "gsc": gs, "usc": us, "dsrow": dsrow,
            "starts": starts,
            "bmeta": np.array([[float(g)]], np.float32),
            "own0m": np.array([[OWN * r]], np.int32),
            "cc": CC, "ss": SS, "masks": masks,
            "swp": swp, "idn": idn, "onc": onc, "onr": onr, "aidx": aidx,
            "mner": np.full((1, 128), -128.0, ml_dtypes.bfloat16),
        })
    return in_maps


def kernel(**inputs):
    in_maps = make_in_maps(inputs)
    nc = _get_nc()
    res = run_bass_kernel_spmd(nc, in_maps, core_ids=list(range(8)))
    out = np.empty((BATCH, S, D), np.float32)
    for c in range(8):
        g, r = c // 4, c % 4
        out[g, OWN * r:OWN * (r + 1), :] = res.results[c]["out_own"]
    if DEBUG:
        kernel.last_results = res
    return out
